# version 02
# baseline (speedup 1.0000x reference)
import contextlib
import numpy as np
import ml_dtypes
import concourse.bass as bass
import concourse.mybir as mybir
from concourse.bass_utils import run_bass_kernel_spmd

F32 = mybir.dt.float32
BF16 = mybir.dt.bfloat16
AF = mybir.ActivationFunctionType
ALU = mybir.AluOpType
AX = mybir.AxisListType

S_TOK = 4096
DM = 2048
NH = 32
NE = 8
ND = 16
NI = 11
NTC = 32
NTILE = 4
TT = 1024
TCT = 8


def build_program():
    nc = bass.Bass("TRN2", target_bir_lowering=False)

    X = nc.dram_tensor("x", [S_TOK, DM], F32, kind="ExternalInput").ap()
    WQ = nc.dram_tensor("wq", [ND, 128, DM], F32, kind="ExternalInput").ap()
    WK = nc.dram_tensor("wk", [DM, DM], F32, kind="ExternalInput").ap()
    WV = nc.dram_tensor("wv", [DM, DM], F32, kind="ExternalInput").ap()
    GW = nc.dram_tensor("gw", [DM, 8], F32, kind="ExternalInput").ap()
    WO4 = nc.dram_tensor("wo4", [4, 128, ND * 512], F32, kind="ExternalInput").ap()
    W13 = nc.dram_tensor("w13", [NE, NI, 128, 4096], F32, kind="ExternalInput").ap()
    W2Q = nc.dram_tensor("w2q", [NE, 4, 128, NI * 512], F32, kind="ExternalInput").ap()
    BOB = nc.dram_tensor("bob", [1, DM], F32, kind="ExternalInput").ap()
    E8 = nc.dram_tensor("e8", [NH, DM], F32, kind="ExternalInput").ap()
    IDB = nc.dram_tensor("idb", [128, 128], BF16, kind="ExternalInput").ap()
    IDF = nc.dram_tensor("idf", [128, 128], F32, kind="ExternalInput").ap()
    ONE1 = nc.dram_tensor("one1", [1, 128], BF16, kind="ExternalInput").ap()
    OUT = nc.dram_tensor("out", [S_TOK, DM], F32, kind="ExternalOutput").ap()

    XSD = nc.dram_tensor("xn_scr_d", [ND, 128, S_TOK], BF16).ap()
    XST = nc.dram_tensor("xn_scr_tc", [NTC, ND, 128, 128], BF16).ap()
    WFM = nc.dram_tensor("wfm", [8, S_TOK], F32).ap()

    ctx = contextlib.ExitStack()
    with ctx:
        ec = ctx.enter_context
        ar1 = ec(nc.sbuf_tensor("ar1", [128, 32768], BF16))
        arv = ec(nc.sbuf_tensor("arv", [128, 33792], BF16))
        ar2 = ec(nc.sbuf_tensor("ar2", [128, 19456], BF16))
        kv_bf = ec(nc.sbuf_tensor("kv_bf", [128, 1040], BF16))
        ksbd = ec(nc.sbuf_tensor("ksbd", [128, 512], BF16))
        e_sb = ec(nc.sbuf_tensor("e_sb", [NH, DM], BF16))
        bo_row = ec(nc.sbuf_tensor("bo_row", [1, DM], BF16))
        idbf = ec(nc.sbuf_tensor("idbf", [128, 128], BF16))
        idf32 = ec(nc.sbuf_tensor("idf32", [128, 128], F32))
        ones1 = ec(nc.sbuf_tensor("ones1", [1, 128], BF16))
        stage = ec(nc.sbuf_tensor("stage", [1, TT], BF16))
        xbuf = ec(nc.sbuf_tensor("xbuf", [128, DM], F32))
        xntok = ec(nc.sbuf_tensor("xntok", [128, DM], BF16))
        xnfm = ec(nc.sbuf_tensor("xnfm", [128, DM], BF16))
        st = ec(nc.sbuf_tensor("st", [128, 12], F32))
        logits = ec(nc.sbuf_tensor("logits", [128, 8], F32))
        t8a = ec(nc.sbuf_tensor("t8a", [128, 8], F32))
        t8b = ec(nc.sbuf_tensor("t8b", [128, 8], F32))
        t8c = ec(nc.sbuf_tensor("t8c", [128, 8], F32))
        w8 = ec(nc.sbuf_tensor("w8", [128, 8], F32))
        wrow = ec(nc.sbuf_tensor("wrow", [8, 128], F32))
        gst = ec(nc.sbuf_tensor("gst", [128, 8], F32))
        nmaxb = ec(nc.sbuf_tensor("nmaxb", [NH, 512], F32))
        rinv = ec(nc.sbuf_tensor("rinv", [NH, 512], BF16))
        bcb = [ec(nc.sbuf_tensor("bcb0", [128, 512], F32)),
               ec(nc.sbuf_tensor("bcb1", [128, 512], F32))]

        psA = ec(nc.psum_tensor("psA", [128, 1024], F32))
        psB = ec(nc.psum_tensor("psB", [128, 1024], F32))
        psC = ec(nc.psum_tensor("psC", [128, 1024], F32))
        psD = ec(nc.psum_tensor("psD", [128, 1024], F32))

        sem_names = ["d_sy", "d_gp", "sT", "sV", "sA", "sP"]
        SEM = {n: ec(nc.semaphore(n)) for n in sem_names}

        y32 = ar1[:, :].bitcast(F32)
        wk_sb = ar1
        wv_sb = arv
        grp = arv
        xn_t = arv[:, 0:16384]
        qz = arv[:, 16384:32768]
        h_sb = arv[:, 16384:27648]
        s_sb = arv[:, 27648:29696].bitcast(F32)
        h1_sb = arv[:, 29696:31744].bitcast(F32)
        bce_sb = arv[:, 31744:33792].bitcast(F32)

        sq_scr = ar2[:, 0:4096].bitcast(F32)
        xn_tc = [ar2[:, 0:2048], ar2[:, 2048:4096]]
        k_sb = ar2[:, 4096:6144]
        vext = ar2[:, 6144:8224]
        kv_f32 = ar2[:, 8224:10304].bitcast(F32)
        gate_sb = ar2[:, 10304:10432]
        wqb = [ar2[:, 0:2048], ar2[:, 2048:4096]]
        woq = [ar2[:, 0:8192], ar2[:, 8192:16384]]
        w13b = [ar2[:, 0:4096], ar2[:, 4096:8192]]
        w2qb = [ar2[:, 8192:13824], ar2[:, 13824:19456]]

        ops = []
        cnt = {n: 0 for n in sem_names}

        def dma(eng, sem, out, in_):
            cnt[sem] += 16
            v = cnt[sem]
            ops.append((eng, lambda e, o=out, i=in_, s=sem: e.dma_start(out=o, in_=i).then_inc(SEM[s], 16),
                        ("inc", sem, 16)))
            return v

        def wait(eng, sem, v):
            if v > 0:
                ops.append((eng, lambda e, s=sem, vv=v: e.wait_ge(SEM[s], vv), ("wait", sem, v)))

        def inc(eng, sem, fn):
            cnt[sem] += 1
            v = cnt[sem]
            ops.append((eng, lambda e, f=fn, s=sem: f(e).then_inc(SEM[s], 1), ("inc", sem, 1)))
            return v

        def just(eng, fn):
            ops.append((eng, lambda e, f=fn: f(e), ("op",)))

        def drain(eng):
            ops.append((eng, lambda e: e.drain(), ("op",)))

        BIG = 1e30

        dma("gpsimd", "d_gp", e_sb[:, :], E8)
        dma("gpsimd", "d_gp", bo_row[:, :], BOB)
        dma("sync", "d_sy", idbf[:, :], IDB)
        dma("sync", "d_sy", idf32[:, :], IDF)
        dma("sync", "d_sy", ones1[:, :], ONE1)
        for d in range(ND):
            dma("gpsimd", "d_gp", wk_sb[:, d * 2048:(d + 1) * 2048], WK[d * 128:(d + 1) * 128, :])
        for d in range(8, ND):
            dma("gpsimd", "d_gp", wv_sb[:, d * 2048:(d + 1) * 2048], WV[d * 128:(d + 1) * 128, :])
        dma("gpsimd", "d_gp", gate_sb.rearrange("p (d j) -> p d j", d=ND),
            GW.rearrange("(d p) j -> p d j", p=128))

        ptr = [psA[:, 0:64].bitcast(BF16), psB[:, 0:64].bitcast(BF16)]
        mark_xn_done = 0
        mark_tr_ev = {}
        mark_grp_store = 0
        mark_tc_store = {}
        last_tr = {}
        p0_sq_done = 0

        for tc in range(NTC):
            g = tc // TCT
            wait("sync", "sV", mark_xn_done)
            mx = dma("sync", "d_sy", xbuf[:, :], X[tc * 128:(tc + 1) * 128, :])
            wait("vector", "d_sy", mx)
            just("vector", lambda e: e.reduce_sum(st[:, 0:1], xbuf[:, :], AX.X))
            wait("scalar", "d_sy", mx)
            p0_sq_done = mA = inc("scalar", "sA", lambda e: e.activation(
                sq_scr[:, :], xbuf[:, :], AF.Square, accum_out=st[:, 1:2]))
            wait("vector", "sA", mA)
            drain("vector")
            just("vector", lambda e: e.tensor_scalar(st[:, 2:3], st[:, 0:1], 1.0 / DM, None, ALU.mult))
            just("vector", lambda e: e.tensor_scalar(st[:, 3:4], st[:, 1:2], 1.0 / DM, None, ALU.mult))
            drain("vector")
            just("vector", lambda e: e.tensor_tensor(st[:, 4:5], st[:, 2:3], st[:, 2:3], ALU.mult))
            drain("vector")
            just("vector", lambda e: e.tensor_tensor(st[:, 5:6], st[:, 3:4], st[:, 4:5], ALU.subtract))
            drain("vector")
            mV = inc("vector", "sV", lambda e: e.tensor_scalar(st[:, 6:7], st[:, 5:6], 1e-5, None, ALU.add))
            wait("scalar", "sV", mV)
            mA = inc("scalar", "sA", lambda e: e.activation(st[:, 7:8], st[:, 6:7], AF.Sqrt))
            wait("vector", "sA", mA)
            just("vector", lambda e: e.reciprocal(st[:, 8:9], st[:, 7:8]))
            if tc > 0:
                wait("vector", "sT", last_tr[tc - 1])
            drain("vector")
            mark_xn_done = inc("vector", "sV", lambda e: e.tensor_scalar(
                xntok[:, :], xbuf[:, :], st[:, 2:3], st[:, 8:9], ALU.subtract, ALU.mult))
            if tc > 0:
                wait("vector", "d_sy", mark_tc_store[tc - 1])
                wait("scalar", "d_sy", mark_tc_store[tc - 1])
            if tc % TCT == 0 and tc > 0:
                wait("vector", "d_sy", mark_grp_store)
                wait("scalar", "d_sy", mark_grp_store)
            wait("tensor", "sV", mark_xn_done)
            for d in range(ND):
                if (tc, d) not in ((0, 0), (0, 1)):
                    pd = tc * ND + d - 2
                    ptc, pdd = divmod(pd, ND)
                    sem, val = mark_tr_ev[(ptc, pdd)]
                    wait("tensor", sem, val)
                last_tr[tc] = inc("tensor", "sT", lambda e, d=d: e.transpose(
                    ptr[d % 2], xntok[:, d * 128:(d + 1) * 128], idbf[:, :]))
                ev_eng, ev_sem = ("vector", "sV") if d % 2 == 0 else ("scalar", "sA")
                cpf = (lambda e, o, i: e.tensor_copy(o, i)) if d % 2 == 0 else (lambda e, o, i: e.copy(o, i))
                wait(ev_eng, "sT", last_tr[tc])
                just(ev_eng, lambda e, d=d, tc=tc, cpf=cpf: cpf(
                    e, grp[:, d * 1024 + (tc % TCT) * 128: d * 1024 + (tc % TCT) * 128 + 128], ptr[d % 2]))
                mark_tr_ev[(tc, d)] = (ev_sem, inc(ev_eng, ev_sem, lambda e, d=d, cpf=cpf: cpf(
                    e, xnfm[:, d * 128:(d + 1) * 128], ptr[d % 2])))
            for d in range(ND):
                sem, val = mark_tr_ev[(tc, d)]
                wait("sync", sem, val)
            mark_tc_store[tc] = dma("sync", "d_sy", XST[tc].rearrange("d p j -> p d j"),
                                    xnfm.rearrange("p (d j) -> p d j", d=ND))
            if tc % TCT == TCT - 1:
                for dd in range(ND):
                    mark_grp_store = dma("sync", "d_sy", XSD[dd][:, g * 1024:(g + 1) * 1024],
                                         grp[:, dd * 1024:(dd + 1) * 1024])

        wait("gpsimd", "d_sy", mark_grp_store)
        for d in range(0, 8):
            dma("gpsimd", "d_gp", wv_sb[:, d * 2048:(d + 1) * 2048], WV[d * 128:(d + 1) * 128, :])

        pKV = [psA[:, 0:512], psA[:, 512:1024], psB[:, 0:512], psB[:, 512:1024]]
        kv0 = psC[:, 0:455]
        kv1 = psC[:, 512:967]
        kv2e = psD[:, 0:130]
        tr8 = psD[0:8, 512:640]
        pgate = psD[:, 640:648]

        def kv_col(pair):
            if pair < 7:
                return pair * 65
            if pair < 14:
                return 455 + (pair - 7) * 65
            return 910 + (pair - 14) * 65

        kvp = []
        for p in range(16):
            if p < 7:
                kvp.append((kv0, p * 65))
            elif p < 14:
                kvp.append((kv1, (p - 7) * 65))
            else:
                kvp.append((kv2e, (p - 14) * 65))

        just("vector", lambda e: e.memset(vext[:, :], 0.0))
        for h in range(NH):
            just("vector", lambda e, h=h: e.memset(vext[:, h * 65 + 64:h * 65 + 65], 1.0))
        mark_vext_ones = inc("vector", "sV", lambda e: e.nop())

        mark_kev = mark_vev = mark_gev = mark_tr8ev = 0
        mark_kvmm = 0
        mark_wfm = {}
        mark_xn_free = {}
        for tc in range(NTC):
            buf = tc % 2
            wait("sync", "d_sy", mark_tc_store[tc])
            if tc == 0:
                wait("sync", "sA", p0_sq_done)
            if tc >= 2:
                wait("sync", "sT", mark_xn_free[tc - 2])
            mxn = dma("sync", "d_sy", xn_tc[buf].rearrange("p (d j) -> p d j", d=ND),
                       XST[tc].rearrange("d p j -> p d j"))
            wait("tensor", "d_sy", mxn)
            if tc == 0:
                wait("tensor", "d_gp", cnt["d_gp"])
                wait("tensor", "sV", mark_vext_ones)
                s31, v31 = mark_tr_ev[(NTC - 1, ND - 2)]
                wait("tensor", s31, v31)
                s31, v31 = mark_tr_ev[(NTC - 1, ND - 1)]
                wait("tensor", s31, v31)
            wait("tensor", "sV", mark_vev)
            wait("tensor", "sA", mark_gev)
            wait("tensor", "sA", mark_tr8ev)
            for d in range(ND):
                for gcol in range(4):
                    just("tensor", lambda e, d=d, gcol=gcol, buf=buf: e.matmul(
                        pKV[gcol], xn_tc[buf][:, d * 128:(d + 1) * 128],
                        wk_sb[:, d * 2048 + gcol * 512: d * 2048 + (gcol + 1) * 512],
                        start=(d == 0), stop=(d == ND - 1)))
                fn = lambda e, d=d, buf=buf: e.matmul(
                    pgate, xn_tc[buf][:, d * 128:(d + 1) * 128],
                    gate_sb[:, d * 8:(d + 1) * 8],
                    start=(d == 0), stop=(d == ND - 1))
                if d == ND - 1:
                    mkg = inc("tensor", "sT", fn)
                else:
                    just("tensor", fn)
            wait("vector", "sT", mkg)
            just("vector", lambda e: e.tensor_relu(k_sb[:, 0:1024], psA[:, :]))
            mark_kev = inc("vector", "sV", lambda e: e.tensor_relu(k_sb[:, 1024:2048], psB[:, :]))
            wait("scalar", "sT", mkg)
            mark_gev = inc("scalar", "sA", lambda e: e.copy(logits[:, :], pgate))
            wait("tensor", "sV", mark_kev)
            for d in range(ND):
                for gcol in range(4):
                    fn = lambda e, d=d, gcol=gcol, buf=buf: e.matmul(
                        pKV[gcol], xn_tc[buf][:, d * 128:(d + 1) * 128],
                        wv_sb[:, d * 2048 + gcol * 512: d * 2048 + (gcol + 1) * 512],
                        start=(d == 0), stop=(d == ND - 1))
                    if d == ND - 1 and gcol == 3:
                        mvd = inc("tensor", "sT", fn)
                    else:
                        just("tensor", fn)
            mark_xn_free[tc] = mvd
            wait("vector", "sT", mvd)
            just("vector", lambda e: e.tensor_copy(
                vext.rearrange("p (h j) -> p h j", j=65)[:, 0:16, 0:64],
                psA.rearrange("p (h j) -> p h j", j=64)))
            mark_vev = inc("vector", "sV", lambda e: e.tensor_copy(
                vext.rearrange("p (h j) -> p h j", j=65)[:, 16:32, 0:64],
                psB.rearrange("p (h j) -> p h j", j=64)))
            wait("tensor", "sV", mark_vev)
            wait("tensor", "sA", mark_gev)
            for h in range(NH):
                pair, base = h // 2, (h % 2) * 64
                tgt, col = kvp[pair]
                st_flag = (tc == 0) and (pair in (0, 7, 14))
                mark_kvmm = inc("tensor", "sT", lambda e, h=h, base=base, tgt=tgt, col=col, st=st_flag: e.matmul(
                    tgt[base:base + 64, col:col + 65],
                    k_sb[:, h * 64:(h + 1) * 64], vext[:, h * 65:(h + 1) * 65],
                    start=st, stop=(tc == NTC - 1)))
            wait("vector", "sA", mark_gev)
            just("vector", lambda e: e.tensor_reduce(gst[:, 0:1], logits[:, :], AX.X, ALU.max))
            drain("vector")
            just("vector", lambda e: e.tensor_scalar(t8a[:, :], logits[:, :], gst[:, 0:1], None, ALU.subtract))
            drain("vector")
            just("vector", lambda e: e.tensor_scalar(t8b[:, :], t8a[:, :], 0.0, None, ALU.is_ge))
            drain("vector")
            just("vector", lambda e: e.scalar_tensor_tensor(t8c[:, :], t8b[:, :], -BIG, t8a[:, :], ALU.mult, ALU.add))
            drain("vector")
            mV = inc("vector", "sV", lambda e: e.tensor_reduce(gst[:, 1:2], t8c[:, :], AX.X, ALU.max))
            wait("scalar", "sV", mV)
            just("scalar", lambda e: e.activation(gst[:, 2:3], gst[:, 1:2], AF.Exp))
            mA = inc("scalar", "sA", lambda e: e.activation(t8b[:, :], t8a[:, :], AF.Exp))
            wait("vector", "sA", mA)
            just("vector", lambda e: e.tensor_scalar(gst[:, 3:4], gst[:, 2:3], 1.0, None, ALU.add))
            drain("vector")
            just("vector", lambda e: e.reciprocal(gst[:, 4:5], gst[:, 3:4]))
            drain("vector")
            just("vector", lambda e: e.tensor_scalar(t8c[:, :], t8a[:, :], gst[:, 1:2], None, ALU.is_ge))
            drain("vector")
            just("vector", lambda e: e.tensor_tensor(t8a[:, :], t8c[:, :], t8b[:, :], ALU.mult))
            drain("vector")
            mV = inc("vector", "sV", lambda e: e.tensor_scalar(w8[:, :], t8a[:, :], gst[:, 4:5], None, ALU.mult))
            wait("tensor", "sV", mV)
            wait("tensor", "sA", mark_tr8ev)
            mT = inc("tensor", "sT", lambda e: e.transpose(tr8, w8[:, :], idf32[:, :]))
            wait("scalar", "sT", mT)
            if tc > 0:
                wait("scalar", "d_sy", mark_wfm[tc - 1])
            mark_tr8ev = inc("scalar", "sA", lambda e: e.copy(wrow[:, :], tr8))
            wait("sync", "sA", mark_tr8ev)
            mark_wfm[tc] = dma("sync", "d_sy", WFM[:, tc * 128:(tc + 1) * 128], wrow[:, :])
        mark_wfm_done = cnt["d_sy"]

        wait("vector", "sT", mark_kvmm)
        just("vector", lambda e: e.tensor_copy(kv_f32[:, 0:455], kv0))
        just("vector", lambda e: e.tensor_copy(kv_f32[:, 455:910], kv1))
        just("vector", lambda e: e.tensor_copy(kv_f32[:, 910:1040], kv2e))
        just("vector", lambda e: e.tensor_copy(kv_bf[:, 0:455], kv0))
        just("vector", lambda e: e.tensor_copy(kv_bf[:, 455:910], kv1))
        just("vector", lambda e: e.tensor_copy(kv_bf[:, 910:1040], kv2e))
        drain("vector")
        just("vector", lambda e: e.memset(ksbd[:, :], 0.0))
        for h in range(NH):
            base = (h % 2) * 64
            kc = kv_col(h // 2) + 64
            dst_col = (h // 2) * 32 + h
            just("vector", lambda e, base=base, kc=kc, dc=dst_col: e.tensor_copy(
                ksbd[base:base + 64, dc:dc + 1], kv_f32[base:base + 64, kc:kc + 1]))
        p1_done = inc("vector", "sV", lambda e: e.nop())
        p1_sA = cnt["sA"]

        mark_y_store = {t: 0 for t in range(TCT)}
        mark_psq_ev = [0, 0]
        mark_pt3_ev = [0, 0]
        mark_pyq_ev = [0, 0, 0, 0]
        mark_bcev = [0, 0]
        mark_sev = [0, 0]
        mark_uev = [0, 0]
        mark_silu = mark_h1 = mark_h = 0
        mark_w13 = {0: 0, 1: 0}
        mark_wq = [0, 0]
        mark_woq = [0, 0]
        mark_w2q = [0, 0]
        last_wq_mm = [0, 0]
        last_woq_mm = [0, 0]
        last_w13_mm = [0, 0]
        last_w2q_mm = [0, 0]
        last_stage_mm = 0
        mark_qev = {}
        mark_zmul = {}
        mark_yadd_last = {}

        wait("sync", "sV", p1_done)
        wait("sync", "sT", mark_kvmm)
        mark_xnt_load = 0
        for d in range(ND):
            mark_xnt_load = dma("sync", "d_sy", xn_t[:, d * 1024:(d + 1) * 1024], XSD[d][:, 0:TT])

        for tile in range(NTILE):
            t0 = tile * TT
            wait("gpsimd", "sV", p1_done)
            wait("gpsimd", "sT", last_w13_mm[0])
            wait("gpsimd", "sT", last_w13_mm[1])
            wait("gpsimd", "sT", last_w2q_mm[0])
            wait("gpsimd", "sT", last_w2q_mm[1])
            mark_wq[0] = dma("gpsimd", "d_gp", wqb[0], WQ[0])
            psq = [psA, psB]
            for q in range(ND):
                b = q % 2
                if q + 1 < ND:
                    nb = (q + 1) % 2
                    wait("gpsimd", "sT", last_wq_mm[nb])
                    mark_wq[nb] = dma("gpsimd", "d_gp", wqb[nb], WQ[q + 1])
                wait("tensor", "d_gp", mark_wq[b])
                wait("tensor", "d_sy", mark_xnt_load)
                wait("tensor", "sV", mark_psq_ev[b])
                if tile == 0 and q == 0:
                    wait("tensor", "sV", p1_done)
                if tile > 0 and q == 0:
                    for pb in range(4):
                        wait("tensor", "sV", mark_pyq_ev[pb])
                for d in range(ND):
                    for half in range(2):
                        fn = lambda e, b=b, d=d, half=half: e.matmul(
                            psq[b][:, half * 512:(half + 1) * 512],
                            wqb[b][:, d * 128:(d + 1) * 128],
                            xn_t[:, d * 1024 + half * 512: d * 1024 + (half + 1) * 512],
                            start=(d == 0), stop=(d == ND - 1))
                        if d == ND - 1 and half == 1:
                            last_wq_mm[b] = last_q_mm = inc("tensor", "sT", fn)
                        else:
                            just("tensor", fn)
                wait("vector", "sT", last_q_mm)
                mark_psq_ev[b] = inc("vector", "sV", lambda e, q=q, b=b: e.tensor_relu(
                    qz[:, q * 1024:(q + 1) * 1024], psq[b][:, :]))
                mark_qev[q] = mark_psq_ev[b]

            pn = psC[0:NH, 0:512]
            pbc = [psC[:, 512:1024], psD[:, 0:512]]
            pqkv = [psD[:, 512:1024], psA[:, 0:512]]
            mark_nred = 0
            mark_zpar = [0, 0]
            for sg in range(2):
                for q in range(ND):
                    wait("tensor", "sV", mark_qev[q])
                if tile == 0 and sg == 0:
                    wait("tensor", "sA", p1_sA)
                if sg == 1:
                    wait("tensor", "sV", mark_nred)
                for c in range(ND):
                    fn = lambda e, c=c, sg=sg: e.matmul(
                        pn, ksbd[:, c * 32:(c + 1) * 32],
                        qz[:, c * 1024 + sg * 512: c * 1024 + (sg + 1) * 512],
                        start=(c == 0), stop=(c == ND - 1))
                    if c == ND - 1:
                        m_n = inc("tensor", "sT", fn)
                    else:
                        just("tensor", fn)
                wait("vector", "sT", m_n)
                just("vector", lambda e: e.tensor_scalar(nmaxb[:, :], pn, 1e-6, None, ALU.max))
                drain("vector")
                just("vector", lambda e: e.reciprocal(nmaxb[:, :], nmaxb[:, :]))
                drain("vector")
                mark_nred = inc("vector", "sV", lambda e: e.tensor_copy(rinv[:, :], nmaxb[:, :]))
                wait("tensor", "sV", mark_nred)
                for c in range(ND):
                    b = c % 2
                    wait("tensor", "sA", mark_bcev[b])
                    mbc = inc("tensor", "sT", lambda e, c=c, b=b: e.matmul(
                        pbc[b], e_sb[:, c * 128:(c + 1) * 128], rinv[:, :],
                        start=True, stop=True))
                    wait("tensor", "sV", mark_zpar[b])
                    just("tensor", lambda e, c=c, b=b, sg=sg: e.matmul(
                        pqkv[b][0:64, :], kv_bf[0:64, kv_col(c):kv_col(c) + 64],
                        qz[0:64, c * 1024 + sg * 512: c * 1024 + (sg + 1) * 512],
                        start=True, stop=True))
                    mqk = inc("tensor", "sT", lambda e, c=c, b=b, sg=sg: e.matmul(
                        pqkv[b][64:128, :], kv_bf[64:128, kv_col(c):kv_col(c) + 64],
                        qz[64:128, c * 1024 + sg * 512: c * 1024 + (sg + 1) * 512],
                        start=True, stop=True))
                    wait("scalar", "sT", mbc)
                    wait("scalar", "sV", mark_zpar[b])
                    mark_bcev[b] = inc("scalar", "sA", lambda e, b=b: e.copy(bcb[b][:, :], pbc[b]))
                    wait("vector", "sT", mqk)
                    wait("vector", "sA", mark_bcev[b])
                    mark_zpar[b] = mark_zmul[(sg, c)] = inc("vector", "sV", lambda e, c=c, b=b, sg=sg: e.tensor_tensor(
                        qz[:, c * 1024 + sg * 512: c * 1024 + (sg + 1) * 512],
                        pqkv[b], bcb[b][:, :], ALU.mult))

            pt3 = [psA[:, 0:512], psB[:, 0:512]]
            wait("gpsimd", "sT", last_wq_mm[0])
            wait("gpsimd", "sT", last_wq_mm[1])
            mark_woq[0] = dma("gpsimd", "d_gp", woq[0], WO4[0])
            for q4 in range(4):
                b = q4 % 2
                if q4 + 1 < 4:
                    nb = (q4 + 1) % 2
                    wait("gpsimd", "sT", last_woq_mm[nb])
                    mark_woq[nb] = dma("gpsimd", "d_gp", woq[nb], WO4[q4 + 1])
                wait("tensor", "d_gp", mark_woq[b])
                for tcl in range(TCT):
                    pb = tcl % 2
                    if q4 == 0 and tcl == 0:
                        for c in range(ND):
                            wait("tensor", "sV", mark_zmul[(1, c)])
                    wait("tensor", "sV", mark_pt3_ev[pb])
                    if tile > 0:
                        wait("tensor", "d_sy", mark_y_store[tcl])
                    for c in range(ND):
                        just("tensor", lambda e, c=c, tcl=tcl, b=b, pb=pb: e.matmul(
                            pt3[pb], qz[:, c * 1024 + tcl * 128: c * 1024 + tcl * 128 + 128],
                            woq[b][:, c * 512:(c + 1) * 512],
                            start=(c == 0), stop=False))
                    last_woq_mm[b] = m3 = inc("tensor", "sT", lambda e, q4=q4, pb=pb: e.matmul(
                        pt3[pb], ones1[0:1, :], bo_row[0:1, q4 * 512:(q4 + 1) * 512],
                        start=False, stop=True))
                    wait("vector", "sT", m3)
                    mark_pt3_ev[pb] = inc("vector", "sV", lambda e, tcl=tcl, q4=q4, pb=pb: e.tensor_copy(
                        y32[:, tcl * 2048 + q4 * 512: tcl * 2048 + (q4 + 1) * 512], pt3[pb]))

            for ex in range(NE):
                wait("gpsimd", "d_sy", mark_wfm_done)
                wait("gpsimd", "sT", last_stage_mm)
                mst = dma("gpsimd", "d_gp", stage[0:1, :], WFM[ex:ex + 1, t0:t0 + TT])
                wait("tensor", "d_gp", mst)
                if ex == 0:
                    wait("tensor", "sV", mark_pt3_ev[0])
                    wait("tensor", "sV", mark_pt3_ev[1])
                else:
                    wait("tensor", "sA", mark_sev[0])
                    wait("tensor", "sV", mark_uev[0])
                    for pb in range(4):
                        wait("tensor", "sV", mark_pyq_ev[pb])
                wait("tensor", "sV", mark_h1)
                for half in range(2):
                    fn = lambda e, half=half: e.matmul(
                        psA[:, half * 512:(half + 1) * 512], ones1[0:1, :],
                        stage[0:1, half * 512:(half + 1) * 512], start=True, stop=True)
                    if half == 1:
                        last_stage_mm = mbs = inc("tensor", "sT", fn)
                    else:
                        just("tensor", fn)
                wait("scalar", "sT", mbs)
                mark_bce = inc("scalar", "sA", lambda e: e.copy(bce_sb[:, :], psA[:, :]))
                wait("gpsimd", "sT", last_w13_mm[0])
                if ex == 0:
                    wait("gpsimd", "sT", last_woq_mm[0])
                    wait("gpsimd", "sT", last_woq_mm[1])
                mark_w13[0] = dma("gpsimd", "d_gp", w13b[0], W13[ex, 0])
                pg = [psA, psC]
                pu = [psB, psD]
                for i in range(NI):
                    b = i % 2
                    if i + 1 < NI:
                        nb = (i + 1) % 2
                        wait("gpsimd", "sT", last_w13_mm[nb])
                        mark_w13[nb] = dma("gpsimd", "d_gp", w13b[nb], W13[ex, i + 1])
                    wait("tensor", "d_gp", mark_w13[b])
                    if i == 0:
                        wait("tensor", "sA", mark_bce)
                    wait("tensor", "sA", mark_sev[b])
                    wait("tensor", "sV", mark_uev[b])
                    for d in range(ND):
                        for half in range(2):
                            just("tensor", lambda e, b=b, d=d, half=half: e.matmul(
                                pg[b][:, half * 512:(half + 1) * 512],
                                w13b[b][:, d * 128:(d + 1) * 128],
                                xn_t[:, d * 1024 + half * 512: d * 1024 + (half + 1) * 512],
                                start=(d == 0), stop=(d == ND - 1)))
                    for d in range(ND):
                        for half in range(2):
                            fn = lambda e, b=b, d=d, half=half: e.matmul(
                                pu[b][:, half * 512:(half + 1) * 512],
                                w13b[b][:, 2048 + d * 128: 2048 + (d + 1) * 128],
                                xn_t[:, d * 1024 + half * 512: d * 1024 + (half + 1) * 512],
                                start=(d == 0), stop=(d == ND - 1))
                            if d == ND - 1 and half == 1:
                                last_w13_mm[b] = mgu = inc("tensor", "sT", fn)
                            else:
                                just("tensor", fn)
                    wait("scalar", "sT", mgu)
                    wait("scalar", "sP", mark_h)
                    mark_sev[b] = mark_silu = inc("scalar", "sA", lambda e, b=b: e.activation(
                        s_sb[:, :], pg[b][:, :], AF.Silu))
                    wait("vector", "sT", mgu)
                    wait("vector", "sP", mark_h)
                    mark_uev[b] = mark_h1 = inc("vector", "sV", lambda e, b=b: e.tensor_tensor(
                        h1_sb[:, :], pu[b][:, :], bce_sb[:, :], ALU.mult))
                    wait("gpsimd", "sA", mark_silu)
                    wait("gpsimd", "sV", mark_h1)
                    mark_h = inc("gpsimd", "sP", lambda e, i=i: e.tensor_tensor(
                        h_sb[:, i * 1024:(i + 1) * 1024], h1_sb[:, :], s_sb[:, :], ALU.mult))
                wait("gpsimd", "sT", last_w2q_mm[0])
                mark_w2q[0] = dma("gpsimd", "d_gp", w2qb[0], W2Q[ex, 0])
                pyq = [psA[:, 0:512], psB[:, 0:512], psC[:, 0:512], psD[:, 0:512]]
                for q4 in range(4):
                    b = q4 % 2
                    if q4 + 1 < 4:
                        nb = (q4 + 1) % 2
                        wait("gpsimd", "sT", last_w2q_mm[nb])
                        mark_w2q[nb] = dma("gpsimd", "d_gp", w2qb[nb], W2Q[ex, q4 + 1])
                    wait("tensor", "d_gp", mark_w2q[b])
                    if q4 == 0:
                        wait("tensor", "sP", mark_h)
                        wait("tensor", "sA", mark_sev[0])
                        wait("tensor", "sA", mark_sev[1])
                        wait("tensor", "sV", mark_uev[0])
                        wait("tensor", "sV", mark_uev[1])
                    for tcl in range(TCT):
                        pb = (q4 * TCT + tcl) % 4
                        wait("tensor", "sV", mark_pyq_ev[pb])
                        for i in range(NI):
                            fn = lambda e, i=i, tcl=tcl, b=b, pb=pb: e.matmul(
                                pyq[pb], h_sb[:, i * 1024 + tcl * 128: i * 1024 + tcl * 128 + 128],
                                w2qb[b][:, i * 512:(i + 1) * 512],
                                start=(i == 0), stop=(i == NI - 1))
                            if i == NI - 1:
                                last_w2q_mm[b] = m4 = inc("tensor", "sT", fn)
                            else:
                                just("tensor", fn)
                        wait("vector", "sT", m4)
                        mark_pyq_ev[pb] = inc("vector", "sV", lambda e, tcl=tcl, q4=q4, pb=pb: e.tensor_tensor(
                            y32[:, tcl * 2048 + q4 * 512: tcl * 2048 + (q4 + 1) * 512],
                            pyq[pb],
                            y32[:, tcl * 2048 + q4 * 512: tcl * 2048 + (q4 + 1) * 512], ALU.add))
                        if ex == NE - 1:
                            mark_yadd_last[(q4, tcl)] = mark_pyq_ev[pb]
                if ex == NE - 1 and tile + 1 < NTILE:
                    wait("sync", "sT", mgu)
                    for d in range(ND):
                        mark_xnt_load = dma("sync", "d_sy", xn_t[:, d * 1024:(d + 1) * 1024],
                                            XSD[d][:, (tile + 1) * TT:(tile + 2) * TT])

            for tcl in range(TCT):
                mxl = dma("sync", "d_sy", xbuf[:, :], X[t0 + tcl * 128: t0 + (tcl + 1) * 128, :])
                for q4 in range(4):
                    wait("vector", "sV", mark_yadd_last[(q4, tcl)])
                wait("vector", "d_sy", mxl)
                mF = inc("vector", "sV", lambda e, tcl=tcl: e.tensor_tensor(
                    y32[:, tcl * 2048:(tcl + 1) * 2048], xbuf[:, :],
                    y32[:, tcl * 2048:(tcl + 1) * 2048], ALU.add))
                wait("sync", "sV", mF)
                mark_y_store[tcl] = dma("sync", "d_sy", OUT[t0 + tcl * 128: t0 + (tcl + 1) * 128, :],
                                        y32[:, tcl * 2048:(tcl + 1) * 2048])

        wait("sync", "d_sy", cnt["d_sy"])
        wait("gpsimd", "d_gp", cnt["d_gp"])

        engs = ["sync", "tensor", "vector", "scalar", "gpsimd"]
        queues = {n: [] for n in engs}
        for name, fn, meta in ops:
            queues[name].append(meta)
        pos = {n: 0 for n in engs}
        sem_now = {n: 0 for n in sem_names}
        progress = True
        while progress:
            progress = False
            for n in engs:
                while pos[n] < len(queues[n]):
                    meta = queues[n][pos[n]]
                    if meta[0] == "wait":
                        if sem_now[meta[1]] < meta[2]:
                            break
                    elif meta[0] == "inc":
                        sem_now[meta[1]] += meta[2]
                    pos[n] += 1
                    progress = True
        stuck = {n: (pos[n], len(queues[n])) for n in engs if pos[n] < len(queues[n])}
        if stuck:
            detail = {n: queues[n][pos[n]] for n in stuck}
            raise RuntimeError(f"plan deadlock: {stuck} blocked at {detail} sems={sem_now}")

        with nc.Block() as block:
            @block.sync
            def _(eng):
                for name, fn, _m in ops:
                    if name == "sync":
                        fn(eng)

            @block.tensor
            def _(eng):
                for name, fn, _m in ops:
                    if name == "tensor":
                        fn(eng)

            @block.vector
            def _(eng):
                for name, fn, _m in ops:
                    if name == "vector":
                        fn(eng)

            @block.scalar
            def _(eng):
                for name, fn, _m in ops:
                    if name == "scalar":
                        fn(eng)

            @block.gpsimd
            def _(eng):
                for name, fn, _m in ops:
                    if name == "gpsimd":
                        fn(eng)

    return nc


def _prep(inputs):
    wq = np.ascontiguousarray(
        inputs["L0_attn_Wq"].reshape(16, 128, 16, 128).transpose(2, 1, 0, 3).reshape(16, 128, 2048))
    wo4 = np.ascontiguousarray(
        inputs["L0_attn_Wo"].reshape(16, 128, 4, 512).transpose(2, 1, 0, 3).reshape(4, 128, 16 * 512))
    w13a = inputs["L0_ffn_w1"].reshape(8, 16, 128, 11, 128).transpose(0, 3, 2, 1, 4)
    w13c = inputs["L0_ffn_w3"].reshape(8, 16, 128, 11, 128).transpose(0, 3, 2, 1, 4)
    w13 = np.ascontiguousarray(np.stack([w13a, w13c], axis=3).reshape(8, 11, 128, 4096))
    w2q = np.ascontiguousarray(
        inputs["L0_ffn_w2"].reshape(8, 11, 128, 4, 512).transpose(0, 3, 2, 1, 4).reshape(8, 4, 128, 11 * 512))
    bob = np.ascontiguousarray(inputs["L0_attn_bo"][None, :])
    e8 = np.zeros((32, 2048), dtype=np.float32)
    for h in range(32):
        e8[h, h * 64:(h + 1) * 64] = 1.0
    return {
        "wq": wq, "wk": np.ascontiguousarray(inputs["L0_attn_Wk"]),
        "wv": np.ascontiguousarray(inputs["L0_attn_Wv"]),
        "gw": np.ascontiguousarray(inputs["L0_ffn_gate_w"]),
        "wo4": wo4, "w13": w13, "w2q": w2q, "bob": bob, "e8": e8,
        "idb": np.eye(128, dtype=ml_dtypes.bfloat16),
        "idf": np.eye(128, dtype=np.float32),
        "one1": np.ones((1, 128), dtype=ml_dtypes.bfloat16),
    }


def kernel(**inputs):
    x = np.asarray(inputs["L0_attn_hidden_states"])
    common = _prep({k: np.asarray(v) for k, v in inputs.items()})
    nc = build_program()
    in_maps = []
    for c in range(8):
        m = dict(common)
        m["x"] = np.ascontiguousarray(x[c])
        in_maps.append(m)
    res = run_bass_kernel_spmd(nc, in_maps, core_ids=list(range(8)))
    out = np.stack([np.asarray(res.results[c]["out"]) for c in range(8)], axis=0)
    return out.astype(np.float32)


# version 04
# speedup vs baseline: 1.0162x; 1.0162x over previous
import contextlib
import numpy as np
import ml_dtypes
import concourse.bass as bass
import concourse.mybir as mybir
from concourse.bass_utils import run_bass_kernel_spmd

F32 = mybir.dt.float32
BF16 = mybir.dt.bfloat16
AF = mybir.ActivationFunctionType
ALU = mybir.AluOpType
AX = mybir.AxisListType

S_TOK = 4096
DM = 2048
NH = 32
NE = 8
ND = 16
NI = 11
NTC = 32
NTILE = 4
TT = 1024
TCT = 8


def build_program():
    nc = bass.Bass("TRN2", target_bir_lowering=False)

    X = nc.dram_tensor("x", [S_TOK, DM], F32, kind="ExternalInput").ap()
    WQ = nc.dram_tensor("wq", [ND, 128, DM], F32, kind="ExternalInput").ap()
    WK = nc.dram_tensor("wk", [DM, DM], F32, kind="ExternalInput").ap()
    WV = nc.dram_tensor("wv", [DM, DM], F32, kind="ExternalInput").ap()
    GW = nc.dram_tensor("gw", [DM, 8], F32, kind="ExternalInput").ap()
    WO4 = nc.dram_tensor("wo4", [4, 128, ND * 512], F32, kind="ExternalInput").ap()
    W13 = nc.dram_tensor("w13", [NE, NI, 128, 4096], F32, kind="ExternalInput").ap()
    W2Q = nc.dram_tensor("w2q", [NE, 4, 128, NI * 512], F32, kind="ExternalInput").ap()
    BOB = nc.dram_tensor("bob", [1, DM], F32, kind="ExternalInput").ap()
    E8 = nc.dram_tensor("e8", [NH, DM], F32, kind="ExternalInput").ap()
    IDB = nc.dram_tensor("idb", [128, 128], BF16, kind="ExternalInput").ap()
    IDF = nc.dram_tensor("idf", [128, 128], F32, kind="ExternalInput").ap()
    ONE1 = nc.dram_tensor("one1", [1, 128], BF16, kind="ExternalInput").ap()
    OUT = nc.dram_tensor("out", [S_TOK, DM], F32, kind="ExternalOutput").ap()

    XSD = nc.dram_tensor("xn_scr_d", [ND, 128, S_TOK], BF16).ap()
    XST = nc.dram_tensor("xn_scr_tc", [NTC, ND, 128, 128], BF16).ap()
    WFM = nc.dram_tensor("wfm", [8, S_TOK], F32).ap()

    ctx = contextlib.ExitStack()
    with ctx:
        ec = ctx.enter_context
        ar1 = ec(nc.sbuf_tensor("ar1", [128, 32768], BF16))
        arv = ec(nc.sbuf_tensor("arv", [128, 33792], BF16))
        ar2 = ec(nc.sbuf_tensor("ar2", [128, 19456], BF16))
        kv_bf = ec(nc.sbuf_tensor("kv_bf", [128, 1040], BF16))
        ksbd = ec(nc.sbuf_tensor("ksbd", [128, 512], BF16))
        e_sb = ec(nc.sbuf_tensor("e_sb", [NH, DM], BF16))
        bo_row = ec(nc.sbuf_tensor("bo_row", [1, DM], BF16))
        idbf = ec(nc.sbuf_tensor("idbf", [128, 128], BF16))
        idf32 = ec(nc.sbuf_tensor("idf32", [128, 128], F32))
        ones1 = ec(nc.sbuf_tensor("ones1", [1, 128], BF16))
        stage = ec(nc.sbuf_tensor("stage", [1, TT], BF16))
        xbuf = ec(nc.sbuf_tensor("xbuf", [128, DM], F32))
        xntok = ec(nc.sbuf_tensor("xntok", [128, DM], BF16))
        xnfm = ec(nc.sbuf_tensor("xnfm", [128, DM], BF16))
        st = ec(nc.sbuf_tensor("st", [128, 12], F32))
        logits = ec(nc.sbuf_tensor("logits", [128, 8], F32))
        t8a = ec(nc.sbuf_tensor("t8a", [128, 8], F32))
        t8b = ec(nc.sbuf_tensor("t8b", [128, 8], F32))
        t8c = ec(nc.sbuf_tensor("t8c", [128, 8], F32))
        w8 = ec(nc.sbuf_tensor("w8", [128, 8], F32))
        wrow = ec(nc.sbuf_tensor("wrow", [8, 128], F32))
        gst = ec(nc.sbuf_tensor("gst", [128, 8], F32))
        nmaxb = ec(nc.sbuf_tensor("nmaxb", [NH, 512], F32))
        rinv = ec(nc.sbuf_tensor("rinv", [NH, 512], BF16))
        bcb = [ec(nc.sbuf_tensor("bcb0", [128, 512], F32)),
               ec(nc.sbuf_tensor("bcb1", [128, 512], F32))]

        psA = ec(nc.psum_tensor("psA", [128, 1024], F32))
        psB = ec(nc.psum_tensor("psB", [128, 1024], F32))
        psC = ec(nc.psum_tensor("psC", [128, 1024], F32))
        psD = ec(nc.psum_tensor("psD", [128, 1024], F32))

        sem_names = ["d_sy", "d_gp", "sT", "sV", "sA", "sP"]
        SEM = {n: ec(nc.semaphore(n)) for n in sem_names}

        y32 = ar1[:, :].bitcast(F32)
        wk_sb = ar1
        wv_sb = arv
        grp = arv
        xn_t = arv[:, 0:16384]
        qz = arv[:, 16384:32768]
        h_sb = arv[:, 16384:27648]
        s_sb = arv[:, 27648:29696].bitcast(F32)
        h1_sb = arv[:, 29696:31744].bitcast(F32)
        bce_sb = arv[:, 31744:33792].bitcast(F32)

        sq_scr = ar2[:, 0:4096].bitcast(F32)
        xn_tc = [ar2[:, 0:2048], ar2[:, 2048:4096]]
        k_sb = ar2[:, 4096:6144]
        vext = ar2[:, 6144:8224]
        kv_f32 = ar2[:, 8224:10304].bitcast(F32)
        gate_sb = ar2[:, 10304:10432]
        wqb = [ar2[:, 0:2048], ar2[:, 2048:4096]]
        woq = [ar2[:, 0:8192], ar2[:, 8192:16384]]
        w13b = [ar2[:, 0:4096], ar2[:, 4096:8192]]
        w2qb = [ar2[:, 8192:13824], ar2[:, 13824:19456]]

        ops = []
        cnt = {n: 0 for n in sem_names}

        def dma(eng, sem, out, in_):
            cnt[sem] += 16
            v = cnt[sem]
            ops.append((eng, lambda e, o=out, i=in_, s=sem: e.dma_start(out=o, in_=i).then_inc(SEM[s], 16),
                        ("inc", sem, 16)))
            return v

        def wait(eng, sem, v):
            if v > 0:
                ops.append((eng, lambda e, s=sem, vv=v: e.wait_ge(SEM[s], vv), ("wait", sem, v)))

        def inc(eng, sem, fn):
            cnt[sem] += 1
            v = cnt[sem]
            ops.append((eng, lambda e, f=fn, s=sem: f(e).then_inc(SEM[s], 1), ("inc", sem, 1)))
            return v

        def just(eng, fn):
            ops.append((eng, lambda e, f=fn: f(e), ("op",)))

        def drain(eng):
            ops.append((eng, lambda e: e.drain(), ("op",)))

        BIG = 1e30

        dma("gpsimd", "d_gp", e_sb[:, :], E8)
        dma("gpsimd", "d_gp", bo_row[:, :], BOB)
        dma("sync", "d_sy", idbf[:, :], IDB)
        dma("sync", "d_sy", idf32[:, :], IDF)
        dma("sync", "d_sy", ones1[:, :], ONE1)
        for d in range(ND):
            dma("gpsimd", "d_gp", wk_sb[:, d * 2048:(d + 1) * 2048], WK[d * 128:(d + 1) * 128, :])
        for d in range(ND):
            dma("gpsimd", "d_gp", wv_sb[:, d * 2048:(d + 1) * 2048], WV[d * 128:(d + 1) * 128, :])
        dma("gpsimd", "d_gp", gate_sb.rearrange("p (d j) -> p d j", d=ND),
            GW.rearrange("(d p) j -> p d j", p=128))

        pKV = [psA[:, 0:512], psA[:, 512:1024], psB[:, 0:512], psB[:, 512:1024]]
        kv0 = psC[:, 0:455]
        kv1 = psC[:, 512:967]
        kv2e = psD[:, 0:130]
        tr8 = psD[0:8, 512:640]
        pgate = psD[:, 640:648]
        ptr4 = [psA[:, 0:64].bitcast(BF16), psA[:, 512:576].bitcast(BF16),
                psB[:, 0:64].bitcast(BF16), psB[:, 512:576].bitcast(BF16)]
        xntok2 = [xntok, ar2[:, 10432:12480]]
        xnfm2 = [xnfm, ar2[:, 12480:14528]]

        def kv_col(pair):
            if pair < 7:
                return pair * 65
            if pair < 14:
                return 455 + (pair - 7) * 65
            return 910 + (pair - 14) * 65

        kvp = []
        for p in range(16):
            if p < 7:
                kvp.append((kv0, p * 65))
            elif p < 14:
                kvp.append((kv1, (p - 7) * 65))
            else:
                kvp.append((kv2e, (p - 14) * 65))

        just("vector", lambda e: e.memset(vext[:, :], 0.0))
        for h in range(NH):
            just("vector", lambda e, h=h: e.memset(vext[:, h * 65 + 64:h * 65 + 65], 1.0))
        mark_vext_ones = inc("vector", "sV", lambda e: e.nop())

        mark_xn_done = 0
        mark_tr_ev = {}
        mark_kev = mark_vev = mark_gev = mark_tr8ev = 0
        mark_kvmm = 0
        mark_wfm = {}
        mark_xsd = {}
        mark_xnfm_free = {}
        for tc in range(NTC):
            buf = tc % 2
            wait("sync", "sV", mark_xn_done)
            mx = dma("sync", "d_sy", xbuf[:, :], X[tc * 128:(tc + 1) * 128, :])
            wait("vector", "d_sy", mx)
            just("vector", lambda e: e.reduce_sum(st[:, 0:1], xbuf[:, :], AX.X))
            wait("scalar", "d_sy", mx)
            mA = inc("scalar", "sA", lambda e: e.activation(
                sq_scr[:, :], xbuf[:, :], AF.Square, accum_out=st[:, 1:2]))
            wait("vector", "sA", mA)
            drain("vector")
            just("vector", lambda e: e.tensor_scalar(st[:, 2:3], st[:, 0:1], 1.0 / DM, None, ALU.mult))
            just("vector", lambda e: e.tensor_scalar(st[:, 3:4], st[:, 1:2], 1.0 / DM, None, ALU.mult))
            drain("vector")
            just("vector", lambda e: e.tensor_tensor(st[:, 4:5], st[:, 2:3], st[:, 2:3], ALU.mult))
            drain("vector")
            just("vector", lambda e: e.tensor_tensor(st[:, 5:6], st[:, 3:4], st[:, 4:5], ALU.subtract))
            drain("vector")
            mV = inc("vector", "sV", lambda e: e.tensor_scalar(st[:, 6:7], st[:, 5:6], 1e-5, None, ALU.add))
            wait("scalar", "sV", mV)
            mA = inc("scalar", "sA", lambda e: e.activation(st[:, 7:8], st[:, 6:7], AF.Sqrt))
            wait("vector", "sA", mA)
            just("vector", lambda e: e.reciprocal(st[:, 8:9], st[:, 7:8]))
            if tc >= 2:
                wait("vector", "sT", mark_tr_last[tc - 2])
            drain("vector")
            mark_xn_done = inc("vector", "sV", lambda e, buf=buf: e.tensor_scalar(
                xntok2[buf][:, :], xbuf[:, :], st[:, 2:3], st[:, 8:9], ALU.subtract, ALU.mult))
            wait("tensor", "sV", mark_xn_done)
            if tc == 0:
                wait("tensor", "d_gp", cnt["d_gp"])
                wait("tensor", "sV", mark_vext_ones)
            wait("tensor", "sA", mark_gev)
            wait("tensor", "sA", mark_tr8ev)
            if tc >= 2:
                wait("tensor", "sT", mark_xnfm_free[tc - 2])
                wait("tensor", "d_sy", mark_xsd[tc - 2])
            for d in range(ND):
                slot = d % 4
                if (tc, d) not in ((0, 0), (0, 1), (0, 2), (0, 3)):
                    pd = tc * ND + d - 4
                    ptc, pdd = divmod(pd, ND)
                    sem, val = mark_tr_ev[(ptc, pdd)]
                    wait("tensor", sem, val)
                mark_tr_last_v = inc("tensor", "sT", lambda e, d=d, buf=buf, slot=slot: e.transpose(
                    ptr4[slot], xntok2[buf][:, d * 128:(d + 1) * 128], idbf[:, :]))
                ev_eng, ev_sem = ("vector", "sV") if d % 2 == 0 else ("scalar", "sA")
                cpf = (lambda e, o, i: e.tensor_copy(o, i)) if d % 2 == 0 else (lambda e, o, i: e.copy(o, i))
                wait(ev_eng, "sT", mark_tr_last_v)
                mark_tr_ev[(tc, d)] = (ev_sem, inc(ev_eng, ev_sem, lambda e, d=d, buf=buf, slot=slot, cpf=cpf: cpf(
                    e, xnfm2[buf][:, d * 128:(d + 1) * 128], ptr4[slot])))
            if tc == 0:
                mark_tr_last = {}
            mark_tr_last[tc] = mark_tr_last_v
            for d in range(ND):
                sem, val = mark_tr_ev[(tc, d)]
                wait("sync", sem, val)
                mark_xsd[tc] = dma("sync", "d_sy", XSD[d][:, tc * 128:(tc + 1) * 128],
                                   xnfm2[buf][:, d * 128:(d + 1) * 128])
            for d in range(ND):
                sem, val = mark_tr_ev[(tc, d)]
                wait("tensor", sem, val)
            wait("tensor", "sV", mark_vev)
            for d in range(ND):
                for gcol in range(4):
                    just("tensor", lambda e, d=d, gcol=gcol, buf=buf: e.matmul(
                        pKV[gcol], xnfm2[buf][:, d * 128:(d + 1) * 128],
                        wk_sb[:, d * 2048 + gcol * 512: d * 2048 + (gcol + 1) * 512],
                        start=(d == 0), stop=(d == ND - 1)))
                fn = lambda e, d=d, buf=buf: e.matmul(
                    pgate, xnfm2[buf][:, d * 128:(d + 1) * 128],
                    gate_sb[:, d * 8:(d + 1) * 8],
                    start=(d == 0), stop=(d == ND - 1))
                if d == ND - 1:
                    mkg = inc("tensor", "sT", fn)
                else:
                    just("tensor", fn)
            wait("vector", "sT", mkg)
            just("vector", lambda e: e.tensor_relu(k_sb[:, 0:1024], psA[:, :]))
            mark_kev = inc("vector", "sV", lambda e: e.tensor_relu(k_sb[:, 1024:2048], psB[:, :]))
            wait("scalar", "sT", mkg)
            mark_gev = inc("scalar", "sA", lambda e: e.copy(logits[:, :], pgate))
            wait("tensor", "sV", mark_kev)
            for d in range(ND):
                for gcol in range(4):
                    fn = lambda e, d=d, gcol=gcol, buf=buf: e.matmul(
                        pKV[gcol], xnfm2[buf][:, d * 128:(d + 1) * 128],
                        wv_sb[:, d * 2048 + gcol * 512: d * 2048 + (gcol + 1) * 512],
                        start=(d == 0), stop=(d == ND - 1))
                    if d == ND - 1 and gcol == 3:
                        mvd = inc("tensor", "sT", fn)
                    else:
                        just("tensor", fn)
            mark_xnfm_free[tc] = mvd
            wait("vector", "sT", mvd)
            just("vector", lambda e: e.tensor_copy(
                vext.rearrange("p (h j) -> p h j", j=65)[:, 0:16, 0:64],
                psA.rearrange("p (h j) -> p h j", j=64)))
            mark_vev = inc("vector", "sV", lambda e: e.tensor_copy(
                vext.rearrange("p (h j) -> p h j", j=65)[:, 16:32, 0:64],
                psB.rearrange("p (h j) -> p h j", j=64)))
            wait("tensor", "sV", mark_vev)
            wait("tensor", "sA", mark_gev)
            for h in range(NH):
                pair, base = h // 2, (h % 2) * 64
                tgt, col = kvp[pair]
                st_flag = (tc == 0) and (pair in (0, 7, 14))
                mark_kvmm = inc("tensor", "sT", lambda e, h=h, base=base, tgt=tgt, col=col, st=st_flag: e.matmul(
                    tgt[base:base + 64, col:col + 65],
                    k_sb[:, h * 64:(h + 1) * 64], vext[:, h * 65:(h + 1) * 65],
                    start=st, stop=(tc == NTC - 1)))
            wait("vector", "sA", mark_gev)
            just("vector", lambda e: e.tensor_reduce(gst[:, 0:1], logits[:, :], AX.X, ALU.max))
            drain("vector")
            just("vector", lambda e: e.tensor_scalar(t8a[:, :], logits[:, :], gst[:, 0:1], None, ALU.subtract))
            drain("vector")
            just("vector", lambda e: e.tensor_scalar(t8b[:, :], t8a[:, :], 0.0, None, ALU.is_ge))
            drain("vector")
            just("vector", lambda e: e.scalar_tensor_tensor(t8c[:, :], t8b[:, :], -BIG, t8a[:, :], ALU.mult, ALU.add))
            drain("vector")
            mV = inc("vector", "sV", lambda e: e.tensor_reduce(gst[:, 1:2], t8c[:, :], AX.X, ALU.max))
            wait("scalar", "sV", mV)
            just("scalar", lambda e: e.activation(gst[:, 2:3], gst[:, 1:2], AF.Exp))
            mA = inc("scalar", "sA", lambda e: e.activation(t8b[:, :], t8a[:, :], AF.Exp))
            wait("vector", "sA", mA)
            just("vector", lambda e: e.tensor_scalar(gst[:, 3:4], gst[:, 2:3], 1.0, None, ALU.add))
            drain("vector")
            just("vector", lambda e: e.reciprocal(gst[:, 4:5], gst[:, 3:4]))
            drain("vector")
            just("vector", lambda e: e.tensor_scalar(t8c[:, :], t8a[:, :], gst[:, 1:2], None, ALU.is_ge))
            drain("vector")
            just("vector", lambda e: e.tensor_tensor(t8a[:, :], t8c[:, :], t8b[:, :], ALU.mult))
            drain("vector")
            mV = inc("vector", "sV", lambda e: e.tensor_scalar(w8[:, :], t8a[:, :], gst[:, 4:5], None, ALU.mult))
            wait("tensor", "sV", mV)
            wait("tensor", "sA", mark_tr8ev)
            mT = inc("tensor", "sT", lambda e: e.transpose(tr8, w8[:, :], idf32[:, :]))
            wait("scalar", "sT", mT)
            if tc > 0:
                wait("scalar", "d_sy", mark_wfm[tc - 1])
            mark_tr8ev = inc("scalar", "sA", lambda e: e.copy(wrow[:, :], tr8))
            wait("sync", "sA", mark_tr8ev)
            mark_wfm[tc] = dma("sync", "d_sy", WFM[:, tc * 128:(tc + 1) * 128], wrow[:, :])
        mark_wfm_done = cnt["d_sy"]

        wait("vector", "sT", mark_kvmm)
        just("vector", lambda e: e.tensor_copy(kv_f32[:, 0:455], kv0))
        just("vector", lambda e: e.tensor_copy(kv_f32[:, 455:910], kv1))
        just("vector", lambda e: e.tensor_copy(kv_f32[:, 910:1040], kv2e))
        just("vector", lambda e: e.tensor_copy(kv_bf[:, 0:455], kv0))
        just("vector", lambda e: e.tensor_copy(kv_bf[:, 455:910], kv1))
        just("vector", lambda e: e.tensor_copy(kv_bf[:, 910:1040], kv2e))
        drain("vector")
        just("vector", lambda e: e.memset(ksbd[:, :], 0.0))
        for h in range(NH):
            base = (h % 2) * 64
            kc = kv_col(h // 2) + 64
            dst_col = (h // 2) * 32 + h
            just("vector", lambda e, base=base, kc=kc, dc=dst_col: e.tensor_copy(
                ksbd[base:base + 64, dc:dc + 1], kv_f32[base:base + 64, kc:kc + 1]))
        p1_done = inc("vector", "sV", lambda e: e.nop())
        p1_sA = cnt["sA"]

        mark_y_store = {t: 0 for t in range(TCT)}
        mark_psq_ev = [0, 0]
        mark_pt3_ev = [0, 0]
        mark_pyq_ev = [0, 0, 0, 0]
        mark_bcev = [0, 0]
        mark_sev = [0, 0]
        mark_uev = [0, 0]
        mark_silu = mark_h1 = mark_h = 0
        mark_w13 = {0: 0, 1: 0}
        mark_wq = [0, 0]
        mark_woq = [0, 0]
        mark_w2q = [0, 0]
        last_wq_mm = [0, 0]
        last_woq_mm = [0, 0]
        last_w13_mm = [0, 0]
        last_w2q_mm = [0, 0]
        last_stage_mm = 0
        mark_qev = {}
        mark_zmul = {}
        mark_yadd_last = {}

        wait("sync", "sV", p1_done)
        wait("sync", "sT", mark_kvmm)
        mark_xnt_load = 0
        for d in range(ND):
            mark_xnt_load = dma("sync", "d_sy", xn_t[:, d * 1024:(d + 1) * 1024], XSD[d][:, 0:TT])

        for tile in range(NTILE):
            t0 = tile * TT
            wait("gpsimd", "sV", p1_done)
            wait("gpsimd", "sT", last_w13_mm[0])
            wait("gpsimd", "sT", last_w13_mm[1])
            wait("gpsimd", "sT", last_w2q_mm[0])
            wait("gpsimd", "sT", last_w2q_mm[1])
            mark_wq[0] = dma("gpsimd", "d_gp", wqb[0], WQ[0])
            psq = [psA, psB]
            for q in range(ND):
                b = q % 2
                if q + 1 < ND:
                    nb = (q + 1) % 2
                    wait("gpsimd", "sT", last_wq_mm[nb])
                    mark_wq[nb] = dma("gpsimd", "d_gp", wqb[nb], WQ[q + 1])
                wait("tensor", "d_gp", mark_wq[b])
                wait("tensor", "d_sy", mark_xnt_load)
                wait("tensor", "sV", mark_psq_ev[b])
                if tile == 0 and q == 0:
                    wait("tensor", "sV", p1_done)
                if tile > 0 and q == 0:
                    for pb in range(4):
                        wait("tensor", "sV", mark_pyq_ev[pb])
                for d in range(ND):
                    for half in range(2):
                        fn = lambda e, b=b, d=d, half=half: e.matmul(
                            psq[b][:, half * 512:(half + 1) * 512],
                            wqb[b][:, d * 128:(d + 1) * 128],
                            xn_t[:, d * 1024 + half * 512: d * 1024 + (half + 1) * 512],
                            start=(d == 0), stop=(d == ND - 1))
                        if d == ND - 1 and half == 1:
                            last_wq_mm[b] = last_q_mm = inc("tensor", "sT", fn)
                        else:
                            just("tensor", fn)
                wait("vector", "sT", last_q_mm)
                mark_psq_ev[b] = inc("vector", "sV", lambda e, q=q, b=b: e.tensor_relu(
                    qz[:, q * 1024:(q + 1) * 1024], psq[b][:, :]))
                mark_qev[q] = mark_psq_ev[b]

            pn = psC[0:NH, 0:512]
            pbc = [psC[:, 512:1024], psD[:, 0:512]]
            pqkv = [psD[:, 512:1024], psA[:, 0:512]]
            mark_nred = 0
            mark_zpar = [0, 0]
            for sg in range(2):
                for q in range(ND):
                    wait("tensor", "sV", mark_qev[q])
                if tile == 0 and sg == 0:
                    wait("tensor", "sA", p1_sA)
                if sg == 1:
                    wait("tensor", "sV", mark_nred)
                for c in range(ND):
                    fn = lambda e, c=c, sg=sg: e.matmul(
                        pn, ksbd[:, c * 32:(c + 1) * 32],
                        qz[:, c * 1024 + sg * 512: c * 1024 + (sg + 1) * 512],
                        start=(c == 0), stop=(c == ND - 1))
                    if c == ND - 1:
                        m_n = inc("tensor", "sT", fn)
                    else:
                        just("tensor", fn)
                wait("vector", "sT", m_n)
                just("vector", lambda e: e.tensor_scalar(nmaxb[:, :], pn, 1e-6, None, ALU.max))
                drain("vector")
                just("vector", lambda e: e.reciprocal(nmaxb[:, :], nmaxb[:, :]))
                drain("vector")
                mark_nred = inc("vector", "sV", lambda e: e.tensor_copy(rinv[:, :], nmaxb[:, :]))
                wait("tensor", "sV", mark_nred)
                for c in range(ND):
                    b = c % 2
                    wait("tensor", "sA", mark_bcev[b])
                    mbc = inc("tensor", "sT", lambda e, c=c, b=b: e.matmul(
                        pbc[b], e_sb[:, c * 128:(c + 1) * 128], rinv[:, :],
                        start=True, stop=True))
                    wait("tensor", "sV", mark_zpar[b])
                    just("tensor", lambda e, c=c, b=b, sg=sg: e.matmul(
                        pqkv[b][0:64, :], kv_bf[0:64, kv_col(c):kv_col(c) + 64],
                        qz[0:64, c * 1024 + sg * 512: c * 1024 + (sg + 1) * 512],
                        start=True, stop=True))
                    mqk = inc("tensor", "sT", lambda e, c=c, b=b, sg=sg: e.matmul(
                        pqkv[b][64:128, :], kv_bf[64:128, kv_col(c):kv_col(c) + 64],
                        qz[64:128, c * 1024 + sg * 512: c * 1024 + (sg + 1) * 512],
                        start=True, stop=True))
                    wait("scalar", "sT", mbc)
                    wait("scalar", "sV", mark_zpar[b])
                    mark_bcev[b] = inc("scalar", "sA", lambda e, b=b: e.copy(bcb[b][:, :], pbc[b]))
                    wait("vector", "sT", mqk)
                    wait("vector", "sA", mark_bcev[b])
                    mark_zpar[b] = mark_zmul[(sg, c)] = inc("vector", "sV", lambda e, c=c, b=b, sg=sg: e.tensor_tensor(
                        qz[:, c * 1024 + sg * 512: c * 1024 + (sg + 1) * 512],
                        pqkv[b], bcb[b][:, :], ALU.mult))

            pt3 = [psA[:, 0:512], psB[:, 0:512]]
            wait("gpsimd", "sT", last_wq_mm[0])
            wait("gpsimd", "sT", last_wq_mm[1])
            mark_woq[0] = dma("gpsimd", "d_gp", woq[0], WO4[0])
            for q4 in range(4):
                b = q4 % 2
                if q4 + 1 < 4:
                    nb = (q4 + 1) % 2
                    wait("gpsimd", "sT", last_woq_mm[nb])
                    mark_woq[nb] = dma("gpsimd", "d_gp", woq[nb], WO4[q4 + 1])
                wait("tensor", "d_gp", mark_woq[b])
                for tcl in range(TCT):
                    pb = tcl % 2
                    if q4 == 0 and tcl == 0:
                        for c in range(ND):
                            wait("tensor", "sV", mark_zmul[(1, c)])
                    wait("tensor", "sV", mark_pt3_ev[pb])
                    if tile > 0:
                        wait("tensor", "d_sy", mark_y_store[tcl])
                    for c in range(ND):
                        just("tensor", lambda e, c=c, tcl=tcl, b=b, pb=pb: e.matmul(
                            pt3[pb], qz[:, c * 1024 + tcl * 128: c * 1024 + tcl * 128 + 128],
                            woq[b][:, c * 512:(c + 1) * 512],
                            start=(c == 0), stop=False))
                    last_woq_mm[b] = m3 = inc("tensor", "sT", lambda e, q4=q4, pb=pb: e.matmul(
                        pt3[pb], ones1[0:1, :], bo_row[0:1, q4 * 512:(q4 + 1) * 512],
                        start=False, stop=True))
                    wait("vector", "sT", m3)
                    mark_pt3_ev[pb] = inc("vector", "sV", lambda e, tcl=tcl, q4=q4, pb=pb: e.tensor_copy(
                        y32[:, tcl * 2048 + q4 * 512: tcl * 2048 + (q4 + 1) * 512], pt3[pb]))

            for ex in range(NE):
                wait("gpsimd", "d_sy", mark_wfm_done)
                wait("gpsimd", "sT", last_stage_mm)
                mst = dma("gpsimd", "d_gp", stage[0:1, :], WFM[ex:ex + 1, t0:t0 + TT])
                wait("tensor", "d_gp", mst)
                if ex == 0:
                    wait("tensor", "sV", mark_pt3_ev[0])
                    wait("tensor", "sV", mark_pt3_ev[1])
                else:
                    wait("tensor", "sA", mark_sev[0])
                    wait("tensor", "sV", mark_uev[0])
                    for pb in range(4):
                        wait("tensor", "sV", mark_pyq_ev[pb])
                wait("tensor", "sV", mark_h1)
                for half in range(2):
                    fn = lambda e, half=half: e.matmul(
                        psA[:, half * 512:(half + 1) * 512], ones1[0:1, :],
                        stage[0:1, half * 512:(half + 1) * 512], start=True, stop=True)
                    if half == 1:
                        last_stage_mm = mbs = inc("tensor", "sT", fn)
                    else:
                        just("tensor", fn)
                wait("scalar", "sT", mbs)
                mark_bce = inc("scalar", "sA", lambda e: e.copy(bce_sb[:, :], psA[:, :]))
                wait("gpsimd", "sT", last_w13_mm[0])
                if ex == 0:
                    wait("gpsimd", "sT", last_woq_mm[0])
                    wait("gpsimd", "sT", last_woq_mm[1])
                mark_w13[0] = dma("gpsimd", "d_gp", w13b[0], W13[ex, 0])
                pg = [psA, psC]
                pu = [psB, psD]
                for i in range(NI):
                    b = i % 2
                    if i + 1 < NI:
                        nb = (i + 1) % 2
                        wait("gpsimd", "sT", last_w13_mm[nb])
                        mark_w13[nb] = dma("gpsimd", "d_gp", w13b[nb], W13[ex, i + 1])
                    wait("tensor", "d_gp", mark_w13[b])
                    if i == 0:
                        wait("tensor", "sA", mark_bce)
                    wait("tensor", "sA", mark_sev[b])
                    wait("tensor", "sV", mark_uev[b])
                    for d in range(ND):
                        for half in range(2):
                            just("tensor", lambda e, b=b, d=d, half=half: e.matmul(
                                pg[b][:, half * 512:(half + 1) * 512],
                                w13b[b][:, d * 128:(d + 1) * 128],
                                xn_t[:, d * 1024 + half * 512: d * 1024 + (half + 1) * 512],
                                start=(d == 0), stop=(d == ND - 1)))
                    for d in range(ND):
                        for half in range(2):
                            fn = lambda e, b=b, d=d, half=half: e.matmul(
                                pu[b][:, half * 512:(half + 1) * 512],
                                w13b[b][:, 2048 + d * 128: 2048 + (d + 1) * 128],
                                xn_t[:, d * 1024 + half * 512: d * 1024 + (half + 1) * 512],
                                start=(d == 0), stop=(d == ND - 1))
                            if d == ND - 1 and half == 1:
                                last_w13_mm[b] = mgu = inc("tensor", "sT", fn)
                            else:
                                just("tensor", fn)
                    wait("scalar", "sT", mgu)
                    wait("scalar", "sP", mark_h)
                    mark_sev[b] = mark_silu = inc("scalar", "sA", lambda e, b=b: e.activation(
                        s_sb[:, :], pg[b][:, :], AF.Silu))
                    wait("vector", "sT", mgu)
                    wait("vector", "sP", mark_h)
                    mark_uev[b] = mark_h1 = inc("vector", "sV", lambda e, b=b: e.tensor_tensor(
                        h1_sb[:, :], pu[b][:, :], bce_sb[:, :], ALU.mult))
                    wait("gpsimd", "sA", mark_silu)
                    wait("gpsimd", "sV", mark_h1)
                    mark_h = inc("gpsimd", "sP", lambda e, i=i: e.tensor_tensor(
                        h_sb[:, i * 1024:(i + 1) * 1024], h1_sb[:, :], s_sb[:, :], ALU.mult))
                wait("gpsimd", "sT", last_w2q_mm[0])
                mark_w2q[0] = dma("gpsimd", "d_gp", w2qb[0], W2Q[ex, 0])
                pyq = [psA[:, 0:512], psB[:, 0:512], psC[:, 0:512], psD[:, 0:512]]
                for q4 in range(4):
                    b = q4 % 2
                    if q4 + 1 < 4:
                        nb = (q4 + 1) % 2
                        wait("gpsimd", "sT", last_w2q_mm[nb])
                        mark_w2q[nb] = dma("gpsimd", "d_gp", w2qb[nb], W2Q[ex, q4 + 1])
                    wait("tensor", "d_gp", mark_w2q[b])
                    if q4 == 0:
                        wait("tensor", "sP", mark_h)
                        wait("tensor", "sA", mark_sev[0])
                        wait("tensor", "sA", mark_sev[1])
                        wait("tensor", "sV", mark_uev[0])
                        wait("tensor", "sV", mark_uev[1])
                    for tcl in range(TCT):
                        pb = (q4 * TCT + tcl) % 4
                        wait("tensor", "sV", mark_pyq_ev[pb])
                        for i in range(NI):
                            fn = lambda e, i=i, tcl=tcl, b=b, pb=pb: e.matmul(
                                pyq[pb], h_sb[:, i * 1024 + tcl * 128: i * 1024 + tcl * 128 + 128],
                                w2qb[b][:, i * 512:(i + 1) * 512],
                                start=(i == 0), stop=(i == NI - 1))
                            if i == NI - 1:
                                last_w2q_mm[b] = m4 = inc("tensor", "sT", fn)
                            else:
                                just("tensor", fn)
                        wait("vector", "sT", m4)
                        mark_pyq_ev[pb] = inc("vector", "sV", lambda e, tcl=tcl, q4=q4, pb=pb: e.tensor_tensor(
                            y32[:, tcl * 2048 + q4 * 512: tcl * 2048 + (q4 + 1) * 512],
                            pyq[pb],
                            y32[:, tcl * 2048 + q4 * 512: tcl * 2048 + (q4 + 1) * 512], ALU.add))
                        if ex == NE - 1:
                            mark_yadd_last[(q4, tcl)] = mark_pyq_ev[pb]
                if ex == NE - 1 and tile + 1 < NTILE:
                    wait("sync", "sT", mgu)
                    for d in range(ND):
                        mark_xnt_load = dma("sync", "d_sy", xn_t[:, d * 1024:(d + 1) * 1024],
                                            XSD[d][:, (tile + 1) * TT:(tile + 2) * TT])

            for tcl in range(TCT):
                mxl = dma("sync", "d_sy", xbuf[:, :], X[t0 + tcl * 128: t0 + (tcl + 1) * 128, :])
                for q4 in range(4):
                    wait("vector", "sV", mark_yadd_last[(q4, tcl)])
                wait("vector", "d_sy", mxl)
                mF = inc("vector", "sV", lambda e, tcl=tcl: e.tensor_tensor(
                    y32[:, tcl * 2048:(tcl + 1) * 2048], xbuf[:, :],
                    y32[:, tcl * 2048:(tcl + 1) * 2048], ALU.add))
                wait("sync", "sV", mF)
                mark_y_store[tcl] = dma("sync", "d_sy", OUT[t0 + tcl * 128: t0 + (tcl + 1) * 128, :],
                                        y32[:, tcl * 2048:(tcl + 1) * 2048])

        wait("sync", "d_sy", cnt["d_sy"])
        wait("gpsimd", "d_gp", cnt["d_gp"])

        engs = ["sync", "tensor", "vector", "scalar", "gpsimd"]
        queues = {n: [] for n in engs}
        for name, fn, meta in ops:
            queues[name].append(meta)
        pos = {n: 0 for n in engs}
        sem_now = {n: 0 for n in sem_names}
        progress = True
        while progress:
            progress = False
            for n in engs:
                while pos[n] < len(queues[n]):
                    meta = queues[n][pos[n]]
                    if meta[0] == "wait":
                        if sem_now[meta[1]] < meta[2]:
                            break
                    elif meta[0] == "inc":
                        sem_now[meta[1]] += meta[2]
                    pos[n] += 1
                    progress = True
        stuck = {n: (pos[n], len(queues[n])) for n in engs if pos[n] < len(queues[n])}
        if stuck:
            detail = {n: queues[n][pos[n]] for n in stuck}
            raise RuntimeError(f"plan deadlock: {stuck} blocked at {detail} sems={sem_now}")

        with nc.Block() as block:
            @block.sync
            def _(eng):
                for name, fn, _m in ops:
                    if name == "sync":
                        fn(eng)

            @block.tensor
            def _(eng):
                for name, fn, _m in ops:
                    if name == "tensor":
                        fn(eng)

            @block.vector
            def _(eng):
                for name, fn, _m in ops:
                    if name == "vector":
                        fn(eng)

            @block.scalar
            def _(eng):
                for name, fn, _m in ops:
                    if name == "scalar":
                        fn(eng)

            @block.gpsimd
            def _(eng):
                for name, fn, _m in ops:
                    if name == "gpsimd":
                        fn(eng)

    return nc


def _prep(inputs):
    wq = np.ascontiguousarray(
        inputs["L0_attn_Wq"].reshape(16, 128, 16, 128).transpose(2, 1, 0, 3).reshape(16, 128, 2048))
    wo4 = np.ascontiguousarray(
        inputs["L0_attn_Wo"].reshape(16, 128, 4, 512).transpose(2, 1, 0, 3).reshape(4, 128, 16 * 512))
    w13a = inputs["L0_ffn_w1"].reshape(8, 16, 128, 11, 128).transpose(0, 3, 2, 1, 4)
    w13c = inputs["L0_ffn_w3"].reshape(8, 16, 128, 11, 128).transpose(0, 3, 2, 1, 4)
    w13 = np.ascontiguousarray(np.stack([w13a, w13c], axis=3).reshape(8, 11, 128, 4096))
    w2q = np.ascontiguousarray(
        inputs["L0_ffn_w2"].reshape(8, 11, 128, 4, 512).transpose(0, 3, 2, 1, 4).reshape(8, 4, 128, 11 * 512))
    bob = np.ascontiguousarray(inputs["L0_attn_bo"][None, :])
    e8 = np.zeros((32, 2048), dtype=np.float32)
    for h in range(32):
        e8[h, h * 64:(h + 1) * 64] = 1.0
    return {
        "wq": wq, "wk": np.ascontiguousarray(inputs["L0_attn_Wk"]),
        "wv": np.ascontiguousarray(inputs["L0_attn_Wv"]),
        "gw": np.ascontiguousarray(inputs["L0_ffn_gate_w"]),
        "wo4": wo4, "w13": w13, "w2q": w2q, "bob": bob, "e8": e8,
        "idb": np.eye(128, dtype=ml_dtypes.bfloat16),
        "idf": np.eye(128, dtype=np.float32),
        "one1": np.ones((1, 128), dtype=ml_dtypes.bfloat16),
    }


def kernel(**inputs):
    x = np.asarray(inputs["L0_attn_hidden_states"])
    common = _prep({k: np.asarray(v) for k, v in inputs.items()})
    nc = build_program()
    in_maps = []
    for c in range(8):
        m = dict(common)
        m["x"] = np.ascontiguousarray(x[c])
        in_maps.append(m)
    res = run_bass_kernel_spmd(nc, in_maps, core_ids=list(range(8)))
    out = np.stack([np.asarray(res.results[c]["out"]) for c in range(8)], axis=0)
    return out.astype(np.float32)


# version 05
# speedup vs baseline: 1.2371x; 1.2173x over previous
import contextlib
import numpy as np
import ml_dtypes
import concourse.bass as bass
import concourse.mybir as mybir
from concourse.bass_utils import run_bass_kernel_spmd

F32 = mybir.dt.float32
BF16 = mybir.dt.bfloat16
AF = mybir.ActivationFunctionType
ALU = mybir.AluOpType
AX = mybir.AxisListType

S_TOK = 4096
DM = 2048
NH = 32
NE = 8
ND = 16
NI = 11
NTC = 32
NTILE = 4
TT = 1024
TCT = 8


def build_program():
    nc = bass.Bass("TRN2", target_bir_lowering=False)

    X = nc.dram_tensor("x", [S_TOK, DM], F32, kind="ExternalInput").ap()
    WQ = nc.dram_tensor("wq", [ND, 128, DM], F32, kind="ExternalInput").ap()
    WK = nc.dram_tensor("wk", [DM, DM], F32, kind="ExternalInput").ap()
    WV = nc.dram_tensor("wv", [DM, DM], F32, kind="ExternalInput").ap()
    GW = nc.dram_tensor("gw", [DM, 8], F32, kind="ExternalInput").ap()
    WO4 = nc.dram_tensor("wo4", [4, 128, ND * 512], F32, kind="ExternalInput").ap()
    W13 = nc.dram_tensor("w13", [NE, NI, 128, 4096], F32, kind="ExternalInput").ap()
    W2Q = nc.dram_tensor("w2q", [NE, 4, 128, NI * 512], F32, kind="ExternalInput").ap()
    BOB = nc.dram_tensor("bob", [1, DM], F32, kind="ExternalInput").ap()
    E8 = nc.dram_tensor("e8", [NH, DM], F32, kind="ExternalInput").ap()
    IDB = nc.dram_tensor("idb", [128, 128], BF16, kind="ExternalInput").ap()
    IDF = nc.dram_tensor("idf", [128, 128], F32, kind="ExternalInput").ap()
    ONE1 = nc.dram_tensor("one1", [1, 128], BF16, kind="ExternalInput").ap()
    OUT = nc.dram_tensor("out", [S_TOK, DM], F32, kind="ExternalOutput").ap()

    XSD = nc.dram_tensor("xn_scr_d", [ND, 128, S_TOK], BF16).ap()
    XST = nc.dram_tensor("xn_scr_tc", [NTC, ND, 128, 128], BF16).ap()
    WFM = nc.dram_tensor("wfm", [8, S_TOK], F32).ap()

    ctx = contextlib.ExitStack()
    with ctx:
        ec = ctx.enter_context
        ar1 = ec(nc.sbuf_tensor("ar1", [128, 32768], BF16))
        arv = ec(nc.sbuf_tensor("arv", [128, 33792], BF16))
        ar2 = ec(nc.sbuf_tensor("ar2", [128, 19456], BF16))
        kv_bf = ec(nc.sbuf_tensor("kv_bf", [128, 1040], BF16))
        ksbd = ec(nc.sbuf_tensor("ksbd", [128, 512], BF16))
        e_sb = ec(nc.sbuf_tensor("e_sb", [NH, DM], BF16))
        bo_row = ec(nc.sbuf_tensor("bo_row", [1, DM], BF16))
        idbf = ec(nc.sbuf_tensor("idbf", [128, 128], BF16))
        idf32 = ec(nc.sbuf_tensor("idf32", [128, 128], F32))
        ones1 = ec(nc.sbuf_tensor("ones1", [1, 128], BF16))
        stage = ec(nc.sbuf_tensor("stage", [1, TT], BF16))
        xbuf = ec(nc.sbuf_tensor("xbuf", [128, DM], F32))
        xntok = ec(nc.sbuf_tensor("xntok", [128, DM], BF16))
        xnfm = ec(nc.sbuf_tensor("xnfm", [128, DM], BF16))
        st = ec(nc.sbuf_tensor("st", [128, 12], F32))
        logits = ec(nc.sbuf_tensor("logits", [128, 8], F32))
        t8a = ec(nc.sbuf_tensor("t8a", [128, 8], F32))
        t8b = ec(nc.sbuf_tensor("t8b", [128, 8], F32))
        t8c = ec(nc.sbuf_tensor("t8c", [128, 8], F32))
        w8 = ec(nc.sbuf_tensor("w8", [128, 8], F32))
        wrow128 = ec(nc.sbuf_tensor("wrow128", [128, 256], F32))
        gst = ec(nc.sbuf_tensor("gst", [128, 8], F32))
        wtok = ec(nc.sbuf_tensor("wtok", [128, 256], F32))
        nmaxb = ec(nc.sbuf_tensor("nmaxb", [NH, 512], F32))
        rinv = ec(nc.sbuf_tensor("rinv", [NH, 512], BF16))
        bcb = [ec(nc.sbuf_tensor("bcb0", [128, 512], F32)),
               ec(nc.sbuf_tensor("bcb1", [128, 512], F32))]

        psA = ec(nc.psum_tensor("psA", [128, 1024], F32))
        psB = ec(nc.psum_tensor("psB", [128, 1024], F32))
        psC = ec(nc.psum_tensor("psC", [128, 1024], F32))
        psD = ec(nc.psum_tensor("psD", [128, 1024], F32))

        sem_names = ["d_sy", "d_gp", "sT", "sV", "sA", "sP"]
        SEM = {n: ec(nc.semaphore(n)) for n in sem_names}

        y32 = ar1[:, :].bitcast(F32)
        wk_sb = ar1
        wv_sb = arv
        grp = arv
        xn_t = arv[:, 0:16384]
        qz = arv[:, 16384:32768]
        h_sb = arv[:, 16384:27648]
        s_sb = arv[:, 27648:29696].bitcast(F32)
        h1_sb = arv[:, 29696:31744].bitcast(F32)
        bce_sb = arv[:, 31744:33792].bitcast(F32)

        sq_scr = ar2[:, 0:4096].bitcast(F32)
        xn_tc = [ar2[:, 0:2048], ar2[:, 2048:4096]]
        k_sb = ar2[:, 4096:6144]
        vext = ar2[:, 6144:8224]
        kv_f32 = ar2[:, 8224:10304].bitcast(F32)
        gate_sb = ar2[:, 10304:10432]
        wqb = [ar2[:, 0:2048], ar2[:, 2048:4096]]
        woq = [ar2[:, 0:8192], ar2[:, 8192:16384]]
        w13b = [ar2[:, 0:4096], ar2[:, 4096:8192]]
        w2qb = [ar2[:, 8192:13824], ar2[:, 13824:19456]]

        ops = []
        cnt = {n: 0 for n in sem_names}

        def dma(eng, sem, out, in_):
            cnt[sem] += 16
            v = cnt[sem]
            ops.append((eng, lambda e, o=out, i=in_, s=sem: e.dma_start(out=o, in_=i).then_inc(SEM[s], 16),
                        ("inc", sem, 16)))
            return v

        def wait(eng, sem, v):
            if v > 0:
                ops.append((eng, lambda e, s=sem, vv=v: e.wait_ge(SEM[s], vv), ("wait", sem, v)))

        def inc(eng, sem, fn):
            cnt[sem] += 1
            v = cnt[sem]
            ops.append((eng, lambda e, f=fn, s=sem: f(e).then_inc(SEM[s], 1), ("inc", sem, 1)))
            return v

        def just(eng, fn):
            ops.append((eng, lambda e, f=fn: f(e), ("op",)))

        def drain(eng):
            ops.append((eng, lambda e: e.drain(), ("op",)))

        BIG = 1e30

        dma("gpsimd", "d_gp", e_sb[:, :], E8)
        dma("gpsimd", "d_gp", bo_row[:, :], BOB)
        dma("sync", "d_sy", idbf[:, :], IDB)
        dma("sync", "d_sy", idf32[:, :], IDF)
        dma("sync", "d_sy", ones1[:, :], ONE1)
        for d in range(ND):
            dma("gpsimd", "d_gp", wk_sb[:, d * 2048:(d + 1) * 2048], WK[d * 128:(d + 1) * 128, :])
        for d in range(ND):
            dma("gpsimd", "d_gp", wv_sb[:, d * 2048:(d + 1) * 2048], WV[d * 128:(d + 1) * 128, :])
        dma("gpsimd", "d_gp", gate_sb.rearrange("p (d j) -> p d j", d=ND),
            GW.rearrange("(d p) j -> p d j", p=128))

        pKV = [psA[:, 0:512], psA[:, 512:1024], psB[:, 0:512], psB[:, 512:1024]]
        kv0 = psC[:, 0:455]
        kv1 = psC[:, 512:967]
        kv2e = psD[:, 0:130]
        pgate = psD[:, 640:648]
        ptr4 = [psA[:, 0:64].bitcast(BF16), psA[:, 512:576].bitcast(BF16),
                psB[:, 0:64].bitcast(BF16), psB[:, 512:576].bitcast(BF16)]
        xntok2 = [xntok, ar2[:, 10432:12480]]
        xnfm2 = [xnfm, ar2[:, 12480:14528]]

        def kv_col(pair):
            if pair < 7:
                return pair * 65
            if pair < 14:
                return 455 + (pair - 7) * 65
            return 910 + (pair - 14) * 65

        kvp = []
        for p in range(16):
            if p < 7:
                kvp.append((kv0, p * 65))
            elif p < 14:
                kvp.append((kv1, (p - 7) * 65))
            else:
                kvp.append((kv2e, (p - 14) * 65))

        just("vector", lambda e: e.memset(vext[:, :], 0.0))
        for h in range(NH):
            just("vector", lambda e, h=h: e.memset(vext[:, h * 65 + 64:h * 65 + 65], 1.0))
        mark_vext_ones = inc("vector", "sV", lambda e: e.nop())

        mark_xn_done = 0
        mark_tr_ev = {}
        mark_kev = mark_vev = mark_gev = 0
        mark_kvmm = 0
        mark_wfm = {}
        mark_xsd = {}
        mark_xnfm_free = {}
        for tc in range(NTC):
            buf = tc % 2
            wait("sync", "sV", mark_xn_done)
            mx = dma("sync", "d_sy", xbuf[:, :], X[tc * 128:(tc + 1) * 128, :])
            wait("vector", "d_sy", mx)
            just("vector", lambda e: e.reduce_sum(st[:, 0:1], xbuf[:, :], AX.X))
            wait("scalar", "d_sy", mx)
            mA = inc("scalar", "sA", lambda e: e.activation(
                sq_scr[:, :], xbuf[:, :], AF.Square, accum_out=st[:, 1:2]))
            wait("vector", "sA", mA)
            drain("vector")
            just("vector", lambda e: e.tensor_scalar(st[:, 2:3], st[:, 0:1], 1.0 / DM, None, ALU.mult))
            just("vector", lambda e: e.tensor_scalar(st[:, 3:4], st[:, 1:2], 1.0 / DM, None, ALU.mult))
            drain("vector")
            just("vector", lambda e: e.tensor_tensor(st[:, 4:5], st[:, 2:3], st[:, 2:3], ALU.mult))
            drain("vector")
            just("vector", lambda e: e.tensor_tensor(st[:, 5:6], st[:, 3:4], st[:, 4:5], ALU.subtract))
            drain("vector")
            mV = inc("vector", "sV", lambda e: e.tensor_scalar(st[:, 6:7], st[:, 5:6], 1e-5, None, ALU.add))
            wait("scalar", "sV", mV)
            mA = inc("scalar", "sA", lambda e: e.activation(st[:, 7:8], st[:, 6:7], AF.Sqrt))
            wait("vector", "sA", mA)
            just("vector", lambda e: e.reciprocal(st[:, 8:9], st[:, 7:8]))
            if tc >= 2:
                wait("vector", "sT", mark_tr_last[tc - 2])
            drain("vector")
            mark_xn_done = inc("vector", "sV", lambda e, buf=buf: e.tensor_scalar(
                xntok2[buf][:, :], xbuf[:, :], st[:, 2:3], st[:, 8:9], ALU.subtract, ALU.mult))
            wait("tensor", "sV", mark_xn_done)
            if tc == 0:
                wait("tensor", "d_gp", cnt["d_gp"])
                wait("tensor", "sV", mark_vext_ones)
            wait("tensor", "sA", mark_gev)
            if tc >= 2:
                wait("tensor", "sT", mark_xnfm_free[tc - 2])
                wait("tensor", "d_sy", mark_xsd[tc - 2])
            for d in range(ND):
                slot = d % 4
                if (tc, d) not in ((0, 0), (0, 1), (0, 2), (0, 3)):
                    pd = tc * ND + d - 4
                    ptc, pdd = divmod(pd, ND)
                    sem, val = mark_tr_ev[(ptc, pdd)]
                    wait("tensor", sem, val)
                mark_tr_last_v = inc("tensor", "sT", lambda e, d=d, buf=buf, slot=slot: e.transpose(
                    ptr4[slot], xntok2[buf][:, d * 128:(d + 1) * 128], idbf[:, :]))
                ev_eng, ev_sem = ("vector", "sV") if d % 2 == 0 else ("scalar", "sA")
                cpf = (lambda e, o, i: e.tensor_copy(o, i)) if d % 2 == 0 else (lambda e, o, i: e.copy(o, i))
                wait(ev_eng, "sT", mark_tr_last_v)
                mark_tr_ev[(tc, d)] = (ev_sem, inc(ev_eng, ev_sem, lambda e, d=d, buf=buf, slot=slot, cpf=cpf: cpf(
                    e, xnfm2[buf][:, d * 128:(d + 1) * 128], ptr4[slot])))
            if tc == 0:
                mark_tr_last = {}
            mark_tr_last[tc] = mark_tr_last_v
            for d in range(ND):
                sem, val = mark_tr_ev[(tc, d)]
                wait("sync", sem, val)
                mark_xsd[tc] = dma("sync", "d_sy", XSD[d][:, tc * 128:(tc + 1) * 128],
                                   xnfm2[buf][:, d * 128:(d + 1) * 128])
            for d in range(ND):
                sem, val = mark_tr_ev[(tc, d)]
                wait("tensor", sem, val)
            wait("tensor", "sV", mark_vev)
            for d in range(ND):
                for gcol in range(4):
                    just("tensor", lambda e, d=d, gcol=gcol, buf=buf: e.matmul(
                        pKV[gcol], xnfm2[buf][:, d * 128:(d + 1) * 128],
                        wk_sb[:, d * 2048 + gcol * 512: d * 2048 + (gcol + 1) * 512],
                        start=(d == 0), stop=(d == ND - 1)))
                fn = lambda e, d=d, buf=buf: e.matmul(
                    pgate, xnfm2[buf][:, d * 128:(d + 1) * 128],
                    gate_sb[:, d * 8:(d + 1) * 8],
                    start=(d == 0), stop=(d == ND - 1))
                if d == ND - 1:
                    mkg = inc("tensor", "sT", fn)
                else:
                    just("tensor", fn)
            wait("vector", "sT", mkg)
            mark_kev = inc("vector", "sV", lambda e: e.tensor_relu(k_sb[:, 0:1024], psA[:, :]))
            wait("scalar", "sT", mkg)
            mark_gev = inc("scalar", "sA", lambda e: e.copy(logits[:, :], pgate))
            mark_kev2 = inc("scalar", "sA", lambda e: e.activation(k_sb[:, 1024:2048], psB[:, :], AF.Relu))
            wait("tensor", "sV", mark_kev)
            wait("tensor", "sA", mark_kev2)
            for d in range(ND):
                for gcol in range(4):
                    fn = lambda e, d=d, gcol=gcol, buf=buf: e.matmul(
                        pKV[gcol], xnfm2[buf][:, d * 128:(d + 1) * 128],
                        wv_sb[:, d * 2048 + gcol * 512: d * 2048 + (gcol + 1) * 512],
                        start=(d == 0), stop=(d == ND - 1))
                    if d == ND - 1 and gcol == 3:
                        mvd = inc("tensor", "sT", fn)
                    else:
                        just("tensor", fn)
            mark_xnfm_free[tc] = mvd
            wait("vector", "sT", mvd)
            mark_vev = inc("vector", "sV", lambda e: e.tensor_copy(
                vext.rearrange("p (h j) -> p h j", j=65)[:, 0:16, 0:64],
                psA.rearrange("p (h j) -> p h j", j=64)))
            wait("scalar", "sT", mvd)
            mark_vev2 = inc("scalar", "sA", lambda e: e.copy(
                vext.rearrange("p (h j) -> p h j", j=65)[:, 16:32, 0:64],
                psB.rearrange("p (h j) -> p h j", j=64)))
            wait("tensor", "sV", mark_vev)
            wait("tensor", "sA", mark_vev2)
            wait("tensor", "sA", mark_gev)
            for h in range(NH):
                pair, base = h // 2, (h % 2) * 64
                tgt, col = kvp[pair]
                st_flag = (tc == 0) and (pair in (0, 7, 14))
                mark_kvmm = inc("tensor", "sT", lambda e, h=h, base=base, tgt=tgt, col=col, st=st_flag: e.matmul(
                    tgt[base:base + 64, col:col + 65],
                    k_sb[:, h * 64:(h + 1) * 64], vext[:, h * 65:(h + 1) * 65],
                    start=st, stop=(tc == NTC - 1)))
            wait("vector", "sA", mark_gev)
            just("vector", lambda e: e.tensor_reduce(gst[:, 0:1], logits[:, :], AX.X, ALU.max))
            drain("vector")
            just("vector", lambda e: e.tensor_scalar(t8a[:, :], logits[:, :], gst[:, 0:1], None, ALU.subtract))
            drain("vector")
            just("vector", lambda e: e.tensor_scalar(t8b[:, :], t8a[:, :], 0.0, None, ALU.is_ge))
            drain("vector")
            just("vector", lambda e: e.scalar_tensor_tensor(t8c[:, :], t8b[:, :], -BIG, t8a[:, :], ALU.mult, ALU.add))
            drain("vector")
            mV = inc("vector", "sV", lambda e: e.tensor_reduce(gst[:, 1:2], t8c[:, :], AX.X, ALU.max))
            wait("scalar", "sV", mV)
            just("scalar", lambda e: e.activation(gst[:, 2:3], gst[:, 1:2], AF.Exp))
            mA = inc("scalar", "sA", lambda e: e.activation(t8b[:, :], t8a[:, :], AF.Exp))
            wait("vector", "sA", mA)
            just("vector", lambda e: e.tensor_scalar(gst[:, 3:4], gst[:, 2:3], 1.0, None, ALU.add))
            drain("vector")
            just("vector", lambda e: e.reciprocal(gst[:, 4:5], gst[:, 3:4]))
            drain("vector")
            just("vector", lambda e: e.tensor_scalar(t8c[:, :], t8a[:, :], gst[:, 1:2], None, ALU.is_ge))
            drain("vector")
            just("vector", lambda e: e.tensor_tensor(t8a[:, :], t8c[:, :], t8b[:, :], ALU.mult))
            drain("vector")
            mV = inc("vector", "sV", lambda e, tc=tc: e.tensor_scalar(
                wtok[:, tc * 8:(tc + 1) * 8], t8a[:, :], gst[:, 4:5], None, ALU.mult))
            mark_wtok = mV
        wait("tensor", "sV", mark_wtok)
        wait("tensor", "sA", mark_gev)
        tr128 = [psD[0:128, 512:640], psD[0:128, 640:768]]
        mwr = 0
        for half in range(2):
            wait("tensor", "sA", mwr)
            mT = inc("tensor", "sT", lambda e, half=half: e.transpose(
                tr128[half], wtok[:, half * 128:(half + 1) * 128], idf32[:, :]))
            wait("scalar", "sT", mT)
            mwr = inc("scalar", "sA", lambda e, half=half: e.copy(wrow128[:, half * 128:(half + 1) * 128], tr128[half]))
            wait("sync", "sA", mwr)
            for ex in range(8):
                src_ap = bass.AP(wrow128.tensor if hasattr(wrow128, "tensor") else wrow128,
                                 ex * 256 + half * 128, [[2048, 16], [1, 128]])
                mark_wfm[(half, ex)] = dma("sync", "d_sy",
                    WFM[ex:ex + 1, half * 2048:(half + 1) * 2048].rearrange("o (tc p) -> (o tc) p", p=128),
                    src_ap)
        mark_wfm_done = cnt["d_sy"]

        wait("vector", "sT", mark_kvmm)
        just("vector", lambda e: e.tensor_copy(kv_f32[:, 0:455], kv0))
        just("vector", lambda e: e.tensor_copy(kv_f32[:, 455:910], kv1))
        just("vector", lambda e: e.tensor_copy(kv_f32[:, 910:1040], kv2e))
        just("vector", lambda e: e.tensor_copy(kv_bf[:, 0:455], kv0))
        just("vector", lambda e: e.tensor_copy(kv_bf[:, 455:910], kv1))
        just("vector", lambda e: e.tensor_copy(kv_bf[:, 910:1040], kv2e))
        drain("vector")
        just("vector", lambda e: e.memset(ksbd[:, :], 0.0))
        for h in range(NH):
            base = (h % 2) * 64
            kc = kv_col(h // 2) + 64
            dst_col = (h // 2) * 32 + h
            just("vector", lambda e, base=base, kc=kc, dc=dst_col: e.tensor_copy(
                ksbd[base:base + 64, dc:dc + 1], kv_f32[base:base + 64, kc:kc + 1]))
        p1_done = inc("vector", "sV", lambda e: e.nop())
        p1_sA = cnt["sA"]

        mark_y_store = {t: 0 for t in range(TCT)}
        mark_psq_ev = [0, 0]
        mark_pt3_ev = [0, 0]
        mark_pyq_ev = [0, 0, 0, 0]
        mark_bcev = [0, 0]
        mark_sev = [0, 0]
        mark_uev = [0, 0]
        mark_silu = mark_h1 = mark_h = 0
        mark_w13 = {0: 0, 1: 0}
        mark_wq = [0, 0]
        mark_woq = [0, 0]
        mark_w2q = [0, 0]
        last_wq_mm = [0, 0]
        last_woq_mm = [0, 0]
        last_w13_mm = [0, 0]
        last_w2q_mm = [0, 0]
        last_stage_mm = 0
        mark_qev = {}
        mark_zmul = {}
        mark_yadd_last = {}

        wait("sync", "sV", p1_done)
        wait("sync", "sT", mark_kvmm)
        mark_xnt_load = 0
        for d in range(ND):
            mark_xnt_load = dma("sync", "d_sy", xn_t[:, d * 1024:(d + 1) * 1024], XSD[d][:, 0:TT])

        for tile in range(NTILE):
            t0 = tile * TT
            wait("gpsimd", "sV", p1_done)
            wait("gpsimd", "sT", last_w13_mm[0])
            wait("gpsimd", "sT", last_w13_mm[1])
            wait("gpsimd", "sT", last_w2q_mm[0])
            wait("gpsimd", "sT", last_w2q_mm[1])
            mark_wq[0] = dma("gpsimd", "d_gp", wqb[0], WQ[0])
            psq = [psA, psB]
            for q in range(ND):
                b = q % 2
                if q + 1 < ND:
                    nb = (q + 1) % 2
                    wait("gpsimd", "sT", last_wq_mm[nb])
                    mark_wq[nb] = dma("gpsimd", "d_gp", wqb[nb], WQ[q + 1])
                wait("tensor", "d_gp", mark_wq[b])
                wait("tensor", "d_sy", mark_xnt_load)
                wait("tensor", "sV", mark_psq_ev[b])
                if tile == 0 and q == 0:
                    wait("tensor", "sV", p1_done)
                if tile > 0 and q == 0:
                    for pb in range(4):
                        wait("tensor", "sV", mark_pyq_ev[pb])
                for d in range(ND):
                    for half in range(2):
                        fn = lambda e, b=b, d=d, half=half: e.matmul(
                            psq[b][:, half * 512:(half + 1) * 512],
                            wqb[b][:, d * 128:(d + 1) * 128],
                            xn_t[:, d * 1024 + half * 512: d * 1024 + (half + 1) * 512],
                            start=(d == 0), stop=(d == ND - 1))
                        if d == ND - 1 and half == 1:
                            last_wq_mm[b] = last_q_mm = inc("tensor", "sT", fn)
                        else:
                            just("tensor", fn)
                wait("vector", "sT", last_q_mm)
                mark_psq_ev[b] = inc("vector", "sV", lambda e, q=q, b=b: e.tensor_relu(
                    qz[:, q * 1024:(q + 1) * 1024], psq[b][:, :]))
                mark_qev[q] = mark_psq_ev[b]

            pn = psC[0:NH, 0:512]
            pbc = [psC[:, 512:1024], psD[:, 0:512]]
            pqkv = [psD[:, 512:1024], psA[:, 0:512]]
            mark_nred = 0
            mark_zpar = [0, 0]
            for sg in range(2):
                for q in range(ND):
                    wait("tensor", "sV", mark_qev[q])
                if tile == 0 and sg == 0:
                    wait("tensor", "sA", p1_sA)
                if sg == 1:
                    wait("tensor", "sV", mark_nred)
                for c in range(ND):
                    fn = lambda e, c=c, sg=sg: e.matmul(
                        pn, ksbd[:, c * 32:(c + 1) * 32],
                        qz[:, c * 1024 + sg * 512: c * 1024 + (sg + 1) * 512],
                        start=(c == 0), stop=(c == ND - 1))
                    if c == ND - 1:
                        m_n = inc("tensor", "sT", fn)
                    else:
                        just("tensor", fn)
                wait("vector", "sT", m_n)
                just("vector", lambda e: e.tensor_scalar(nmaxb[:, :], pn, 1e-6, None, ALU.max))
                drain("vector")
                just("vector", lambda e: e.reciprocal(nmaxb[:, :], nmaxb[:, :]))
                drain("vector")
                mark_nred = inc("vector", "sV", lambda e: e.tensor_copy(rinv[:, :], nmaxb[:, :]))
                wait("tensor", "sV", mark_nred)
                for c in range(ND):
                    b = c % 2
                    wait("tensor", "sA", mark_bcev[b])
                    mbc = inc("tensor", "sT", lambda e, c=c, b=b: e.matmul(
                        pbc[b], e_sb[:, c * 128:(c + 1) * 128], rinv[:, :],
                        start=True, stop=True))
                    wait("tensor", "sV", mark_zpar[b])
                    just("tensor", lambda e, c=c, b=b, sg=sg: e.matmul(
                        pqkv[b][0:64, :], kv_bf[0:64, kv_col(c):kv_col(c) + 64],
                        qz[0:64, c * 1024 + sg * 512: c * 1024 + (sg + 1) * 512],
                        start=True, stop=True))
                    mqk = inc("tensor", "sT", lambda e, c=c, b=b, sg=sg: e.matmul(
                        pqkv[b][64:128, :], kv_bf[64:128, kv_col(c):kv_col(c) + 64],
                        qz[64:128, c * 1024 + sg * 512: c * 1024 + (sg + 1) * 512],
                        start=True, stop=True))
                    wait("scalar", "sT", mbc)
                    wait("scalar", "sV", mark_zpar[b])
                    mark_bcev[b] = inc("scalar", "sA", lambda e, b=b: e.copy(bcb[b][:, :], pbc[b]))
                    wait("vector", "sT", mqk)
                    wait("vector", "sA", mark_bcev[b])
                    mark_zpar[b] = mark_zmul[(sg, c)] = inc("vector", "sV", lambda e, c=c, b=b, sg=sg: e.tensor_tensor(
                        qz[:, c * 1024 + sg * 512: c * 1024 + (sg + 1) * 512],
                        pqkv[b], bcb[b][:, :], ALU.mult))

            pt3 = [psA[:, 0:512], psB[:, 0:512]]
            wait("gpsimd", "sT", last_wq_mm[0])
            wait("gpsimd", "sT", last_wq_mm[1])
            mark_woq[0] = dma("gpsimd", "d_gp", woq[0], WO4[0])
            for q4 in range(4):
                b = q4 % 2
                if q4 + 1 < 4:
                    nb = (q4 + 1) % 2
                    wait("gpsimd", "sT", last_woq_mm[nb])
                    mark_woq[nb] = dma("gpsimd", "d_gp", woq[nb], WO4[q4 + 1])
                wait("tensor", "d_gp", mark_woq[b])
                for tcl in range(TCT):
                    pb = tcl % 2
                    if q4 == 0 and tcl == 0:
                        for c in range(ND):
                            wait("tensor", "sV", mark_zmul[(1, c)])
                    wait("tensor", "sV", mark_pt3_ev[pb])
                    if tile > 0:
                        wait("tensor", "d_sy", mark_y_store[tcl])
                    for c in range(ND):
                        just("tensor", lambda e, c=c, tcl=tcl, b=b, pb=pb: e.matmul(
                            pt3[pb], qz[:, c * 1024 + tcl * 128: c * 1024 + tcl * 128 + 128],
                            woq[b][:, c * 512:(c + 1) * 512],
                            start=(c == 0), stop=False))
                    last_woq_mm[b] = m3 = inc("tensor", "sT", lambda e, q4=q4, pb=pb: e.matmul(
                        pt3[pb], ones1[0:1, :], bo_row[0:1, q4 * 512:(q4 + 1) * 512],
                        start=False, stop=True))
                    wait("vector", "sT", m3)
                    mark_pt3_ev[pb] = inc("vector", "sV", lambda e, tcl=tcl, q4=q4, pb=pb: e.tensor_copy(
                        y32[:, tcl * 2048 + q4 * 512: tcl * 2048 + (q4 + 1) * 512], pt3[pb]))

            for ex in range(NE):
                wait("gpsimd", "d_sy", mark_wfm_done)
                wait("gpsimd", "sT", last_stage_mm)
                mst = dma("gpsimd", "d_gp", stage[0:1, :], WFM[ex:ex + 1, t0:t0 + TT])
                wait("tensor", "d_gp", mst)
                if ex == 0:
                    wait("tensor", "sV", mark_pt3_ev[0])
                    wait("tensor", "sV", mark_pt3_ev[1])
                else:
                    wait("tensor", "sA", mark_sev[0])
                    wait("tensor", "sV", mark_uev[0])
                    for pb in range(4):
                        wait("tensor", "sV", mark_pyq_ev[pb])
                wait("tensor", "sV", mark_h1)
                for half in range(2):
                    fn = lambda e, half=half: e.matmul(
                        psA[:, half * 512:(half + 1) * 512], ones1[0:1, :],
                        stage[0:1, half * 512:(half + 1) * 512], start=True, stop=True)
                    if half == 1:
                        last_stage_mm = mbs = inc("tensor", "sT", fn)
                    else:
                        just("tensor", fn)
                wait("scalar", "sT", mbs)
                mark_bce = inc("scalar", "sA", lambda e: e.copy(bce_sb[:, :], psA[:, :]))
                wait("gpsimd", "sT", last_w13_mm[0])
                if ex == 0:
                    wait("gpsimd", "sT", last_woq_mm[0])
                    wait("gpsimd", "sT", last_woq_mm[1])
                mark_w13[0] = dma("gpsimd", "d_gp", w13b[0], W13[ex, 0])
                pg = [psA, psC]
                pu = [psB, psD]
                for i in range(NI):
                    b = i % 2
                    if i + 1 < NI:
                        nb = (i + 1) % 2
                        wait("gpsimd", "sT", last_w13_mm[nb])
                        mark_w13[nb] = dma("gpsimd", "d_gp", w13b[nb], W13[ex, i + 1])
                    wait("tensor", "d_gp", mark_w13[b])
                    if i == 0:
                        wait("tensor", "sA", mark_bce)
                    wait("tensor", "sA", mark_sev[b])
                    wait("tensor", "sV", mark_uev[b])
                    for d in range(ND):
                        for half in range(2):
                            just("tensor", lambda e, b=b, d=d, half=half: e.matmul(
                                pg[b][:, half * 512:(half + 1) * 512],
                                w13b[b][:, d * 128:(d + 1) * 128],
                                xn_t[:, d * 1024 + half * 512: d * 1024 + (half + 1) * 512],
                                start=(d == 0), stop=(d == ND - 1)))
                    for d in range(ND):
                        for half in range(2):
                            fn = lambda e, b=b, d=d, half=half: e.matmul(
                                pu[b][:, half * 512:(half + 1) * 512],
                                w13b[b][:, 2048 + d * 128: 2048 + (d + 1) * 128],
                                xn_t[:, d * 1024 + half * 512: d * 1024 + (half + 1) * 512],
                                start=(d == 0), stop=(d == ND - 1))
                            if d == ND - 1 and half == 1:
                                last_w13_mm[b] = mgu = inc("tensor", "sT", fn)
                            else:
                                just("tensor", fn)
                    wait("scalar", "sT", mgu)
                    wait("scalar", "sP", mark_h)
                    mark_sev[b] = mark_silu = inc("scalar", "sA", lambda e, b=b: e.activation(
                        s_sb[:, :], pg[b][:, :], AF.Silu))
                    wait("vector", "sT", mgu)
                    wait("vector", "sP", mark_h)
                    mark_uev[b] = mark_h1 = inc("vector", "sV", lambda e, b=b: e.tensor_tensor(
                        h1_sb[:, :], pu[b][:, :], bce_sb[:, :], ALU.mult))
                    wait("gpsimd", "sA", mark_silu)
                    wait("gpsimd", "sV", mark_h1)
                    mark_h = inc("gpsimd", "sP", lambda e, i=i: e.tensor_tensor(
                        h_sb[:, i * 1024:(i + 1) * 1024], h1_sb[:, :], s_sb[:, :], ALU.mult))
                wait("gpsimd", "sT", last_w2q_mm[0])
                mark_w2q[0] = dma("gpsimd", "d_gp", w2qb[0], W2Q[ex, 0])
                pyq = [psA[:, 0:512], psB[:, 0:512], psC[:, 0:512], psD[:, 0:512]]
                for q4 in range(4):
                    b = q4 % 2
                    if q4 + 1 < 4:
                        nb = (q4 + 1) % 2
                        wait("gpsimd", "sT", last_w2q_mm[nb])
                        mark_w2q[nb] = dma("gpsimd", "d_gp", w2qb[nb], W2Q[ex, q4 + 1])
                    wait("tensor", "d_gp", mark_w2q[b])
                    if q4 == 0:
                        wait("tensor", "sP", mark_h)
                        wait("tensor", "sA", mark_sev[0])
                        wait("tensor", "sA", mark_sev[1])
                        wait("tensor", "sV", mark_uev[0])
                        wait("tensor", "sV", mark_uev[1])
                    for tcl in range(TCT):
                        pb = (q4 * TCT + tcl) % 4
                        wait("tensor", "sV", mark_pyq_ev[pb])
                        for i in range(NI):
                            fn = lambda e, i=i, tcl=tcl, b=b, pb=pb: e.matmul(
                                pyq[pb], h_sb[:, i * 1024 + tcl * 128: i * 1024 + tcl * 128 + 128],
                                w2qb[b][:, i * 512:(i + 1) * 512],
                                start=(i == 0), stop=(i == NI - 1))
                            if i == NI - 1:
                                last_w2q_mm[b] = m4 = inc("tensor", "sT", fn)
                            else:
                                just("tensor", fn)
                        wait("vector", "sT", m4)
                        mark_pyq_ev[pb] = inc("vector", "sV", lambda e, tcl=tcl, q4=q4, pb=pb: e.tensor_tensor(
                            y32[:, tcl * 2048 + q4 * 512: tcl * 2048 + (q4 + 1) * 512],
                            pyq[pb],
                            y32[:, tcl * 2048 + q4 * 512: tcl * 2048 + (q4 + 1) * 512], ALU.add))
                        if ex == NE - 1:
                            mark_yadd_last[(q4, tcl)] = mark_pyq_ev[pb]
                if ex == NE - 1 and tile + 1 < NTILE:
                    wait("sync", "sT", mgu)
                    for d in range(ND):
                        mark_xnt_load = dma("sync", "d_sy", xn_t[:, d * 1024:(d + 1) * 1024],
                                            XSD[d][:, (tile + 1) * TT:(tile + 2) * TT])

            for tcl in range(TCT):
                mxl = dma("sync", "d_sy", xbuf[:, :], X[t0 + tcl * 128: t0 + (tcl + 1) * 128, :])
                for q4 in range(4):
                    wait("vector", "sV", mark_yadd_last[(q4, tcl)])
                wait("vector", "d_sy", mxl)
                mF = inc("vector", "sV", lambda e, tcl=tcl: e.tensor_tensor(
                    y32[:, tcl * 2048:(tcl + 1) * 2048], xbuf[:, :],
                    y32[:, tcl * 2048:(tcl + 1) * 2048], ALU.add))
                wait("sync", "sV", mF)
                mark_y_store[tcl] = dma("sync", "d_sy", OUT[t0 + tcl * 128: t0 + (tcl + 1) * 128, :],
                                        y32[:, tcl * 2048:(tcl + 1) * 2048])

        wait("sync", "d_sy", cnt["d_sy"])
        wait("gpsimd", "d_gp", cnt["d_gp"])

        engs = ["sync", "tensor", "vector", "scalar", "gpsimd"]
        queues = {n: [] for n in engs}
        for name, fn, meta in ops:
            queues[name].append(meta)
        pos = {n: 0 for n in engs}
        sem_now = {n: 0 for n in sem_names}
        progress = True
        while progress:
            progress = False
            for n in engs:
                while pos[n] < len(queues[n]):
                    meta = queues[n][pos[n]]
                    if meta[0] == "wait":
                        if sem_now[meta[1]] < meta[2]:
                            break
                    elif meta[0] == "inc":
                        sem_now[meta[1]] += meta[2]
                    pos[n] += 1
                    progress = True
        stuck = {n: (pos[n], len(queues[n])) for n in engs if pos[n] < len(queues[n])}
        if stuck:
            detail = {n: queues[n][pos[n]] for n in stuck}
            raise RuntimeError(f"plan deadlock: {stuck} blocked at {detail} sems={sem_now}")

        with nc.Block() as block:
            @block.sync
            def _(eng):
                for name, fn, _m in ops:
                    if name == "sync":
                        fn(eng)

            @block.tensor
            def _(eng):
                for name, fn, _m in ops:
                    if name == "tensor":
                        fn(eng)

            @block.vector
            def _(eng):
                for name, fn, _m in ops:
                    if name == "vector":
                        fn(eng)

            @block.scalar
            def _(eng):
                for name, fn, _m in ops:
                    if name == "scalar":
                        fn(eng)

            @block.gpsimd
            def _(eng):
                for name, fn, _m in ops:
                    if name == "gpsimd":
                        fn(eng)

    return nc


def _prep(inputs):
    wq = np.ascontiguousarray(
        inputs["L0_attn_Wq"].reshape(16, 128, 16, 128).transpose(2, 1, 0, 3).reshape(16, 128, 2048))
    wo4 = np.ascontiguousarray(
        inputs["L0_attn_Wo"].reshape(16, 128, 4, 512).transpose(2, 1, 0, 3).reshape(4, 128, 16 * 512))
    w13a = inputs["L0_ffn_w1"].reshape(8, 16, 128, 11, 128).transpose(0, 3, 2, 1, 4)
    w13c = inputs["L0_ffn_w3"].reshape(8, 16, 128, 11, 128).transpose(0, 3, 2, 1, 4)
    w13 = np.ascontiguousarray(np.stack([w13a, w13c], axis=3).reshape(8, 11, 128, 4096))
    w2q = np.ascontiguousarray(
        inputs["L0_ffn_w2"].reshape(8, 11, 128, 4, 512).transpose(0, 3, 2, 1, 4).reshape(8, 4, 128, 11 * 512))
    bob = np.ascontiguousarray(inputs["L0_attn_bo"][None, :])
    e8 = np.zeros((32, 2048), dtype=np.float32)
    for h in range(32):
        e8[h, h * 64:(h + 1) * 64] = 1.0
    return {
        "wq": wq, "wk": np.ascontiguousarray(inputs["L0_attn_Wk"]),
        "wv": np.ascontiguousarray(inputs["L0_attn_Wv"]),
        "gw": np.ascontiguousarray(inputs["L0_ffn_gate_w"]),
        "wo4": wo4, "w13": w13, "w2q": w2q, "bob": bob, "e8": e8,
        "idb": np.eye(128, dtype=ml_dtypes.bfloat16),
        "idf": np.eye(128, dtype=np.float32),
        "one1": np.ones((1, 128), dtype=ml_dtypes.bfloat16),
    }


def kernel(**inputs):
    x = np.asarray(inputs["L0_attn_hidden_states"])
    common = _prep({k: np.asarray(v) for k, v in inputs.items()})
    nc = build_program()
    in_maps = []
    for c in range(8):
        m = dict(common)
        m["x"] = np.ascontiguousarray(x[c])
        in_maps.append(m)
    res = run_bass_kernel_spmd(nc, in_maps, core_ids=list(range(8)))
    out = np.stack([np.asarray(res.results[c]["out"]) for c in range(8)], axis=0)
    return out.astype(np.float32)


# version 06
# speedup vs baseline: 1.2592x; 1.0179x over previous
import contextlib
import numpy as np
import ml_dtypes
import concourse.bass as bass
import concourse.mybir as mybir
from concourse.bass_utils import run_bass_kernel_spmd

F32 = mybir.dt.float32
BF16 = mybir.dt.bfloat16
AF = mybir.ActivationFunctionType
ALU = mybir.AluOpType
AX = mybir.AxisListType

S_TOK = 4096
DM = 2048
NH = 32
NE = 8
ND = 16
NI = 11
NTC = 32
NTILE = 4
TT = 1024
TCT = 8


def build_program():
    nc = bass.Bass("TRN2", target_bir_lowering=False)

    X = nc.dram_tensor("x", [S_TOK, DM], F32, kind="ExternalInput").ap()
    WQ = nc.dram_tensor("wq", [ND, 128, DM], F32, kind="ExternalInput").ap()
    WK = nc.dram_tensor("wk", [DM, DM], F32, kind="ExternalInput").ap()
    WV = nc.dram_tensor("wv", [DM, DM], F32, kind="ExternalInput").ap()
    GW = nc.dram_tensor("gw", [DM, 8], F32, kind="ExternalInput").ap()
    WO4 = nc.dram_tensor("wo4", [4, 128, ND * 512], F32, kind="ExternalInput").ap()
    W13 = nc.dram_tensor("w13", [NE, NI, 128, 4096], F32, kind="ExternalInput").ap()
    W2Q = nc.dram_tensor("w2q", [NE, 4, 128, NI * 512], F32, kind="ExternalInput").ap()
    BOB = nc.dram_tensor("bob", [1, DM], F32, kind="ExternalInput").ap()
    E8 = nc.dram_tensor("e8", [NH, DM], F32, kind="ExternalInput").ap()
    IDB = nc.dram_tensor("idb", [128, 128], BF16, kind="ExternalInput").ap()
    IDF = nc.dram_tensor("idf", [128, 128], F32, kind="ExternalInput").ap()
    ONE1 = nc.dram_tensor("one1", [1, 128], BF16, kind="ExternalInput").ap()
    OUT = nc.dram_tensor("out", [S_TOK, DM], F32, kind="ExternalOutput").ap()

    XSD = nc.dram_tensor("xn_scr_d", [ND, 128, S_TOK], BF16).ap()
    XST = nc.dram_tensor("xn_scr_tc", [NTC, ND, 128, 128], BF16).ap()
    WFM = nc.dram_tensor("wfm", [8, S_TOK], F32).ap()

    ctx = contextlib.ExitStack()
    with ctx:
        ec = ctx.enter_context
        ar1 = ec(nc.sbuf_tensor("ar1", [128, 32768], BF16))
        arv = ec(nc.sbuf_tensor("arv", [128, 33792], BF16))
        ar2 = ec(nc.sbuf_tensor("ar2", [128, 19456], BF16))
        kv_bf = ec(nc.sbuf_tensor("kv_bf", [128, 1040], BF16))
        ksbd = ec(nc.sbuf_tensor("ksbd", [128, 512], BF16))
        e_sb = ec(nc.sbuf_tensor("e_sb", [NH, DM], BF16))
        bo_row = ec(nc.sbuf_tensor("bo_row", [1, DM], BF16))
        idbf = ec(nc.sbuf_tensor("idbf", [128, 128], BF16))
        idf32 = ec(nc.sbuf_tensor("idf32", [128, 128], F32))
        ones1 = ec(nc.sbuf_tensor("ones1", [1, 128], BF16))
        stage2 = [ec(nc.sbuf_tensor("stage0", [1, TT], BF16)),
                  ec(nc.sbuf_tensor("stage1", [1, TT], BF16))]
        xbuf = ec(nc.sbuf_tensor("xbuf", [128, DM], F32))
        st = ec(nc.sbuf_tensor("st", [128, 12], F32))
        logits = ec(nc.sbuf_tensor("logits", [128, 8], F32))
        t8a = ec(nc.sbuf_tensor("t8a", [128, 8], F32))
        t8b = ec(nc.sbuf_tensor("t8b", [128, 8], F32))
        t8c = ec(nc.sbuf_tensor("t8c", [128, 8], F32))
        w8 = ec(nc.sbuf_tensor("w8", [128, 8], F32))
        wrow128 = ec(nc.sbuf_tensor("wrow128", [128, 256], F32))
        gst = ec(nc.sbuf_tensor("gst", [128, 8], F32))
        wtok = ec(nc.sbuf_tensor("wtok", [128, 256], F32))
        nmaxb = ec(nc.sbuf_tensor("nmaxb", [NH, 512], F32))
        rinv = ec(nc.sbuf_tensor("rinv", [NH, 512], BF16))
        bcb = [ec(nc.sbuf_tensor("bcb%d" % i, [128, 512], F32)) for i in range(4)]

        psA = ec(nc.psum_tensor("psA", [128, 1024], F32))
        psB = ec(nc.psum_tensor("psB", [128, 1024], F32))
        psC = ec(nc.psum_tensor("psC", [128, 1024], F32))
        psD = ec(nc.psum_tensor("psD", [128, 1024], F32))

        sem_names = ["d_sy", "d_gp", "sT", "sV", "sA", "sP"]
        SEM = {n: ec(nc.semaphore(n)) for n in sem_names}

        y32 = ar1[:, :].bitcast(F32)
        wk_sb = ar1
        wv_sb = arv
        grp = arv
        xn_t = arv[:, 0:16384]
        qz = arv[:, 16384:32768]
        h_sb = arv[:, 16384:27648]
        s_sb = arv[:, 27648:29696].bitcast(F32)
        h1_sb = arv[:, 29696:31744].bitcast(F32)
        bce_sb = arv[:, 31744:33792].bitcast(F32)

        sq_scr = ar2[:, 0:4096].bitcast(F32)
        xn_tc = [ar2[:, 0:2048], ar2[:, 2048:4096]]
        k_sb = ar2[:, 4096:6144]
        vext = ar2[:, 6144:8224]
        kv_f32 = ar2[:, 8224:10304].bitcast(F32)
        gate_sb = ar2[:, 10304:10432]
        wqb = [ar2[:, 0:2048], ar2[:, 2048:4096]]
        woq = [ar2[:, 0:8192], ar2[:, 8192:16384]]
        w13b = [ar2[:, 0:4096], ar2[:, 4096:8192]]
        w2qb = [ar2[:, 8192:13824], ar2[:, 13824:19456]]

        ops = []
        cnt = {n: 0 for n in sem_names}

        def dma(eng, sem, out, in_):
            cnt[sem] += 16
            v = cnt[sem]
            ops.append((eng, lambda e, o=out, i=in_, s=sem: e.dma_start(out=o, in_=i).then_inc(SEM[s], 16),
                        ("inc", sem, 16)))
            return v

        def wait(eng, sem, v):
            if v > 0:
                ops.append((eng, lambda e, s=sem, vv=v: e.wait_ge(SEM[s], vv), ("wait", sem, v)))

        def inc(eng, sem, fn):
            cnt[sem] += 1
            v = cnt[sem]
            ops.append((eng, lambda e, f=fn, s=sem: f(e).then_inc(SEM[s], 1), ("inc", sem, 1)))
            return v

        def just(eng, fn):
            ops.append((eng, lambda e, f=fn: f(e), ("op",)))

        def drain(eng):
            ops.append((eng, lambda e: e.drain(), ("op",)))

        BIG = 1e30

        dma("gpsimd", "d_gp", e_sb[:, :], E8)
        dma("gpsimd", "d_gp", bo_row[:, :], BOB)
        dma("sync", "d_sy", idbf[:, :], IDB)
        dma("sync", "d_sy", idf32[:, :], IDF)
        dma("sync", "d_sy", ones1[:, :], ONE1)
        for d in range(ND):
            dma("gpsimd", "d_gp", wk_sb[:, d * 2048:(d + 1) * 2048], WK[d * 128:(d + 1) * 128, :])
        for d in range(ND):
            dma("gpsimd", "d_gp", wv_sb[:, d * 2048:(d + 1) * 2048], WV[d * 128:(d + 1) * 128, :])
        dma("gpsimd", "d_gp", gate_sb.rearrange("p (d j) -> p d j", d=ND),
            GW.rearrange("(d p) j -> p d j", p=128))

        pKV = [psA[:, 0:512], psA[:, 512:1024], psB[:, 0:512], psB[:, 512:1024]]
        kv0 = psC[:, 0:455]
        kv1 = psC[:, 512:967]
        kv2e = psD[:, 0:130]
        pgate = psD[:, 640:648]
        ptr4 = [psA[:, 0:64].bitcast(BF16), psA[:, 512:576].bitcast(BF16),
                psB[:, 0:64].bitcast(BF16), psB[:, 512:576].bitcast(BF16)]
        xntok2 = [ar2[:, 14528:16576], ar2[:, 10432:12480]]
        xnfm2 = [ar2[:, 16576:18624], ar2[:, 12480:14528]]

        def kv_col(pair):
            if pair < 7:
                return pair * 65
            if pair < 14:
                return 455 + (pair - 7) * 65
            return 910 + (pair - 14) * 65

        kvp = []
        for p in range(16):
            if p < 7:
                kvp.append((kv0, p * 65))
            elif p < 14:
                kvp.append((kv1, (p - 7) * 65))
            else:
                kvp.append((kv2e, (p - 14) * 65))

        just("vector", lambda e: e.memset(vext[:, :], 0.0))
        for h in range(NH):
            just("vector", lambda e, h=h: e.memset(vext[:, h * 65 + 64:h * 65 + 65], 1.0))
        mark_vext_ones = inc("vector", "sV", lambda e: e.nop())

        mark_xn_done = {}
        mark_tr_ev = {}
        mark_kaev = mark_kbev = mark_vaev = mark_vbev = mark_gev = 0
        mark_kvmm = 0
        mark_wfm = {}
        mark_xsd = {}
        mark_xnfm_free = {}

        def emit_ln(tc):
            buf = tc % 2
            wait("sync", "sV", mark_xn_done.get(tc - 1, 0))
            mx = dma("sync", "d_sy", xbuf[:, :], X[tc * 128:(tc + 1) * 128, :])
            wait("vector", "d_sy", mx)
            just("vector", lambda e: e.reduce_sum(st[:, 0:1], xbuf[:, :], AX.X))
            wait("scalar", "d_sy", mx)
            mA = inc("scalar", "sA", lambda e: e.activation(
                sq_scr[:, :], xbuf[:, :], AF.Square, accum_out=st[:, 1:2]))
            wait("vector", "sA", mA)
            drain("vector")
            just("vector", lambda e: e.tensor_scalar(st[:, 2:3], st[:, 0:1], 1.0 / DM, None, ALU.mult))
            just("vector", lambda e: e.tensor_scalar(st[:, 3:4], st[:, 1:2], 1.0 / DM, None, ALU.mult))
            drain("vector")
            just("vector", lambda e: e.tensor_tensor(st[:, 4:5], st[:, 2:3], st[:, 2:3], ALU.mult))
            drain("vector")
            just("vector", lambda e: e.tensor_tensor(st[:, 5:6], st[:, 3:4], st[:, 4:5], ALU.subtract))
            drain("vector")
            mV = inc("vector", "sV", lambda e: e.tensor_scalar(st[:, 6:7], st[:, 5:6], 1e-5, None, ALU.add))
            wait("scalar", "sV", mV)
            mA = inc("scalar", "sA", lambda e: e.activation(st[:, 7:8], st[:, 6:7], AF.Sqrt))
            wait("vector", "sA", mA)
            just("vector", lambda e: e.reciprocal(st[:, 8:9], st[:, 7:8]))
            if tc >= 2:
                wait("vector", "sT", mark_tr_last[tc - 2])
            drain("vector")
            mark_xn_done[tc] = inc("vector", "sV", lambda e, buf=buf: e.tensor_scalar(
                xntok2[buf][:, :], xbuf[:, :], st[:, 2:3], st[:, 8:9], ALU.subtract, ALU.mult))

        emit_ln(0)
        mark_tr_last = {}
        for tc in range(NTC):
            buf = tc % 2
            wait("tensor", "sV", mark_xn_done[tc])
            if tc == 0:
                wait("tensor", "d_gp", cnt["d_gp"])
                wait("tensor", "sV", mark_vext_ones)
            wait("tensor", "sA", mark_gev)
            if tc >= 2:
                wait("tensor", "sT", mark_xnfm_free[tc - 2])
                wait("tensor", "d_sy", mark_xsd[tc - 2])
            for d in range(ND):
                slot = d % 4
                if (tc, d) not in ((0, 0), (0, 1), (0, 2), (0, 3)):
                    pd = tc * ND + d - 4
                    ptc, pdd = divmod(pd, ND)
                    sem, val = mark_tr_ev[(ptc, pdd)]
                    wait("tensor", sem, val)
                mark_tr_last_v = inc("tensor", "sT", lambda e, d=d, buf=buf, slot=slot: e.transpose(
                    ptr4[slot], xntok2[buf][:, d * 128:(d + 1) * 128], idbf[:, :]))
                ev_eng, ev_sem = ("vector", "sV") if d % 2 == 0 else ("scalar", "sA")
                cpf = (lambda e, o, i: e.tensor_copy(o, i)) if d % 2 == 0 else (lambda e, o, i: e.copy(o, i))
                wait(ev_eng, "sT", mark_tr_last_v)
                mark_tr_ev[(tc, d)] = (ev_sem, inc(ev_eng, ev_sem, lambda e, d=d, buf=buf, slot=slot, cpf=cpf: cpf(
                    e, xnfm2[buf][:, d * 128:(d + 1) * 128], ptr4[slot])))
            mark_tr_last[tc] = mark_tr_last_v
            for d in range(ND):
                sem, val = mark_tr_ev[(tc, d)]
                wait("sync", sem, val)
                mark_xsd[tc] = dma("sync", "d_sy", XSD[d][:, tc * 128:(tc + 1) * 128],
                                   xnfm2[buf][:, d * 128:(d + 1) * 128])
            for d in range(ND):
                sem, val = mark_tr_ev[(tc, d)]
                wait("tensor", sem, val)
            if tc + 1 < NTC:
                emit_ln(tc + 1)
            wait("tensor", "sV", mark_vaev)
            for d in range(ND):
                for gcol in range(2):
                    just("tensor", lambda e, d=d, gcol=gcol, buf=buf: e.matmul(
                        pKV[gcol], xnfm2[buf][:, d * 128:(d + 1) * 128],
                        wk_sb[:, d * 2048 + gcol * 512: d * 2048 + (gcol + 1) * 512],
                        start=(d == 0), stop=(d == ND - 1)))
                fn = lambda e, d=d, buf=buf: e.matmul(
                    pgate, xnfm2[buf][:, d * 128:(d + 1) * 128],
                    gate_sb[:, d * 8:(d + 1) * 8],
                    start=(d == 0), stop=(d == ND - 1))
                if d == ND - 1:
                    mka = inc("tensor", "sT", fn)
                else:
                    just("tensor", fn)
            wait("vector", "sT", mka)
            mark_kaev = inc("vector", "sV", lambda e: e.tensor_relu(k_sb[:, 0:1024], psA[:, :]))
            wait("scalar", "sT", mka)
            mark_gev = inc("scalar", "sA", lambda e: e.copy(logits[:, :], pgate))
            wait("tensor", "sA", mark_vbev)
            for d in range(ND):
                for gcol in range(2):
                    fn = lambda e, d=d, gcol=gcol, buf=buf: e.matmul(
                        pKV[2 + gcol], xnfm2[buf][:, d * 128:(d + 1) * 128],
                        wk_sb[:, d * 2048 + 1024 + gcol * 512: d * 2048 + 1024 + (gcol + 1) * 512],
                        start=(d == 0), stop=(d == ND - 1))
                    if d == ND - 1 and gcol == 1:
                        mkb = inc("tensor", "sT", fn)
                    else:
                        just("tensor", fn)
            wait("scalar", "sT", mkb)
            mark_kbev = inc("scalar", "sA", lambda e: e.activation(k_sb[:, 1024:2048], psB[:, :], AF.Relu))
            wait("tensor", "sV", mark_kaev)
            for d in range(ND):
                for gcol in range(2):
                    fn = lambda e, d=d, gcol=gcol, buf=buf: e.matmul(
                        pKV[gcol], xnfm2[buf][:, d * 128:(d + 1) * 128],
                        wv_sb[:, d * 2048 + gcol * 512: d * 2048 + (gcol + 1) * 512],
                        start=(d == 0), stop=(d == ND - 1))
                    if d == ND - 1 and gcol == 1:
                        mva = inc("tensor", "sT", fn)
                    else:
                        just("tensor", fn)
            wait("vector", "sT", mva)
            mark_vaev = inc("vector", "sV", lambda e: e.tensor_copy(
                vext.rearrange("p (h j) -> p h j", j=65)[:, 0:16, 0:64],
                psA.rearrange("p (h j) -> p h j", j=64)))
            wait("tensor", "sA", mark_kbev)
            for d in range(ND):
                for gcol in range(2):
                    fn = lambda e, d=d, gcol=gcol, buf=buf: e.matmul(
                        pKV[2 + gcol], xnfm2[buf][:, d * 128:(d + 1) * 128],
                        wv_sb[:, d * 2048 + 1024 + gcol * 512: d * 2048 + 1024 + (gcol + 1) * 512],
                        start=(d == 0), stop=(d == ND - 1))
                    if d == ND - 1 and gcol == 1:
                        mvb = inc("tensor", "sT", fn)
                    else:
                        just("tensor", fn)
            mark_xnfm_free[tc] = mvb
            wait("scalar", "sT", mvb)
            mark_vbev = inc("scalar", "sA", lambda e: e.copy(
                vext.rearrange("p (h j) -> p h j", j=65)[:, 16:32, 0:64],
                psB.rearrange("p (h j) -> p h j", j=64)))
            wait("tensor", "sV", mark_vaev)
            wait("tensor", "sA", mark_vbev)
            wait("tensor", "sA", mark_gev)
            for h in range(NH):
                pair, base = h // 2, (h % 2) * 64
                tgt, col = kvp[pair]
                st_flag = (tc == 0) and (pair in (0, 7, 14))
                mark_kvmm = inc("tensor", "sT", lambda e, h=h, base=base, tgt=tgt, col=col, st=st_flag: e.matmul(
                    tgt[base:base + 64, col:col + 65],
                    k_sb[:, h * 64:(h + 1) * 64], vext[:, h * 65:(h + 1) * 65],
                    start=st, stop=(tc == NTC - 1)))
            wait("vector", "sA", mark_gev)
            just("vector", lambda e: e.tensor_reduce(gst[:, 0:1], logits[:, :], AX.X, ALU.max))
            drain("vector")
            just("vector", lambda e: e.tensor_scalar(t8a[:, :], logits[:, :], gst[:, 0:1], None, ALU.subtract))
            drain("vector")
            just("vector", lambda e: e.tensor_scalar(t8b[:, :], t8a[:, :], 0.0, None, ALU.is_ge))
            drain("vector")
            just("vector", lambda e: e.scalar_tensor_tensor(t8c[:, :], t8b[:, :], -BIG, t8a[:, :], ALU.mult, ALU.add))
            drain("vector")
            mV = inc("vector", "sV", lambda e: e.tensor_reduce(gst[:, 1:2], t8c[:, :], AX.X, ALU.max))
            wait("scalar", "sV", mV)
            just("scalar", lambda e: e.activation(gst[:, 2:3], gst[:, 1:2], AF.Exp))
            mA = inc("scalar", "sA", lambda e: e.activation(t8b[:, :], t8a[:, :], AF.Exp))
            wait("vector", "sA", mA)
            just("vector", lambda e: e.tensor_scalar(gst[:, 3:4], gst[:, 2:3], 1.0, None, ALU.add))
            drain("vector")
            just("vector", lambda e: e.reciprocal(gst[:, 4:5], gst[:, 3:4]))
            drain("vector")
            just("vector", lambda e: e.tensor_scalar(t8c[:, :], t8a[:, :], gst[:, 1:2], None, ALU.is_ge))
            drain("vector")
            just("vector", lambda e: e.tensor_tensor(t8a[:, :], t8c[:, :], t8b[:, :], ALU.mult))
            drain("vector")
            mV = inc("vector", "sV", lambda e, tc=tc: e.tensor_scalar(
                wtok[:, tc * 8:(tc + 1) * 8], t8a[:, :], gst[:, 4:5], None, ALU.mult))
            mark_wtok = mV
        wait("tensor", "sV", mark_wtok)
        wait("tensor", "sA", mark_gev)
        tr128 = [psD[0:128, 512:640], psD[0:128, 640:768]]
        mwr = 0
        for half in range(2):
            wait("tensor", "sA", mwr)
            mT = inc("tensor", "sT", lambda e, half=half: e.transpose(
                tr128[half], wtok[:, half * 128:(half + 1) * 128], idf32[:, :]))
            wait("scalar", "sT", mT)
            mwr = inc("scalar", "sA", lambda e, half=half: e.copy(wrow128[:, half * 128:(half + 1) * 128], tr128[half]))
            wait("sync", "sA", mwr)
            for ex in range(8):
                src_ap = bass.AP(wrow128.tensor if hasattr(wrow128, "tensor") else wrow128,
                                 ex * 256 + half * 128, [[2048, 16], [1, 128]])
                mark_wfm[(half, ex)] = dma("sync", "d_sy",
                    WFM[ex:ex + 1, half * 2048:(half + 1) * 2048].rearrange("o (tc p) -> (o tc) p", p=128),
                    src_ap)
        mark_wfm_done = cnt["d_sy"]

        wait("vector", "sT", mark_kvmm)
        just("vector", lambda e: e.tensor_copy(kv_f32[:, 0:455], kv0))
        just("vector", lambda e: e.tensor_copy(kv_f32[:, 455:910], kv1))
        just("vector", lambda e: e.tensor_copy(kv_f32[:, 910:1040], kv2e))
        just("vector", lambda e: e.tensor_copy(kv_bf[:, 0:455], kv0))
        just("vector", lambda e: e.tensor_copy(kv_bf[:, 455:910], kv1))
        just("vector", lambda e: e.tensor_copy(kv_bf[:, 910:1040], kv2e))
        drain("vector")
        just("vector", lambda e: e.memset(ksbd[:, :], 0.0))
        for h in range(NH):
            base = (h % 2) * 64
            kc = kv_col(h // 2) + 64
            dst_col = (h // 2) * 32 + h
            just("vector", lambda e, base=base, kc=kc, dc=dst_col: e.tensor_copy(
                ksbd[base:base + 64, dc:dc + 1], kv_f32[base:base + 64, kc:kc + 1]))
        p1_done = inc("vector", "sV", lambda e: e.nop())
        p1_sA = cnt["sA"]

        mark_y_store = {t: 0 for t in range(TCT)}
        mark_psq_ev = [0, 0]
        mark_pt3_ev = [0, 0]
        mark_pyq_ev = [0, 0, 0, 0]
        mark_bcev = [0, 0]
        mark_sev = [0, 0]
        mark_uev = [0, 0]
        mark_silu = mark_h1 = mark_h = 0
        mark_w13 = {0: 0, 1: 0}
        mark_wq = [0, 0]
        mark_woq = [0, 0]
        mark_w2q = [0, 0]
        last_wq_mm = [0, 0]
        last_woq_mm = [0, 0]
        last_w13_mm = [0, 0]
        last_w2q_mm = [0, 0]
        last_stage_mm = 0
        mark_qev = {}
        mark_zmul = {}
        mark_yadd_last = {}

        wait("sync", "sV", p1_done)
        wait("sync", "sT", mark_kvmm)
        mark_xnt_load = 0
        for d in range(ND):
            mark_xnt_load = dma("sync", "d_sy", xn_t[:, d * 1024:(d + 1) * 1024], XSD[d][:, 0:TT])

        for tile in range(NTILE):
            t0 = tile * TT
            wait("gpsimd", "sV", p1_done)
            wait("gpsimd", "sT", last_w13_mm[0])
            wait("gpsimd", "sT", last_w13_mm[1])
            wait("gpsimd", "sT", last_w2q_mm[0])
            wait("gpsimd", "sT", last_w2q_mm[1])
            mark_wq[0] = dma("gpsimd", "d_gp", wqb[0], WQ[0])
            psq = [psA, psB]
            for q in range(ND):
                b = q % 2
                if q + 1 < ND:
                    nb = (q + 1) % 2
                    wait("gpsimd", "sT", last_wq_mm[nb])
                    mark_wq[nb] = dma("gpsimd", "d_gp", wqb[nb], WQ[q + 1])
                wait("tensor", "d_gp", mark_wq[b])
                wait("tensor", "d_sy", mark_xnt_load)
                wait("tensor", "sV", mark_psq_ev[b])
                if tile == 0 and q == 0:
                    wait("tensor", "sV", p1_done)
                if tile > 0 and q == 0:
                    for pb in range(4):
                        wait("tensor", "sV", mark_pyq_ev[pb])
                for d in range(ND):
                    for half in range(2):
                        fn = lambda e, b=b, d=d, half=half: e.matmul(
                            psq[b][:, half * 512:(half + 1) * 512],
                            wqb[b][:, d * 128:(d + 1) * 128],
                            xn_t[:, d * 1024 + half * 512: d * 1024 + (half + 1) * 512],
                            start=(d == 0), stop=(d == ND - 1))
                        if d == ND - 1 and half == 1:
                            last_wq_mm[b] = last_q_mm = inc("tensor", "sT", fn)
                        else:
                            just("tensor", fn)
                wait("vector", "sT", last_q_mm)
                mark_psq_ev[b] = inc("vector", "sV", lambda e, q=q, b=b: e.tensor_relu(
                    qz[:, q * 1024:(q + 1) * 1024], psq[b][:, :]))
                mark_qev[q] = mark_psq_ev[b]

            pn = psC[0:NH, 0:512]
            pbc = [psC[:, 512:1024], psD[:, 0:512], psA[:, 512:1024], psB[:, 512:1024]]
            pqkv = [psD[:, 512:1024], psA[:, 0:512]]
            mark_nred = 0
            mark_zpar = [0, 0]
            mark_zb4 = [0, 0, 0, 0]
            mark_bcev = [0, 0, 0, 0]
            for sg in range(2):
                for q in range(ND):
                    wait("tensor", "sV", mark_qev[q])
                if tile == 0 and sg == 0:
                    wait("tensor", "sA", p1_sA)
                if sg == 1:
                    wait("tensor", "sV", mark_nred)
                for c in range(ND):
                    fn = lambda e, c=c, sg=sg: e.matmul(
                        pn, ksbd[:, c * 32:(c + 1) * 32],
                        qz[:, c * 1024 + sg * 512: c * 1024 + (sg + 1) * 512],
                        start=(c == 0), stop=(c == ND - 1))
                    if c == ND - 1:
                        m_n = inc("tensor", "sT", fn)
                    else:
                        just("tensor", fn)
                wait("vector", "sT", m_n)
                just("vector", lambda e: e.tensor_scalar(nmaxb[:, :], pn, 1e-6, None, ALU.max))
                drain("vector")
                just("vector", lambda e: e.reciprocal(nmaxb[:, :], nmaxb[:, :]))
                drain("vector")
                mark_nred = inc("vector", "sV", lambda e: e.tensor_copy(rinv[:, :], nmaxb[:, :]))
                wait("tensor", "sV", mark_nred)
                for c in range(ND):
                    b = c % 2
                    b4 = c % 4
                    wait("tensor", "sA", mark_bcev[b4])
                    mbc = inc("tensor", "sT", lambda e, c=c, b4=b4: e.matmul(
                        pbc[b4], e_sb[:, c * 128:(c + 1) * 128], rinv[:, :],
                        start=True, stop=True))
                    wait("tensor", "sV", mark_zpar[b])
                    just("tensor", lambda e, c=c, b=b, sg=sg: e.matmul(
                        pqkv[b][0:64, :], kv_bf[0:64, kv_col(c):kv_col(c) + 64],
                        qz[0:64, c * 1024 + sg * 512: c * 1024 + (sg + 1) * 512],
                        start=True, stop=True))
                    mqk = inc("tensor", "sT", lambda e, c=c, b=b, sg=sg: e.matmul(
                        pqkv[b][64:128, :], kv_bf[64:128, kv_col(c):kv_col(c) + 64],
                        qz[64:128, c * 1024 + sg * 512: c * 1024 + (sg + 1) * 512],
                        start=True, stop=True))
                    wait("scalar", "sT", mbc)
                    wait("scalar", "sV", mark_zb4[b4])
                    mark_bcev[b4] = inc("scalar", "sA", lambda e, b4=b4: e.copy(bcb[b4][:, :], pbc[b4]))
                    wait("vector", "sT", mqk)
                    wait("vector", "sA", mark_bcev[b4])
                    mark_zpar[b] = mark_zb4[b4] = mark_zmul[(sg, c)] = inc(
                        "vector", "sV", lambda e, c=c, b=b, b4=b4, sg=sg: e.tensor_tensor(
                            qz[:, c * 1024 + sg * 512: c * 1024 + (sg + 1) * 512],
                            pqkv[b], bcb[b4][:, :], ALU.mult))

            pt3 = [psA[:, 0:512], psB[:, 0:512]]
            wait("gpsimd", "sT", last_wq_mm[0])
            wait("gpsimd", "sT", last_wq_mm[1])
            mark_woq[0] = dma("gpsimd", "d_gp", woq[0], WO4[0])
            for q4 in range(4):
                b = q4 % 2
                if q4 + 1 < 4:
                    nb = (q4 + 1) % 2
                    wait("gpsimd", "sT", last_woq_mm[nb])
                    mark_woq[nb] = dma("gpsimd", "d_gp", woq[nb], WO4[q4 + 1])
                wait("tensor", "d_gp", mark_woq[b])
                for tcl in range(TCT):
                    pb = tcl % 2
                    if q4 == 0 and tcl == 0:
                        for c in range(ND):
                            wait("tensor", "sV", mark_zmul[(1, c)])
                    wait("tensor", "sV", mark_pt3_ev[pb])
                    if tile > 0:
                        wait("tensor", "d_sy", mark_y_store[tcl])
                    for c in range(ND):
                        just("tensor", lambda e, c=c, tcl=tcl, b=b, pb=pb: e.matmul(
                            pt3[pb], qz[:, c * 1024 + tcl * 128: c * 1024 + tcl * 128 + 128],
                            woq[b][:, c * 512:(c + 1) * 512],
                            start=(c == 0), stop=False))
                    last_woq_mm[b] = m3 = inc("tensor", "sT", lambda e, q4=q4, pb=pb: e.matmul(
                        pt3[pb], ones1[0:1, :], bo_row[0:1, q4 * 512:(q4 + 1) * 512],
                        start=False, stop=True))
                    wait("vector", "sT", m3)
                    mark_pt3_ev[pb] = inc("vector", "sV", lambda e, tcl=tcl, q4=q4, pb=pb: e.tensor_copy(
                        y32[:, tcl * 2048 + q4 * 512: tcl * 2048 + (q4 + 1) * 512], pt3[pb]))

            for ex in range(NE):
                sbuf_st = stage2[ex % 2]
                if ex == 0:
                    wait("gpsimd", "d_sy", mark_wfm_done)
                    wait("gpsimd", "sT", last_stage_mm)
                    mark_stage = dma("gpsimd", "d_gp", sbuf_st[0:1, :], WFM[ex:ex + 1, t0:t0 + TT])
                mst = mark_stage
                wait("tensor", "d_gp", mst)
                if ex == 0:
                    wait("tensor", "sV", mark_pt3_ev[0])
                    wait("tensor", "sV", mark_pt3_ev[1])
                else:
                    wait("tensor", "sA", mark_sev[0])
                    wait("tensor", "sV", mark_uev[0])
                    for pb in range(4):
                        wait("tensor", "sV", mark_pyq_ev[pb])
                wait("tensor", "sV", mark_h1)
                for half in range(2):
                    fn = lambda e, half=half, sbuf_st=sbuf_st: e.matmul(
                        psA[:, half * 512:(half + 1) * 512], ones1[0:1, :],
                        sbuf_st[0:1, half * 512:(half + 1) * 512], start=True, stop=True)
                    if half == 1:
                        last_stage_mm = mbs = inc("tensor", "sT", fn)
                    else:
                        just("tensor", fn)
                wait("scalar", "sT", mbs)
                mark_bce = inc("scalar", "sA", lambda e: e.copy(bce_sb[:, :], psA[:, :]))
                if ex == 0:
                    wait("gpsimd", "sT", last_w13_mm[0])
                    wait("gpsimd", "sT", last_woq_mm[0])
                    wait("gpsimd", "sT", last_woq_mm[1])
                    mark_w13[0] = dma("gpsimd", "d_gp", w13b[0], W13[ex, 0])
                pg = [psA, psC]
                pu = [psB, psD]
                for i in range(NI):
                    b = i % 2
                    if i + 1 < NI:
                        nb = (i + 1) % 2
                        wait("gpsimd", "sT", last_w13_mm[nb])
                        mark_w13[nb] = dma("gpsimd", "d_gp", w13b[nb], W13[ex, i + 1])
                    wait("tensor", "d_gp", mark_w13[b])
                    if i == 0:
                        wait("tensor", "sA", mark_bce)
                    wait("tensor", "sA", mark_sev[b])
                    wait("tensor", "sV", mark_uev[b])
                    for d in range(ND):
                        for half in range(2):
                            just("tensor", lambda e, b=b, d=d, half=half: e.matmul(
                                pg[b][:, half * 512:(half + 1) * 512],
                                w13b[b][:, d * 128:(d + 1) * 128],
                                xn_t[:, d * 1024 + half * 512: d * 1024 + (half + 1) * 512],
                                start=(d == 0), stop=(d == ND - 1)))
                    for d in range(ND):
                        for half in range(2):
                            fn = lambda e, b=b, d=d, half=half: e.matmul(
                                pu[b][:, half * 512:(half + 1) * 512],
                                w13b[b][:, 2048 + d * 128: 2048 + (d + 1) * 128],
                                xn_t[:, d * 1024 + half * 512: d * 1024 + (half + 1) * 512],
                                start=(d == 0), stop=(d == ND - 1))
                            if d == ND - 1 and half == 1:
                                last_w13_mm[b] = mgu = inc("tensor", "sT", fn)
                            else:
                                just("tensor", fn)
                    wait("scalar", "sT", mgu)
                    wait("scalar", "sP", mark_h)
                    mark_sev[b] = mark_silu = inc("scalar", "sA", lambda e, b=b: e.activation(
                        s_sb[:, :], pg[b][:, :], AF.Silu))
                    wait("vector", "sT", mgu)
                    wait("vector", "sP", mark_h)
                    mark_uev[b] = mark_h1 = inc("vector", "sV", lambda e, b=b: e.tensor_tensor(
                        h1_sb[:, :], pu[b][:, :], bce_sb[:, :], ALU.mult))
                    wait("gpsimd", "sA", mark_silu)
                    wait("gpsimd", "sV", mark_h1)
                    mark_h = inc("gpsimd", "sP", lambda e, i=i: e.tensor_tensor(
                        h_sb[:, i * 1024:(i + 1) * 1024], h1_sb[:, :], s_sb[:, :], ALU.mult))
                wait("gpsimd", "sT", last_w2q_mm[0])
                mark_w2q[0] = dma("gpsimd", "d_gp", w2qb[0], W2Q[ex, 0])
                if ex + 1 < NE:
                    wait("gpsimd", "sT", last_stage_mm)
                    mark_stage = dma("gpsimd", "d_gp", stage2[(ex + 1) % 2][0:1, :],
                                     WFM[ex + 1:ex + 2, t0:t0 + TT])
                    wait("gpsimd", "sT", last_w13_mm[0])
                    mark_w13[0] = dma("gpsimd", "d_gp", w13b[0], W13[ex + 1, 0])
                pyq = [psA[:, 0:512], psB[:, 0:512], psC[:, 0:512], psD[:, 0:512]]
                for q4 in range(4):
                    b = q4 % 2
                    if q4 + 1 < 4:
                        nb = (q4 + 1) % 2
                        wait("gpsimd", "sT", last_w2q_mm[nb])
                        mark_w2q[nb] = dma("gpsimd", "d_gp", w2qb[nb], W2Q[ex, q4 + 1])
                    wait("tensor", "d_gp", mark_w2q[b])
                    if q4 == 0:
                        wait("tensor", "sP", mark_h)
                        wait("tensor", "sA", mark_sev[0])
                        wait("tensor", "sA", mark_sev[1])
                        wait("tensor", "sV", mark_uev[0])
                        wait("tensor", "sV", mark_uev[1])
                    for tcl in range(TCT):
                        pb = (q4 * TCT + tcl) % 4
                        wait("tensor", "sV", mark_pyq_ev[pb])
                        for i in range(NI):
                            fn = lambda e, i=i, tcl=tcl, b=b, pb=pb: e.matmul(
                                pyq[pb], h_sb[:, i * 1024 + tcl * 128: i * 1024 + tcl * 128 + 128],
                                w2qb[b][:, i * 512:(i + 1) * 512],
                                start=(i == 0), stop=(i == NI - 1))
                            if i == NI - 1:
                                last_w2q_mm[b] = m4 = inc("tensor", "sT", fn)
                            else:
                                just("tensor", fn)
                        wait("vector", "sT", m4)
                        mark_pyq_ev[pb] = inc("vector", "sV", lambda e, tcl=tcl, q4=q4, pb=pb: e.tensor_tensor(
                            y32[:, tcl * 2048 + q4 * 512: tcl * 2048 + (q4 + 1) * 512],
                            pyq[pb],
                            y32[:, tcl * 2048 + q4 * 512: tcl * 2048 + (q4 + 1) * 512], ALU.add))
                        if ex == NE - 1:
                            mark_yadd_last[(q4, tcl)] = mark_pyq_ev[pb]
                        if ex == NE - 1 and q4 == 3:
                            mxl = dma("sync", "d_sy", xbuf[:, :],
                                      X[t0 + tcl * 128: t0 + (tcl + 1) * 128, :])
                            for q4b in range(4):
                                wait("vector", "sV", mark_yadd_last[(q4b, tcl)])
                            wait("vector", "d_sy", mxl)
                            mF = inc("vector", "sV", lambda e, tcl=tcl: e.tensor_tensor(
                                y32[:, tcl * 2048:(tcl + 1) * 2048], xbuf[:, :],
                                y32[:, tcl * 2048:(tcl + 1) * 2048], ALU.add))
                            wait("sync", "sV", mF)
                            mark_y_store[tcl] = dma("sync", "d_sy",
                                OUT[t0 + tcl * 128: t0 + (tcl + 1) * 128, :],
                                y32[:, tcl * 2048:(tcl + 1) * 2048])
                if ex == NE - 1 and tile + 1 < NTILE:
                    wait("sync", "sT", mgu)
                    for d in range(ND):
                        mark_xnt_load = dma("sync", "d_sy", xn_t[:, d * 1024:(d + 1) * 1024],
                                            XSD[d][:, (tile + 1) * TT:(tile + 2) * TT])


        wait("sync", "d_sy", cnt["d_sy"])
        wait("gpsimd", "d_gp", cnt["d_gp"])

        engs = ["sync", "tensor", "vector", "scalar", "gpsimd"]
        queues = {n: [] for n in engs}
        for name, fn, meta in ops:
            queues[name].append(meta)
        pos = {n: 0 for n in engs}
        sem_now = {n: 0 for n in sem_names}
        progress = True
        while progress:
            progress = False
            for n in engs:
                while pos[n] < len(queues[n]):
                    meta = queues[n][pos[n]]
                    if meta[0] == "wait":
                        if sem_now[meta[1]] < meta[2]:
                            break
                    elif meta[0] == "inc":
                        sem_now[meta[1]] += meta[2]
                    pos[n] += 1
                    progress = True
        stuck = {n: (pos[n], len(queues[n])) for n in engs if pos[n] < len(queues[n])}
        if stuck:
            detail = {n: queues[n][pos[n]] for n in stuck}
            raise RuntimeError(f"plan deadlock: {stuck} blocked at {detail} sems={sem_now}")

        with nc.Block() as block:
            @block.sync
            def _(eng):
                for name, fn, _m in ops:
                    if name == "sync":
                        fn(eng)

            @block.tensor
            def _(eng):
                for name, fn, _m in ops:
                    if name == "tensor":
                        fn(eng)

            @block.vector
            def _(eng):
                for name, fn, _m in ops:
                    if name == "vector":
                        fn(eng)

            @block.scalar
            def _(eng):
                for name, fn, _m in ops:
                    if name == "scalar":
                        fn(eng)

            @block.gpsimd
            def _(eng):
                for name, fn, _m in ops:
                    if name == "gpsimd":
                        fn(eng)

    return nc


def _prep(inputs):
    wq = np.ascontiguousarray(
        inputs["L0_attn_Wq"].reshape(16, 128, 16, 128).transpose(2, 1, 0, 3).reshape(16, 128, 2048))
    wo4 = np.ascontiguousarray(
        inputs["L0_attn_Wo"].reshape(16, 128, 4, 512).transpose(2, 1, 0, 3).reshape(4, 128, 16 * 512))
    w13a = inputs["L0_ffn_w1"].reshape(8, 16, 128, 11, 128).transpose(0, 3, 2, 1, 4)
    w13c = inputs["L0_ffn_w3"].reshape(8, 16, 128, 11, 128).transpose(0, 3, 2, 1, 4)
    w13 = np.ascontiguousarray(np.stack([w13a, w13c], axis=3).reshape(8, 11, 128, 4096))
    w2q = np.ascontiguousarray(
        inputs["L0_ffn_w2"].reshape(8, 11, 128, 4, 512).transpose(0, 3, 2, 1, 4).reshape(8, 4, 128, 11 * 512))
    bob = np.ascontiguousarray(inputs["L0_attn_bo"][None, :])
    e8 = np.zeros((32, 2048), dtype=np.float32)
    for h in range(32):
        e8[h, h * 64:(h + 1) * 64] = 1.0
    return {
        "wq": wq, "wk": np.ascontiguousarray(inputs["L0_attn_Wk"]),
        "wv": np.ascontiguousarray(inputs["L0_attn_Wv"]),
        "gw": np.ascontiguousarray(inputs["L0_ffn_gate_w"]),
        "wo4": wo4, "w13": w13, "w2q": w2q, "bob": bob, "e8": e8,
        "idb": np.eye(128, dtype=ml_dtypes.bfloat16),
        "idf": np.eye(128, dtype=np.float32),
        "one1": np.ones((1, 128), dtype=ml_dtypes.bfloat16),
    }


def kernel(**inputs):
    x = np.asarray(inputs["L0_attn_hidden_states"])
    common = _prep({k: np.asarray(v) for k, v in inputs.items()})
    nc = build_program()
    in_maps = []
    for c in range(8):
        m = dict(common)
        m["x"] = np.ascontiguousarray(x[c])
        in_maps.append(m)
    res = run_bass_kernel_spmd(nc, in_maps, core_ids=list(range(8)))
    out = np.stack([np.asarray(res.results[c]["out"]) for c in range(8)], axis=0)
    return out.astype(np.float32)


# version 07
# speedup vs baseline: 1.2717x; 1.0100x over previous
import contextlib
import numpy as np
import ml_dtypes
import concourse.bass as bass
import concourse.mybir as mybir
from concourse.bass_utils import run_bass_kernel_spmd

F32 = mybir.dt.float32
BF16 = mybir.dt.bfloat16
AF = mybir.ActivationFunctionType
ALU = mybir.AluOpType
AX = mybir.AxisListType

S_TOK = 4096
DM = 2048
NH = 32
NE = 8
ND = 16
NI = 11
NTC = 32
NTILE = 4
TT = 1024
TCT = 8


def build_program():
    nc = bass.Bass("TRN2", target_bir_lowering=False)

    X = nc.dram_tensor("x", [S_TOK, DM], F32, kind="ExternalInput").ap()
    WQ = nc.dram_tensor("wq", [ND, 128, DM], F32, kind="ExternalInput").ap()
    WK = nc.dram_tensor("wk", [DM, DM], F32, kind="ExternalInput").ap()
    WV = nc.dram_tensor("wv", [DM, DM], F32, kind="ExternalInput").ap()
    GW = nc.dram_tensor("gw", [DM, 8], F32, kind="ExternalInput").ap()
    WO4 = nc.dram_tensor("wo4", [4, 128, ND * 512], F32, kind="ExternalInput").ap()
    W13 = nc.dram_tensor("w13", [NE, NI, 128, 4096], F32, kind="ExternalInput").ap()
    W2Q = nc.dram_tensor("w2q", [NE, 4, 128, NI * 512], F32, kind="ExternalInput").ap()
    BOB = nc.dram_tensor("bob", [1, DM], F32, kind="ExternalInput").ap()
    E8 = nc.dram_tensor("e8", [NH, DM], F32, kind="ExternalInput").ap()
    IDB = nc.dram_tensor("idb", [128, 128], BF16, kind="ExternalInput").ap()
    IDF = nc.dram_tensor("idf", [128, 128], F32, kind="ExternalInput").ap()
    ONE1 = nc.dram_tensor("one1", [1, 128], BF16, kind="ExternalInput").ap()
    OUT = nc.dram_tensor("out", [S_TOK, DM], F32, kind="ExternalOutput").ap()

    XSD = nc.dram_tensor("xn_scr_d", [ND, 128, S_TOK], BF16).ap()
    XST = nc.dram_tensor("xn_scr_tc", [NTC, ND, 128, 128], BF16).ap()
    WFM = nc.dram_tensor("wfm", [8, S_TOK], F32).ap()

    ctx = contextlib.ExitStack()
    with ctx:
        ec = ctx.enter_context
        ar1 = ec(nc.sbuf_tensor("ar1", [128, 32768], BF16))
        arv = ec(nc.sbuf_tensor("arv", [128, 33792], BF16))
        ar2 = ec(nc.sbuf_tensor("ar2", [128, 19456], BF16))
        kv_bf = ec(nc.sbuf_tensor("kv_bf", [128, 1040], BF16))
        ksbd = ec(nc.sbuf_tensor("ksbd", [128, 512], BF16))
        e_sb = ec(nc.sbuf_tensor("e_sb", [NH, DM], BF16))
        bo_row = ec(nc.sbuf_tensor("bo_row", [1, DM], BF16))
        idbf = ec(nc.sbuf_tensor("idbf", [128, 128], BF16))
        idf32 = ec(nc.sbuf_tensor("idf32", [128, 128], F32))
        ones1 = ec(nc.sbuf_tensor("ones1", [1, 128], BF16))
        stage2 = [ec(nc.sbuf_tensor("stage0", [1, TT], BF16)),
                  ec(nc.sbuf_tensor("stage1", [1, TT], BF16))]
        xbuf = ec(nc.sbuf_tensor("xbuf", [128, DM], F32))
        st = ec(nc.sbuf_tensor("st", [128, 12], F32))
        logits = ec(nc.sbuf_tensor("logits", [128, 8], F32))
        t8a = ec(nc.sbuf_tensor("t8a", [128, 8], F32))
        t8b = ec(nc.sbuf_tensor("t8b", [128, 8], F32))
        t8c = ec(nc.sbuf_tensor("t8c", [128, 8], F32))
        w8 = ec(nc.sbuf_tensor("w8", [128, 8], F32))
        wrow128 = ec(nc.sbuf_tensor("wrow128", [128, 256], F32))
        gst = ec(nc.sbuf_tensor("gst", [128, 8], F32))
        wtok = ec(nc.sbuf_tensor("wtok", [128, 256], F32))
        nmaxb = ec(nc.sbuf_tensor("nmaxb", [NH, 512], F32))
        rinv = ec(nc.sbuf_tensor("rinv", [NH, 512], BF16))
        bcb = [ec(nc.sbuf_tensor("bcb%d" % i, [128, 512], F32)) for i in range(4)]

        psA = ec(nc.psum_tensor("psA", [128, 1024], F32))
        psB = ec(nc.psum_tensor("psB", [128, 1024], F32))
        psC = ec(nc.psum_tensor("psC", [128, 1024], F32))
        psD = ec(nc.psum_tensor("psD", [128, 1024], F32))

        sem_names = ["d_sy", "d_gp", "sT", "sV", "sA", "sP"]
        SEM = {n: ec(nc.semaphore(n)) for n in sem_names}

        y32 = ar1[:, :].bitcast(F32)
        wk_sb = ar1
        wv_sb = arv
        grp = arv
        xn_t = arv[:, 0:16384]
        qz = arv[:, 16384:32768]
        h_sb = arv[:, 16384:27648]
        s_sb = arv[:, 27648:29696].bitcast(F32)
        h1_sb = arv[:, 29696:31744].bitcast(F32)
        bce_sb = arv[:, 31744:33792].bitcast(F32)

        sq_scr = ar2[:, 0:4096].bitcast(F32)
        xn_tc = [ar2[:, 0:2048], ar2[:, 2048:4096]]
        k_sb = ar2[:, 4096:6144]
        vext = ar2[:, 6144:8224]
        kv_f32 = ar2[:, 8224:10304].bitcast(F32)
        gate_sb = ar2[:, 10304:10432]
        wqb = [ar2[:, 0:2048], ar2[:, 2048:4096]]
        woq = [ar2[:, 0:8192], ar2[:, 8192:16384]]
        w13b = [ar2[:, 0:4096], ar2[:, 4096:8192]]
        w2qb = [ar2[:, 8192:13824], ar2[:, 13824:19456]]

        ops = []
        cnt = {n: 0 for n in sem_names}

        def dma(eng, sem, out, in_):
            cnt[sem] += 16
            v = cnt[sem]
            ops.append((eng, lambda e, o=out, i=in_, s=sem: e.dma_start(out=o, in_=i).then_inc(SEM[s], 16),
                        ("inc", sem, 16)))
            return v

        def wait(eng, sem, v):
            if v > 0:
                ops.append((eng, lambda e, s=sem, vv=v: e.wait_ge(SEM[s], vv), ("wait", sem, v)))

        def inc(eng, sem, fn):
            cnt[sem] += 1
            v = cnt[sem]
            ops.append((eng, lambda e, f=fn, s=sem: f(e).then_inc(SEM[s], 1), ("inc", sem, 1)))
            return v

        def just(eng, fn):
            ops.append((eng, lambda e, f=fn: f(e), ("op",)))

        def drain(eng):
            ops.append((eng, lambda e: e.drain(), ("op",)))

        BIG = 1e30

        dma("gpsimd", "d_gp", e_sb[:, :], E8)
        dma("gpsimd", "d_gp", bo_row[:, :], BOB)
        dma("sync", "d_sy", idbf[:, :], IDB)
        dma("sync", "d_sy", idf32[:, :], IDF)
        dma("sync", "d_sy", ones1[:, :], ONE1)
        for d in range(ND):
            dma("gpsimd", "d_gp", wk_sb[:, d * 2048:(d + 1) * 2048], WK[d * 128:(d + 1) * 128, :])
        for d in range(ND):
            dma("gpsimd", "d_gp", wv_sb[:, d * 2048:(d + 1) * 2048], WV[d * 128:(d + 1) * 128, :])
        dma("gpsimd", "d_gp", gate_sb.rearrange("p (d j) -> p d j", d=ND),
            GW.rearrange("(d p) j -> p d j", p=128))

        pKV = [psA[:, 0:512], psA[:, 512:1024], psB[:, 0:512], psB[:, 512:1024]]
        kv0 = psC[:, 0:455]
        kv1 = psC[:, 512:967]
        kv2e = psD[:, 0:130]
        pgate = psD[:, 640:648]
        ptr4 = [psA[:, 0:64].bitcast(BF16), psA[:, 512:576].bitcast(BF16),
                psB[:, 0:64].bitcast(BF16), psB[:, 512:576].bitcast(BF16)]
        xntok2 = [ar2[:, 14528:16576], ar2[:, 10432:12480]]
        xnfm2 = [ar2[:, 16576:18624], ar2[:, 12480:14528]]

        def kv_col(pair):
            if pair < 7:
                return pair * 65
            if pair < 14:
                return 455 + (pair - 7) * 65
            return 910 + (pair - 14) * 65

        kvp = []
        for p in range(16):
            if p < 7:
                kvp.append((kv0, p * 65))
            elif p < 14:
                kvp.append((kv1, (p - 7) * 65))
            else:
                kvp.append((kv2e, (p - 14) * 65))

        just("vector", lambda e: e.memset(vext[:, :], 0.0))
        for h in range(NH):
            just("vector", lambda e, h=h: e.memset(vext[:, h * 65 + 64:h * 65 + 65], 1.0))
        mark_vext_ones = inc("vector", "sV", lambda e: e.nop())

        mark_xn_done = {}
        mark_tr_ev = {}
        mark_kaev = mark_kbev = mark_vaev = mark_vbev = mark_gev = 0
        mark_kvmm = 0
        mark_wfm = {}
        mark_xsd = {}
        mark_xnfm_free = {}

        def emit_ln(tc):
            buf = tc % 2
            wait("sync", "sV", mark_xn_done.get(tc - 1, 0))
            mx = dma("sync", "d_sy", xbuf[:, :], X[tc * 128:(tc + 1) * 128, :])
            wait("vector", "d_sy", mx)
            just("vector", lambda e: e.reduce_sum(st[:, 0:1], xbuf[:, :], AX.X))
            wait("scalar", "d_sy", mx)
            mA = inc("scalar", "sA", lambda e: e.activation(
                sq_scr[:, :], xbuf[:, :], AF.Square, accum_out=st[:, 1:2]))
            wait("vector", "sA", mA)
            drain("vector")
            just("vector", lambda e: e.tensor_scalar(st[:, 2:3], st[:, 0:1], 1.0 / DM, None, ALU.mult))
            just("vector", lambda e: e.tensor_scalar(st[:, 3:4], st[:, 1:2], 1.0 / DM, None, ALU.mult))
            drain("vector")
            just("vector", lambda e: e.tensor_tensor(st[:, 4:5], st[:, 2:3], st[:, 2:3], ALU.mult))
            drain("vector")
            just("vector", lambda e: e.tensor_tensor(st[:, 5:6], st[:, 3:4], st[:, 4:5], ALU.subtract))
            drain("vector")
            mV = inc("vector", "sV", lambda e: e.tensor_scalar(st[:, 6:7], st[:, 5:6], 1e-5, None, ALU.add))
            wait("scalar", "sV", mV)
            mA = inc("scalar", "sA", lambda e: e.activation(st[:, 7:8], st[:, 6:7], AF.Sqrt))
            wait("vector", "sA", mA)
            just("vector", lambda e: e.reciprocal(st[:, 8:9], st[:, 7:8]))
            if tc >= 2:
                wait("vector", "sT", mark_tr_last[tc - 2])
            drain("vector")
            mark_xn_done[tc] = inc("vector", "sV", lambda e, buf=buf: e.tensor_scalar(
                xntok2[buf][:, :], xbuf[:, :], st[:, 2:3], st[:, 8:9], ALU.subtract, ALU.mult))

        emit_ln(0)
        mark_tr_last = {}
        for tc in range(NTC):
            buf = tc % 2
            wait("tensor", "sV", mark_xn_done[tc])
            if tc == 0:
                wait("tensor", "d_gp", cnt["d_gp"])
                wait("tensor", "sV", mark_vext_ones)
            wait("tensor", "sA", mark_gev)
            if tc >= 2:
                wait("tensor", "sT", mark_xnfm_free[tc - 2])
                wait("tensor", "d_sy", mark_xsd[tc - 2])
            for d in range(ND):
                slot = d % 4
                if (tc, d) not in ((0, 0), (0, 1), (0, 2), (0, 3)):
                    pd = tc * ND + d - 4
                    ptc, pdd = divmod(pd, ND)
                    sem, val = mark_tr_ev[(ptc, pdd)]
                    wait("tensor", sem, val)
                mark_tr_last_v = inc("tensor", "sT", lambda e, d=d, buf=buf, slot=slot: e.transpose(
                    ptr4[slot], xntok2[buf][:, d * 128:(d + 1) * 128], idbf[:, :]))
                ev_eng, ev_sem = ("vector", "sV") if d % 2 == 0 else ("scalar", "sA")
                cpf = (lambda e, o, i: e.tensor_copy(o, i)) if d % 2 == 0 else (lambda e, o, i: e.copy(o, i))
                wait(ev_eng, "sT", mark_tr_last_v)
                mark_tr_ev[(tc, d)] = (ev_sem, inc(ev_eng, ev_sem, lambda e, d=d, buf=buf, slot=slot, cpf=cpf: cpf(
                    e, xnfm2[buf][:, d * 128:(d + 1) * 128], ptr4[slot])))
            mark_tr_last[tc] = mark_tr_last_v
            for d in range(ND):
                sem, val = mark_tr_ev[(tc, d)]
                wait("sync", sem, val)
                mark_xsd[tc] = dma("sync", "d_sy", XSD[d][:, tc * 128:(tc + 1) * 128],
                                   xnfm2[buf][:, d * 128:(d + 1) * 128])
            for d in range(ND):
                sem, val = mark_tr_ev[(tc, d)]
                wait("tensor", sem, val)
            if tc + 1 < NTC:
                emit_ln(tc + 1)
            wait("tensor", "sV", mark_vaev)
            for d in range(ND):
                for gcol in range(2):
                    just("tensor", lambda e, d=d, gcol=gcol, buf=buf: e.matmul(
                        pKV[gcol], xnfm2[buf][:, d * 128:(d + 1) * 128],
                        wk_sb[:, d * 2048 + gcol * 512: d * 2048 + (gcol + 1) * 512],
                        start=(d == 0), stop=(d == ND - 1)))
                fn = lambda e, d=d, buf=buf: e.matmul(
                    pgate, xnfm2[buf][:, d * 128:(d + 1) * 128],
                    gate_sb[:, d * 8:(d + 1) * 8],
                    start=(d == 0), stop=(d == ND - 1))
                if d == ND - 1:
                    mka = inc("tensor", "sT", fn)
                else:
                    just("tensor", fn)
            wait("vector", "sT", mka)
            mark_kaev = inc("vector", "sV", lambda e: e.tensor_relu(k_sb[:, 0:1024], psA[:, :]))
            wait("scalar", "sT", mka)
            mark_gev = inc("scalar", "sA", lambda e: e.copy(logits[:, :], pgate))
            wait("tensor", "sA", mark_vbev)
            for d in range(ND):
                for gcol in range(2):
                    fn = lambda e, d=d, gcol=gcol, buf=buf: e.matmul(
                        pKV[2 + gcol], xnfm2[buf][:, d * 128:(d + 1) * 128],
                        wk_sb[:, d * 2048 + 1024 + gcol * 512: d * 2048 + 1024 + (gcol + 1) * 512],
                        start=(d == 0), stop=(d == ND - 1))
                    if d == ND - 1 and gcol == 1:
                        mkb = inc("tensor", "sT", fn)
                    else:
                        just("tensor", fn)
            wait("scalar", "sT", mkb)
            mark_kbev = inc("scalar", "sA", lambda e: e.activation(k_sb[:, 1024:2048], psB[:, :], AF.Relu))
            wait("tensor", "sV", mark_kaev)
            for d in range(ND):
                for gcol in range(2):
                    fn = lambda e, d=d, gcol=gcol, buf=buf: e.matmul(
                        pKV[gcol], xnfm2[buf][:, d * 128:(d + 1) * 128],
                        wv_sb[:, d * 2048 + gcol * 512: d * 2048 + (gcol + 1) * 512],
                        start=(d == 0), stop=(d == ND - 1))
                    if d == ND - 1 and gcol == 1:
                        mva = inc("tensor", "sT", fn)
                    else:
                        just("tensor", fn)
            wait("vector", "sT", mva)
            mark_vaev = inc("vector", "sV", lambda e: e.tensor_copy(
                vext.rearrange("p (h j) -> p h j", j=65)[:, 0:16, 0:64],
                psA.rearrange("p (h j) -> p h j", j=64)))
            wait("tensor", "sA", mark_kbev)
            for d in range(ND):
                for gcol in range(2):
                    fn = lambda e, d=d, gcol=gcol, buf=buf: e.matmul(
                        pKV[2 + gcol], xnfm2[buf][:, d * 128:(d + 1) * 128],
                        wv_sb[:, d * 2048 + 1024 + gcol * 512: d * 2048 + 1024 + (gcol + 1) * 512],
                        start=(d == 0), stop=(d == ND - 1))
                    if d == ND - 1 and gcol == 1:
                        mvb = inc("tensor", "sT", fn)
                    else:
                        just("tensor", fn)
            mark_xnfm_free[tc] = mvb
            wait("scalar", "sT", mvb)
            mark_vbev = inc("scalar", "sA", lambda e: e.copy(
                vext.rearrange("p (h j) -> p h j", j=65)[:, 16:32, 0:64],
                psB.rearrange("p (h j) -> p h j", j=64)))
            wait("tensor", "sV", mark_vaev)
            wait("tensor", "sA", mark_vbev)
            wait("tensor", "sA", mark_gev)
            for h in range(NH):
                pair, base = h // 2, (h % 2) * 64
                tgt, col = kvp[pair]
                st_flag = (tc == 0) and (pair in (0, 7, 14))
                mark_kvmm = inc("tensor", "sT", lambda e, h=h, base=base, tgt=tgt, col=col, st=st_flag: e.matmul(
                    tgt[base:base + 64, col:col + 65],
                    k_sb[:, h * 64:(h + 1) * 64], vext[:, h * 65:(h + 1) * 65],
                    start=st, stop=(tc == NTC - 1)))
            wait("vector", "sA", mark_gev)
            just("vector", lambda e: e.tensor_reduce(gst[:, 0:1], logits[:, :], AX.X, ALU.max))
            drain("vector")
            just("vector", lambda e: e.tensor_scalar(t8a[:, :], logits[:, :], gst[:, 0:1], None, ALU.subtract))
            drain("vector")
            just("vector", lambda e: e.tensor_scalar(t8b[:, :], t8a[:, :], 0.0, None, ALU.is_ge))
            drain("vector")
            just("vector", lambda e: e.scalar_tensor_tensor(t8c[:, :], t8b[:, :], -BIG, t8a[:, :], ALU.mult, ALU.add))
            drain("vector")
            mV = inc("vector", "sV", lambda e: e.tensor_reduce(gst[:, 1:2], t8c[:, :], AX.X, ALU.max))
            wait("scalar", "sV", mV)
            just("scalar", lambda e: e.activation(gst[:, 2:3], gst[:, 1:2], AF.Exp))
            mA = inc("scalar", "sA", lambda e: e.activation(t8b[:, :], t8a[:, :], AF.Exp))
            wait("vector", "sA", mA)
            just("vector", lambda e: e.tensor_scalar(gst[:, 3:4], gst[:, 2:3], 1.0, None, ALU.add))
            drain("vector")
            just("vector", lambda e: e.reciprocal(gst[:, 4:5], gst[:, 3:4]))
            drain("vector")
            just("vector", lambda e: e.tensor_scalar(t8c[:, :], t8a[:, :], gst[:, 1:2], None, ALU.is_ge))
            drain("vector")
            just("vector", lambda e: e.tensor_tensor(t8a[:, :], t8c[:, :], t8b[:, :], ALU.mult))
            drain("vector")
            mV = inc("vector", "sV", lambda e, tc=tc: e.tensor_scalar(
                wtok[:, tc * 8:(tc + 1) * 8], t8a[:, :], gst[:, 4:5], None, ALU.mult))
            mark_wtok = mV
        wait("tensor", "sV", mark_wtok)
        wait("tensor", "sA", mark_gev)
        tr128 = [psD[0:128, 512:640], psD[0:128, 640:768]]
        mwr = 0
        for half in range(2):
            wait("tensor", "sA", mwr)
            mT = inc("tensor", "sT", lambda e, half=half: e.transpose(
                tr128[half], wtok[:, half * 128:(half + 1) * 128], idf32[:, :]))
            wait("scalar", "sT", mT)
            mwr = inc("scalar", "sA", lambda e, half=half: e.copy(wrow128[:, half * 128:(half + 1) * 128], tr128[half]))
            wait("sync", "sA", mwr)
            for ex in range(8):
                src_ap = bass.AP(wrow128.tensor if hasattr(wrow128, "tensor") else wrow128,
                                 ex * 256 + half * 128, [[2048, 16], [1, 128]])
                mark_wfm[(half, ex)] = dma("sync", "d_sy",
                    WFM[ex:ex + 1, half * 2048:(half + 1) * 2048].rearrange("o (tc p) -> (o tc) p", p=128),
                    src_ap)
        mark_wfm_done = cnt["d_sy"]

        wait("vector", "sT", mark_kvmm)
        just("vector", lambda e: e.tensor_copy(kv_f32[:, 0:455], kv0))
        just("vector", lambda e: e.tensor_copy(kv_f32[:, 455:910], kv1))
        just("vector", lambda e: e.tensor_copy(kv_f32[:, 910:1040], kv2e))
        just("vector", lambda e: e.tensor_copy(kv_bf[:, 0:455], kv0))
        just("vector", lambda e: e.tensor_copy(kv_bf[:, 455:910], kv1))
        just("vector", lambda e: e.tensor_copy(kv_bf[:, 910:1040], kv2e))
        drain("vector")
        just("vector", lambda e: e.memset(ksbd[:, :], 0.0))
        for h in range(NH):
            base = (h % 2) * 64
            kc = kv_col(h // 2) + 64
            dst_col = (h // 2) * 32 + h
            just("vector", lambda e, base=base, kc=kc, dc=dst_col: e.tensor_copy(
                ksbd[base:base + 64, dc:dc + 1], kv_f32[base:base + 64, kc:kc + 1]))
        p1_done = inc("vector", "sV", lambda e: e.nop())
        p1_sA = cnt["sA"]

        mark_y_store = {t: 0 for t in range(TCT)}
        mark_psq_ev = [0, 0]
        mark_pt3_ev = [0, 0]
        mark_pyq_ev = [0, 0, 0, 0]
        mark_bcev = [0, 0]
        mark_sev = [0, 0]
        mark_uev = [0, 0]
        mark_silu = mark_h1 = mark_h = 0
        mark_w13 = {0: 0, 1: 0}
        mark_wq = [0, 0]
        mark_woq = [0, 0]
        mark_w2q = [0, 0]
        last_wq_mm = [0, 0]
        last_woq_mm = [0, 0]
        last_w13_mm = [0, 0]
        last_w2q_mm = [0, 0]
        last_stage_mm = 0
        mark_qev = {}
        mark_zmul = {}
        mark_yadd_last = {}

        wait("sync", "sV", p1_done)
        wait("sync", "sT", mark_kvmm)
        mark_xnt_load = 0
        for d in range(ND):
            mark_xnt_load = dma("sync", "d_sy", xn_t[:, d * 1024:(d + 1) * 1024], XSD[d][:, 0:TT])

        for tile in range(NTILE):
            t0 = tile * TT
            wait("gpsimd", "sV", p1_done)
            wait("gpsimd", "sT", last_w13_mm[0])
            wait("gpsimd", "sT", last_w13_mm[1])
            wait("gpsimd", "sT", last_w2q_mm[0])
            wait("gpsimd", "sT", last_w2q_mm[1])
            mark_wq[0] = dma("gpsimd", "d_gp", wqb[0], WQ[0])
            psq = [psA, psB]
            for q in range(ND):
                b = q % 2
                if q + 1 < ND:
                    nb = (q + 1) % 2
                    wait("gpsimd", "sT", last_wq_mm[nb])
                    mark_wq[nb] = dma("gpsimd", "d_gp", wqb[nb], WQ[q + 1])
                wait("tensor", "d_gp", mark_wq[b])
                wait("tensor", "d_sy", mark_xnt_load)
                wait("tensor", "sV", mark_psq_ev[b])
                if tile == 0 and q == 0:
                    wait("tensor", "sV", p1_done)
                if tile > 0 and q == 0:
                    for pb in range(4):
                        wait("tensor", "sV", mark_pyq_ev[pb])
                for d in range(ND):
                    for half in range(2):
                        fn = lambda e, b=b, d=d, half=half: e.matmul(
                            psq[b][:, half * 512:(half + 1) * 512],
                            wqb[b][:, d * 128:(d + 1) * 128],
                            xn_t[:, d * 1024 + half * 512: d * 1024 + (half + 1) * 512],
                            start=(d == 0), stop=(d == ND - 1))
                        if d == ND - 1 and half == 1:
                            last_wq_mm[b] = last_q_mm = inc("tensor", "sT", fn)
                        else:
                            just("tensor", fn)
                wait("vector", "sT", last_q_mm)
                mark_psq_ev[b] = inc("vector", "sV", lambda e, q=q, b=b: e.tensor_relu(
                    qz[:, q * 1024:(q + 1) * 1024], psq[b][:, :]))
                mark_qev[q] = mark_psq_ev[b]

            pn = psC[0:NH, 0:512]
            pbc = [psC[:, 512:1024], psD[:, 0:512], psA[:, 512:1024], psB[:, 512:1024]]
            pqkv = [psD[:, 512:1024], psA[:, 0:512]]
            mark_nred = 0
            mark_zpar = [0, 0]
            mark_zb4 = [0, 0, 0, 0]
            mark_bcev = [0, 0, 0, 0]
            for sg in range(2):
                for q in range(ND):
                    wait("tensor", "sV", mark_qev[q])
                if tile == 0 and sg == 0:
                    wait("tensor", "sA", p1_sA)
                if sg == 1:
                    wait("tensor", "sV", mark_nred)
                for c in range(ND):
                    fn = lambda e, c=c, sg=sg: e.matmul(
                        pn, ksbd[:, c * 32:(c + 1) * 32],
                        qz[:, c * 1024 + sg * 512: c * 1024 + (sg + 1) * 512],
                        start=(c == 0), stop=(c == ND - 1))
                    if c == ND - 1:
                        m_n = inc("tensor", "sT", fn)
                    else:
                        just("tensor", fn)
                wait("vector", "sT", m_n)
                just("vector", lambda e: e.tensor_scalar(nmaxb[:, :], pn, 1e-6, None, ALU.max))
                drain("vector")
                just("vector", lambda e: e.reciprocal(nmaxb[:, :], nmaxb[:, :]))
                drain("vector")
                mark_nred = inc("vector", "sV", lambda e: e.tensor_copy(rinv[:, :], nmaxb[:, :]))
                wait("tensor", "sV", mark_nred)
                for c in range(ND):
                    b = c % 2
                    b4 = c % 4
                    wait("tensor", "sA", mark_bcev[b4])
                    mbc = inc("tensor", "sT", lambda e, c=c, b4=b4: e.matmul(
                        pbc[b4], e_sb[:, c * 128:(c + 1) * 128], rinv[:, :],
                        start=True, stop=True))
                    wait("tensor", "sV", mark_zpar[b])
                    just("tensor", lambda e, c=c, b=b, sg=sg: e.matmul(
                        pqkv[b][0:64, :], kv_bf[0:64, kv_col(c):kv_col(c) + 64],
                        qz[0:64, c * 1024 + sg * 512: c * 1024 + (sg + 1) * 512],
                        start=True, stop=True))
                    mqk = inc("tensor", "sT", lambda e, c=c, b=b, sg=sg: e.matmul(
                        pqkv[b][64:128, :], kv_bf[64:128, kv_col(c):kv_col(c) + 64],
                        qz[64:128, c * 1024 + sg * 512: c * 1024 + (sg + 1) * 512],
                        start=True, stop=True))
                    wait("scalar", "sT", mbc)
                    wait("scalar", "sV", mark_zb4[b4])
                    mark_bcev[b4] = inc("scalar", "sA", lambda e, b4=b4: e.copy(bcb[b4][:, :], pbc[b4]))
                    wait("vector", "sT", mqk)
                    wait("vector", "sA", mark_bcev[b4])
                    mark_zpar[b] = mark_zb4[b4] = mark_zmul[(sg, c)] = inc(
                        "vector", "sV", lambda e, c=c, b=b, b4=b4, sg=sg: e.tensor_tensor(
                            qz[:, c * 1024 + sg * 512: c * 1024 + (sg + 1) * 512],
                            pqkv[b], bcb[b4][:, :], ALU.mult))

            pt3 = [psA[:, 0:512], psB[:, 0:512]]
            wait("gpsimd", "sT", last_wq_mm[0])
            wait("gpsimd", "sT", last_wq_mm[1])
            mark_woq[0] = dma("gpsimd", "d_gp", woq[0], WO4[0])
            for q4 in range(4):
                b = q4 % 2
                if q4 + 1 < 4:
                    nb = (q4 + 1) % 2
                    wait("gpsimd", "sT", last_woq_mm[nb])
                    mark_woq[nb] = dma("gpsimd", "d_gp", woq[nb], WO4[q4 + 1])
                wait("tensor", "d_gp", mark_woq[b])
                for tcl in range(TCT):
                    pb = tcl % 2
                    if q4 == 0 and tcl == 0:
                        for c in range(ND):
                            wait("tensor", "sV", mark_zmul[(1, c)])
                    wait("tensor", "sV", mark_pt3_ev[pb])
                    if tile > 0:
                        wait("tensor", "d_sy", mark_y_store[tcl])
                    for c in range(ND):
                        just("tensor", lambda e, c=c, tcl=tcl, b=b, pb=pb: e.matmul(
                            pt3[pb], qz[:, c * 1024 + tcl * 128: c * 1024 + tcl * 128 + 128],
                            woq[b][:, c * 512:(c + 1) * 512],
                            start=(c == 0), stop=False))
                    last_woq_mm[b] = m3 = inc("tensor", "sT", lambda e, q4=q4, pb=pb: e.matmul(
                        pt3[pb], ones1[0:1, :], bo_row[0:1, q4 * 512:(q4 + 1) * 512],
                        start=False, stop=True))
                    wait("vector", "sT", m3)
                    mark_pt3_ev[pb] = inc("vector", "sV", lambda e, tcl=tcl, q4=q4, pb=pb: e.tensor_copy(
                        y32[:, tcl * 2048 + q4 * 512: tcl * 2048 + (q4 + 1) * 512], pt3[pb]))

            for ex in range(NE):
                sbuf_st = stage2[ex % 2]
                if ex == 0:
                    wait("gpsimd", "d_sy", mark_wfm_done)
                    wait("gpsimd", "sT", last_stage_mm)
                    mark_stage = dma("gpsimd", "d_gp", sbuf_st[0:1, :], WFM[ex:ex + 1, t0:t0 + TT])
                mst = mark_stage
                wait("tensor", "d_gp", mst)
                if ex == 0:
                    wait("tensor", "sV", mark_pt3_ev[0])
                    wait("tensor", "sV", mark_pt3_ev[1])
                else:
                    wait("tensor", "sA", mark_sev[0])
                    wait("tensor", "sV", mark_uev[0])
                    for pb in range(4):
                        wait("tensor", "sV", mark_pyq_ev[pb])
                wait("tensor", "sV", mark_h1)
                for half in range(2):
                    fn = lambda e, half=half, sbuf_st=sbuf_st: e.matmul(
                        psA[:, half * 512:(half + 1) * 512], ones1[0:1, :],
                        sbuf_st[0:1, half * 512:(half + 1) * 512], start=True, stop=True)
                    if half == 1:
                        last_stage_mm = mbs = inc("tensor", "sT", fn)
                    else:
                        just("tensor", fn)
                wait("scalar", "sT", mbs)
                mark_bce = inc("scalar", "sA", lambda e: e.copy(bce_sb[:, :], psA[:, :]))
                if ex == 0:
                    wait("gpsimd", "sT", last_w13_mm[0])
                    wait("gpsimd", "sT", last_woq_mm[0])
                    wait("gpsimd", "sT", last_woq_mm[1])
                    mark_w13[0] = dma("gpsimd", "d_gp", w13b[0], W13[ex, 0])
                wait("gpsimd", "sT", last_w2q_mm[0])
                mark_w2q[0] = dma("gpsimd", "d_gp", w2qb[0], W2Q[ex, 0])
                pg = [psA, psC]
                pu = [psB, psD]
                for i in range(NI):
                    b = i % 2
                    if i + 1 < NI:
                        nb = (i + 1) % 2
                        wait("gpsimd", "sT", last_w13_mm[nb])
                        mark_w13[nb] = dma("gpsimd", "d_gp", w13b[nb], W13[ex, i + 1])
                    wait("tensor", "d_gp", mark_w13[b])
                    if i == 0:
                        wait("tensor", "sA", mark_bce)
                    wait("tensor", "sA", mark_sev[b])
                    wait("tensor", "sV", mark_uev[b])
                    for d in range(ND):
                        for half in range(2):
                            just("tensor", lambda e, b=b, d=d, half=half: e.matmul(
                                pg[b][:, half * 512:(half + 1) * 512],
                                w13b[b][:, d * 128:(d + 1) * 128],
                                xn_t[:, d * 1024 + half * 512: d * 1024 + (half + 1) * 512],
                                start=(d == 0), stop=(d == ND - 1)))
                    for d in range(ND):
                        for half in range(2):
                            fn = lambda e, b=b, d=d, half=half: e.matmul(
                                pu[b][:, half * 512:(half + 1) * 512],
                                w13b[b][:, 2048 + d * 128: 2048 + (d + 1) * 128],
                                xn_t[:, d * 1024 + half * 512: d * 1024 + (half + 1) * 512],
                                start=(d == 0), stop=(d == ND - 1))
                            if d == ND - 1 and half == 1:
                                last_w13_mm[b] = mgu = inc("tensor", "sT", fn)
                            else:
                                just("tensor", fn)
                    wait("scalar", "sT", mgu)
                    wait("scalar", "sP", mark_h)
                    mark_sev[b] = mark_silu = inc("scalar", "sA", lambda e, b=b: e.activation(
                        s_sb[:, :], pg[b][:, :], AF.Silu))
                    wait("vector", "sT", mgu)
                    wait("vector", "sP", mark_h)
                    mark_uev[b] = mark_h1 = inc("vector", "sV", lambda e, b=b: e.tensor_tensor(
                        h1_sb[:, :], pu[b][:, :], bce_sb[:, :], ALU.mult))
                    wait("gpsimd", "sA", mark_silu)
                    wait("gpsimd", "sV", mark_h1)
                    mark_h = inc("gpsimd", "sP", lambda e, i=i: e.tensor_tensor(
                        h_sb[:, i * 1024:(i + 1) * 1024], h1_sb[:, :], s_sb[:, :], ALU.mult))
                    if i == NI - 2:
                        mark_h9 = mark_h
                if ex == NE - 1 and tile + 1 < NTILE:
                    wait("sync", "sT", mgu)
                    for d in range(ND):
                        mark_xnt_load = dma("sync", "d_sy", xn_t[:, d * 1024:(d + 1) * 1024],
                                            XSD[d][:, (tile + 1) * TT:(tile + 2) * TT])
                if ex + 1 < NE:
                    wait("gpsimd", "sT", last_stage_mm)
                    mark_stage = dma("gpsimd", "d_gp", stage2[(ex + 1) % 2][0:1, :],
                                     WFM[ex + 1:ex + 2, t0:t0 + TT])
                    wait("gpsimd", "sT", last_w13_mm[0])
                    mark_w13[0] = dma("gpsimd", "d_gp", w13b[0], W13[ex + 1, 0])
                pyq = [psA[:, 0:512], psB[:, 0:512], psC[:, 0:512], psD[:, 0:512]]
                for q4 in range(4):
                    b = q4 % 2
                    if q4 + 1 < 4:
                        nb = (q4 + 1) % 2
                        wait("gpsimd", "sT", last_w2q_mm[nb])
                        mark_w2q[nb] = dma("gpsimd", "d_gp", w2qb[nb], W2Q[ex, q4 + 1])
                    wait("tensor", "d_gp", mark_w2q[b])
                    if q4 == 0:
                        wait("tensor", "sP", mark_h9)
                        wait("tensor", "sA", mark_sev[0])
                        wait("tensor", "sA", mark_sev[1])
                        wait("tensor", "sV", mark_uev[0])
                        wait("tensor", "sV", mark_uev[1])
                    for tcl in range(TCT):
                        pb = (q4 * TCT + tcl) % 4
                        wait("tensor", "sV", mark_pyq_ev[pb])
                        for i in range(NI):
                            if q4 == 0 and tcl == 0 and i == NI - 1:
                                wait("tensor", "sP", mark_h)
                            fn = lambda e, i=i, tcl=tcl, b=b, pb=pb: e.matmul(
                                pyq[pb], h_sb[:, i * 1024 + tcl * 128: i * 1024 + tcl * 128 + 128],
                                w2qb[b][:, i * 512:(i + 1) * 512],
                                start=(i == 0), stop=(i == NI - 1))
                            if i == NI - 1:
                                last_w2q_mm[b] = m4 = inc("tensor", "sT", fn)
                            else:
                                just("tensor", fn)
                        wait("vector", "sT", m4)
                        mark_pyq_ev[pb] = inc("vector", "sV", lambda e, tcl=tcl, q4=q4, pb=pb: e.tensor_tensor(
                            y32[:, tcl * 2048 + q4 * 512: tcl * 2048 + (q4 + 1) * 512],
                            pyq[pb],
                            y32[:, tcl * 2048 + q4 * 512: tcl * 2048 + (q4 + 1) * 512], ALU.add))
                        if ex == NE - 1:
                            mark_yadd_last[(q4, tcl)] = mark_pyq_ev[pb]
                        if ex == NE - 1 and q4 == 3:
                            mxl = dma("sync", "d_sy", xbuf[:, :],
                                      X[t0 + tcl * 128: t0 + (tcl + 1) * 128, :])
                            for q4b in range(4):
                                wait("vector", "sV", mark_yadd_last[(q4b, tcl)])
                            wait("vector", "d_sy", mxl)
                            mF = inc("vector", "sV", lambda e, tcl=tcl: e.tensor_tensor(
                                y32[:, tcl * 2048:(tcl + 1) * 2048], xbuf[:, :],
                                y32[:, tcl * 2048:(tcl + 1) * 2048], ALU.add))
                            wait("sync", "sV", mF)
                            mark_y_store[tcl] = dma("sync", "d_sy",
                                OUT[t0 + tcl * 128: t0 + (tcl + 1) * 128, :],
                                y32[:, tcl * 2048:(tcl + 1) * 2048])


        wait("sync", "d_sy", cnt["d_sy"])
        wait("gpsimd", "d_gp", cnt["d_gp"])

        engs = ["sync", "tensor", "vector", "scalar", "gpsimd"]
        queues = {n: [] for n in engs}
        for name, fn, meta in ops:
            queues[name].append(meta)
        pos = {n: 0 for n in engs}
        sem_now = {n: 0 for n in sem_names}
        progress = True
        while progress:
            progress = False
            for n in engs:
                while pos[n] < len(queues[n]):
                    meta = queues[n][pos[n]]
                    if meta[0] == "wait":
                        if sem_now[meta[1]] < meta[2]:
                            break
                    elif meta[0] == "inc":
                        sem_now[meta[1]] += meta[2]
                    pos[n] += 1
                    progress = True
        stuck = {n: (pos[n], len(queues[n])) for n in engs if pos[n] < len(queues[n])}
        if stuck:
            detail = {n: queues[n][pos[n]] for n in stuck}
            raise RuntimeError(f"plan deadlock: {stuck} blocked at {detail} sems={sem_now}")

        with nc.Block() as block:
            @block.sync
            def _(eng):
                for name, fn, _m in ops:
                    if name == "sync":
                        fn(eng)

            @block.tensor
            def _(eng):
                for name, fn, _m in ops:
                    if name == "tensor":
                        fn(eng)

            @block.vector
            def _(eng):
                for name, fn, _m in ops:
                    if name == "vector":
                        fn(eng)

            @block.scalar
            def _(eng):
                for name, fn, _m in ops:
                    if name == "scalar":
                        fn(eng)

            @block.gpsimd
            def _(eng):
                for name, fn, _m in ops:
                    if name == "gpsimd":
                        fn(eng)

    return nc


def _prep(inputs):
    wq = np.ascontiguousarray(
        inputs["L0_attn_Wq"].reshape(16, 128, 16, 128).transpose(2, 1, 0, 3).reshape(16, 128, 2048))
    wo4 = np.ascontiguousarray(
        inputs["L0_attn_Wo"].reshape(16, 128, 4, 512).transpose(2, 1, 0, 3).reshape(4, 128, 16 * 512))
    w13a = inputs["L0_ffn_w1"].reshape(8, 16, 128, 11, 128).transpose(0, 3, 2, 1, 4)
    w13c = inputs["L0_ffn_w3"].reshape(8, 16, 128, 11, 128).transpose(0, 3, 2, 1, 4)
    w13 = np.ascontiguousarray(np.stack([w13a, w13c], axis=3).reshape(8, 11, 128, 4096))
    w2q = np.ascontiguousarray(
        inputs["L0_ffn_w2"].reshape(8, 11, 128, 4, 512).transpose(0, 3, 2, 1, 4).reshape(8, 4, 128, 11 * 512))
    bob = np.ascontiguousarray(inputs["L0_attn_bo"][None, :])
    e8 = np.zeros((32, 2048), dtype=np.float32)
    for h in range(32):
        e8[h, h * 64:(h + 1) * 64] = 1.0
    return {
        "wq": wq, "wk": np.ascontiguousarray(inputs["L0_attn_Wk"]),
        "wv": np.ascontiguousarray(inputs["L0_attn_Wv"]),
        "gw": np.ascontiguousarray(inputs["L0_ffn_gate_w"]),
        "wo4": wo4, "w13": w13, "w2q": w2q, "bob": bob, "e8": e8,
        "idb": np.eye(128, dtype=ml_dtypes.bfloat16),
        "idf": np.eye(128, dtype=np.float32),
        "one1": np.ones((1, 128), dtype=ml_dtypes.bfloat16),
    }


def kernel(**inputs):
    x = np.asarray(inputs["L0_attn_hidden_states"])
    common = _prep({k: np.asarray(v) for k, v in inputs.items()})
    nc = build_program()
    in_maps = []
    for c in range(8):
        m = dict(common)
        m["x"] = np.ascontiguousarray(x[c])
        in_maps.append(m)
    res = run_bass_kernel_spmd(nc, in_maps, core_ids=list(range(8)))
    out = np.stack([np.asarray(res.results[c]["out"]) for c in range(8)], axis=0)
    return out.astype(np.float32)
